# version 35
# baseline (speedup 1.0000x reference)
import numpy as np
import ml_dtypes
import concourse.bass as bass
import concourse.mybir as mybir
from concourse.bass_utils import run_bass_kernel_spmd

F32 = mybir.dt.float32
BF16 = mybir.dt.bfloat16
AF = mybir.ActivationFunctionType
ALU = mybir.AluOpType
AXX = mybir.AxisListType.X

D = 1024
W1N = 4096
EL = 600
ELC = [128, 128, 128, 128, 88]
NEXP = 6
LT = 100
LN_EPS = 1e-5
FEPS = float(np.finfo(np.float32).eps)
ISC = float(1024 ** -0.5)

ENGINES = ("sync", "vector", "scalar", "tensor", "gpsimd")


class Plan:
    def __init__(self):
        self.ops = {e: [] for e in ENGINES}
        self.cnt = {}

    def inc(self, sem, by=1):
        k = id(sem)
        self.cnt[k] = self.cnt.get(k, 0) + by
        return self.cnt[k]

    def val(self, sem):
        return self.cnt.get(id(sem), 0)


def build(S, dbg=False):
    NT = S // 128
    nc = bass.Bass("TRN2", target_bir_lowering=False)

    x_d = nc.dram_tensor("x", [S, D], F32, kind="ExternalInput").ap()
    w1_d = nc.dram_tensor("w1", [D, W1N], F32, kind="ExternalInput").ap()
    b1_d = nc.dram_tensor("b1", [W1N], F32, kind="ExternalInput").ap()
    at_d = nc.dram_tensor("at", [NEXP, 16, LT], F32, kind="ExternalInput").ap()
    bm_d = nc.dram_tensor("bm", [NEXP, 16, D], F32, kind="ExternalInput").ap()
    wg_d = nc.dram_tensor("wg", [D, NEXP], F32, kind="ExternalInput").ap()
    wt_d = nc.dram_tensor("wt", [D, D], F32, kind="ExternalInput").ap()
    wtb_d = nc.dram_tensor("wtb", [D], F32, kind="ExternalInput").ap()
    wd_d = nc.dram_tensor("wd", [D, D], F32, kind="ExternalInput").ap()
    wdb_d = nc.dram_tensor("wdb", [D], F32, kind="ExternalInput").ap()
    sc_d = nc.dram_tensor("sc", [1], F32, kind="ExternalInput").ap()
    id_d = nc.dram_tensor("ident", [128, 128], F32, kind="ExternalInput").ap()
    uo_d = nc.dram_tensor("uones", [128, 128], BF16, kind="ExternalInput").ap()
    uo32_d = nc.dram_tensor("uones32", [128, 128], F32, kind="ExternalInput").ap()
    io_d = nc.dram_tensor("iota6", [128, NEXP], F32, kind="ExternalInput").ap()
    o1_d = nc.dram_tensor("ones1f", [1, 128], F32, kind="ExternalInput").ap()
    ob_d = nc.dram_tensor("onesb", [1, 128], BF16, kind="ExternalInput").ap()
    o2_d = nc.dram_tensor("ones2b", [2, 128], BF16, kind="ExternalInput").ap()
    cv_d = nc.dram_tensor("cvals", [128, 8], F32, kind="ExternalInput").ap()
    out_d = nc.dram_tensor("out", [S, D], F32, kind="ExternalOutput").ap()
    h_d = nc.dram_tensor("hscr", [S, D], F32,
                         kind="ExternalOutput" if dbg else "Internal").ap()
    spwd_d = nc.dram_tensor("spwd", [128, 16384], BF16, kind="Internal").ap()
    sptk_d = nc.dram_tensor("sptk", [128, 9600], BF16, kind="Internal").ap()
    spvw_d = nc.dram_tensor("spvw", [128, 10240], BF16, kind="Internal").ap()
    spwg_d = nc.dram_tensor("spwg", [128, 96], BF16, kind="Internal").ap()

    from contextlib import ExitStack
    with ExitStack() as _es:
        def _sb(*a):
            return _es.enter_context(nc.sbuf_tensor(*a))

        def _ps(*a):
            return _es.enter_context(nc.psum_tensor(*a))

        def _sem(n):
            return _es.enter_context(nc.semaphore(n))

        a16 = _sb("a16", [128, 79872], BF16)
        a32 = _sb("a32", [128, 11264], F32)
        st = _sb("stats", [128, 96], F32)
        wtbs = _sb("wtbs", [128, 8], F32)
        iota = _sb("iota_sb", [128, NEXP], F32)
        scin = _sb("scin", [1, 1], F32)
        scv = _sb("scv", [128, 1], F32)
        ident = _sb("ident_sb", [128, 128], F32)
        uones = _sb("uones_sb", [128, 128], BF16)
        uones32 = _sb("uones32_sb", [128, 128], F32)
        ones1f = _sb("ones1f_sb", [1, 128], F32)
        onesb = _sb("onesb_sb", [1, 128], BF16)
        ones2b = _sb("ones2b_sb", [2, 128], BF16)
        lnst = _sb("lnstats", [128, 96], F32)
        carry = _sb("carry", [1, D], F32)
        cst = _sb("cvals_sb", [128, 8], F32)
        for _k, _v in enumerate([LN_EPS, -0.5, -1.0, ISC, -ISC, 1000.0, FEPS, 0.0]):
            nc.const_aps.register(F32, _v, cst[:, _k:_k + 1])
        psA = _ps("psA", [128, 3072], F32)
        psB = _ps("psB", [128, 1024], F32)
        ld_c = _sem("ld_c")
        t2v = _sem("t2v")
        vch = _sem("vch")
        sch = _sem("sch")
        v2t = vch
        v2s = vch
        s_dn = sch
        ld_x = _sem("ld_x")
        c_dma = _sem("c_dma")
        st_h = _sem("st_h")
        ld_h = _sem("ld_h")
        st_o = _sem("st_o")
        ld_x2 = _sem("ld_x2")
        ld_h2 = _sem("ld_h2")
        st_h2 = _sem("st_h2")
        st_o2 = _sem("st_o2")
        spl = _sem("spl")
        spl2 = _sem("spl2")
        spe = _sem("spe")
        spf = _sem("spf")
        spf2 = _sem("spf2")
        block = _es.enter_context(nc.Block())
        P = Plan()

        A_wth, A_wtl = 0, 8192
        A_wdh0 = 16384
        A_tkh0 = 32768
        A_tkl0 = 32768 + 4800
        A_vth, A_vtl = 42368, 47168
        A_vwh0 = 51968
        A_vwl0 = 51968 + 5120
        A_wgh0 = 62208
        A_wgl0 = 62208 + 48
        A_W1h, A_W1l = 0, 32768
        A_xnTh, A_xnTl = 65536, 66560
        A_uh, A_ul = 67584, 68608
        B_wdh, B_wdl = 0, 8192
        B_tkh, B_tkl = 16384, 21184
        B_vwh, B_vwl = 25984, 31104
        B_wgh, B_wgl = 36224, 36272
        B_hnTh, B_hnTl = 36352, 37376
        B_maTh, B_maTl = 38400, 39040
        X_b1h, X_b1l = 69632, 73728
        X_wdbh, X_wdbl = 77824, 78848

        C_xb, C_xn, C_ku, C_q, C_g, C_g2, C_kv, C_hb = \
            0, 2048, 3072, 4096, 5120, 6144, 7168, 8192
        C_sq = 10240
        C_hb2, C_hn2, C_exps, C_mask, C_s1, C_s2 = \
            0, 3072, 6144, 6784, 7424, 8448
        C_s3 = 9472
        C_stage = 0
        C_vts = 8192
        C_wgs = 8792

        def ap16(off, n, p0=0, p1=128):
            return a16[p0:p1, off:off + n]

        def ap32(off, n, p0=0, p1=128):
            return a32[p0:p1, off:off + n]

        def W(eng, sem, val):
            if val > 0:
                P.ops[eng].append(lambda e, s=sem, v=val: e.wait_ge(s, v))

        def dma(eng, out, in_, sem, slow=False):
            v = P.inc(sem, 16)
            P.ops[eng].append(lambda e, o=out, i=in_, s=sem, sl=slow:
                              e.dma_start(out=o, in_=i,
                                          allow_slow_non_contiguous=sl).then_inc(s, 16))
            return v

        def op(eng, fn, sem=None):
            if eng in ("vector", "scalar"):
                ch = vch if eng == "vector" else sch
                prev = P.val(ch)
                if prev > 0:
                    P.ops[eng].append(lambda e, s=ch, v=prev: e.wait_ge(s, v))
                v = P.inc(ch, 1)
                P.ops[eng].append(lambda e, f=fn, s=ch: f(e).then_inc(s, 1))
                return v
            if sem is None:
                P.ops[eng].append(fn)
                return None
            v = P.inc(sem, 1)
            P.ops[eng].append(lambda e, f=fn, s=sem: f(e).then_inc(s, 1))
            return v

        def vsp(sem=None):
            return P.val(vch)

        def rsp():
            return op("vector", lambda v_: v_.tensor_copy(st[:, 94:95], st[:, 88:89]))

        def xsem(i):
            return ld_x if i % 2 == 0 else ld_x2

        def hsem(i):
            return ld_h if i % 2 == 0 else ld_h2

        def shsem(i):
            return st_h if i % 2 == 0 else st_h2

        def sosem(i):
            return st_o if i % 2 == 0 else st_o2

        def nop(eng, sem):
            v = P.inc(sem, 1)
            P.ops[eng].append(lambda e, s=sem: e.nop().then_inc(s, 1))
            return v

        def ln_tiny(o):
            vsp()
            vsp()
            op("vector", lambda v_: v_.tensor_scalar(st[:, o + 2:o + 3], st[:, o:o + 1],
                                                     1.0 / D, None, ALU.mult))
            op("vector", lambda v_: v_.tensor_scalar(st[:, o + 4:o + 5], st[:, o + 1:o + 2],
                                                     1.0 / D, None, ALU.mult))
            vsp()
            op("vector", lambda v_: v_.tensor_tensor(st[:, o + 3:o + 4], st[:, o + 2:o + 3],
                                                     st[:, o + 2:o + 3], ALU.mult))
            vsp()
            vsp()
            op("vector", lambda v_: v_.tensor_tensor(st[:, o + 5:o + 6], st[:, o + 4:o + 5],
                                                     st[:, o + 3:o + 4], ALU.subtract))
            vsp()
            return vsp(v2s)

        def ln_istd(o):
            op("scalar", lambda s, o=o: s.activation(st[:, o + 6:o + 7], st[:, o + 5:o + 6],
                                                     AF.Ln, bias=LN_EPS))
            return op("scalar", lambda s, o=o: s.activation(st[:, o + 7:o + 8],
                                                            st[:, o + 6:o + 7],
                                                            AF.Exp, scale=-0.5), s_dn)

        dma("sync", ident[:], id_d, ld_c)
        dma("sync", uones[:], uo_d, ld_c)
        dma("sync", uones32[:], uo32_d, ld_c)
        dma("sync", iota[:], io_d, ld_c)
        dma("sync", ones1f[:], o1_d, ld_c)
        dma("sync", onesb[:], ob_d, ld_c)
        dma("sync", ones2b[:], o2_d, ld_c)
        dma("sync", cst[:], cv_d, ld_c)
        dma("sync", scin[:], sc_d.rearrange("(a n) -> a n", a=1), ld_c)
        dma("sync", ap32(C_wgs, 48).rearrange("p (c e) -> p c e", e=6),
            wg_d.rearrange("(c p) e -> p c e", p=128), ld_c)
        dma("sync", a32[0:1, 0:4096], b1_d.rearrange("(a n) -> a n", a=1), ld_c)
        dma("sync", wtbs[:], wtb_d.rearrange("(c p) -> p c", p=128), ld_c, slow=True)
        v_bias = dma("sync", a32[0:1, 4096:5120], wdb_d.rearrange("(a n) -> a n", a=1), ld_c)

        op("vector", lambda v_: v_.memset(st[:, 84:96], 0.0))
        W("vector", ld_c, v_bias)
        op("vector", lambda v_: v_.tensor_copy(ap16(X_b1h, 4096, 0, 1), a32[0:1, 0:4096]))
        op("vector", lambda v_: v_.tensor_tensor(ap16(X_b1l, 4096, 0, 1), a32[0:1, 0:4096],
                                                 ap16(X_b1h, 4096, 0, 1), ALU.subtract))
        op("vector", lambda v_: v_.tensor_copy(ap16(X_wdbh, 1024, 0, 1), a32[0:1, 4096:5120]))
        vb1 = op("vector", lambda v_: v_.tensor_tensor(ap16(X_wdbl, 1024, 0, 1),
                                                       a32[0:1, 4096:5120],
                                                       ap16(X_wdbh, 1024, 0, 1),
                                                       ALU.subtract), v2t)

        W("sync", v2t, vb1)
        dma("sync", a16[1:2, X_b1h:X_b1h + 4096], ap16(X_b1l, 4096, 0, 1), ld_c)
        v_brow = dma("sync", a16[1:2, X_wdbh:X_wdbh + 1024], ap16(X_wdbl, 1024, 0, 1), ld_c)
        dma("sync", a32[0:16, 6144:6744].rearrange("r (e l) -> r e l", l=LT),
            at_d.rearrange("e r l -> r e l"), ld_c)
        v_ld = dma("sync", a32[0:16, 0:6144].rearrange("r (e c) -> r e c", c=D),
                   bm_d.rearrange("e r c -> r e c"), ld_c)

        _wt_slot_last = [0, 0]

        def wt_chunk(c):
            slot = 8840 + (c % 2) * 1024
            if _wt_slot_last[c % 2] > 0:
                W("sync", vch, _wt_slot_last[c % 2])
            sm = spl if c % 2 == 0 else spl2
            vd = dma("sync", ap32(slot, 1024), wt_d[c * 128:(c + 1) * 128, :], sm)
            W("vector", sm, vd)
            op("vector", lambda v_, c=c, sl=slot:
               v_.tensor_copy(ap16(A_wth + c * 1024, 1024), ap32(sl, 1024)))
            vv = op("vector", lambda v_, c=c, sl=slot:
                    v_.tensor_tensor(ap16(A_wtl + c * 1024, 1024), ap32(sl, 1024),
                                     ap16(A_wth + c * 1024, 1024), ALU.subtract), v2t)
            _wt_slot_last[c % 2] = vv
            return vv

        W("tensor", ld_c, v_ld)
        tok_mm_done = None
        _tk_evac = [0, 0]
        for c in range(8):
            pso = psB if c % 2 == 0 else psA
            if _tk_evac[c % 2] > 0:
                W("tensor", v2t, _tk_evac[c % 2])
            for e in range(NEXP):
                lhs = a32[0:16, e * 1024 + c * 128: e * 1024 + (c + 1) * 128]
                rhs = a32[0:16, 6144 + e * LT: 6144 + (e + 1) * LT]
                o = pso[:, e * 128: e * 128 + LT]
                last = (e == NEXP - 1)
                v = op("tensor", lambda t, o=o, l=lhs, r=rhs: t.matmul(o, l, r, start=True,
                                                                      stop=True),
                       t2v if last else None)
            tok_mm_done = v
            W("vector", t2v, v)
            src = pso[:, 0:768].rearrange("p (e x) -> p e x", x=128)[:, :, 0:LT]
            dsth = ap16(A_tkh0 + c * EL, EL).rearrange("p (e l) -> p e l", l=LT)
            dstl = ap16(A_tkl0 + c * EL, EL).rearrange("p (e l) -> p e l", l=LT)
            op("vector", lambda v_, d=dsth, s=src: v_.tensor_copy(d, s))
            v = op("vector", lambda v_, d=dstl, s=src, hh=dsth:
                   v_.tensor_tensor(d, s, hh, ALU.subtract), v2t)
            _tk_evac[c % 2] = v
            wt_chunk(c)
        tok_evac_done = P.val(v2t)
        wt_ready = tok_evac_done

        W("tensor", v2t, tok_evac_done)
        W("tensor", ld_c, v_ld)
        v = op("tensor", lambda t: t.matmul(psB[:, 0:1], ones1f[:], scin[:], start=True,
                                            stop=True), t2v)
        W("vector", t2v, v)
        op("vector", lambda v_: v_.tensor_copy(scv[:], psB[:, 0:1]))
        op("vector", lambda v_: v_.tensor_copy(ap16(A_wgh0, 48), ap32(C_wgs, 48)))
        op("vector", lambda v_: v_.tensor_tensor(ap16(A_wgl0, 48), ap32(C_wgs, 48),
                                                 ap16(A_wgh0, 48), ALU.subtract), v2t)
        spill_ready = P.val(v2t)

        _slot_last = [0, 0]

        def split_weight(dram_ap, ncols, offh, offl, pre_wait=0):
            splits = []
            for c in range(8):
                slot = C_stage + (c % 2) * 4096
                if _slot_last[c % 2] > 0:
                    W("sync", vch, _slot_last[c % 2])
                sm = spl if c % 2 == 0 else spl2
                vd = dma("sync", ap32(slot, ncols), dram_ap[c * 128:(c + 1) * 128, :], sm)
                W("vector", sm, vd)
                if c == 0 and pre_wait:
                    W("vector", ld_c, pre_wait)
                op("vector", lambda v_, c=c, s=slot, n=ncols, oh=offh:
                   v_.tensor_copy(ap16(oh + c * n, n), ap32(s, n)))
                vv = op("vector", lambda v_, c=c, s=slot, n=ncols, oh=offh, ol=offl:
                        v_.tensor_tensor(ap16(ol + c * n, n), ap32(s, n), ap16(oh + c * n, n),
                                         ALU.subtract), v2t)
                splits.append(vv)
                _slot_last[c % 2] = vv
            return P.val(v2t)

        W("sync", t2v, tok_mm_done)
        W("sync", v2t, vb1)
        wd_splits = []

        def wd_chunk_stage(c):
            slot = C_stage + (c % 2) * 4096
            if _slot_last[c % 2] > 0:
                W("sync", vch, _slot_last[c % 2])
            else:
                W("sync", vch, _wt_slot_last[c % 2])
            sm = spl if c % 2 == 0 else spl2
            vd = dma("sync", ap32(slot, 1024), wd_d[c * 128:(c + 1) * 128, :], sm)
            W("vector", sm, vd)
            op("vector", lambda v_, c=c, sl=slot:
               v_.tensor_copy(ap16(A_wdh0 + c * 1024, 1024), ap32(sl, 1024)))
            vv = op("vector", lambda v_, c=c, sl=slot:
                    v_.tensor_tensor(ap16(A_wdh0 + 8192 + c * 1024, 1024), ap32(sl, 1024),
                                     ap16(A_wdh0 + c * 1024, 1024), ALU.subtract), v2t)
            _slot_last[c % 2] = vv
            return vv

        W("tensor", v2t, wt_ready)
        _vt_evac = [0, 0]
        for d in range(8):
            base = (d % 2) * 1024
            if _vt_evac[d % 2] > 0:
                W("tensor", v2t, _vt_evac[d % 2])
            for p in range(3):
                lo = (A_wth, A_wth, A_wtl)[p]
                ro = (A_tkh0, A_tkl0, A_tkh0)[p]
                for c in range(8):
                    for s0, s1 in ((0, 512), (512, EL)):
                        lhs = ap16(lo + c * 1024 + d * 128, 128)
                        rhs = ap16(ro + c * EL + s0, s1 - s0)
                        o = psA[:, base + s0:base + s1]
                        stt = (p == 0 and c == 0)
                        stp = (p == 2 and c == 7)
                        last = stp and s1 == EL
                        v = op("tensor", lambda t, o=o, l=lhs, r=rhs, a=stt, b=stp:
                               t.matmul(o, l, r, start=a, stop=b), t2v if last else None)
            W("vector", t2v, v)
            op("vector", lambda v_, d=d, b=base: v_.tensor_scalar(ap32(C_vts, EL),
                                                                  psA[:, b:b + EL],
                                                                  wtbs[:, d:d + 1], None,
                                                                  ALU.add))
            op("vector", lambda v_, d=d: v_.tensor_copy(ap16(A_vth + d * EL, EL),
                                                        ap32(C_vts, EL)))
            vv = op("vector", lambda v_, d=d:
                    v_.tensor_tensor(ap16(A_vtl + d * EL, EL), ap32(C_vts, EL),
                                     ap16(A_vth + d * EL, EL), ALU.subtract), v2t)
            _vt_evac[d % 2] = vv
            wd_splits.append(wd_chunk_stage(d))
        vt_mm_done = P.val(t2v)
        wd_ready = P.val(vch)

        W("gpsimd", v2t, spill_ready)
        dma("gpsimd", sptk_d, ap16(A_tkh0, 9600), spe)
        dma("gpsimd", spwg_d, ap16(A_wgh0, 96), spe)

        op("vector", lambda v_: v_.memset(
            ap16(A_vth, 4800).rearrange("p (d e l) -> p d e l", e=6, l=LT)[:, :, :, 0:1], 0.0))
        op("vector", lambda v_: v_.memset(
            ap16(A_vtl, 4800).rearrange("p (d e l) -> p d e l", e=6, l=LT)[:, :, :, 0:1], 0.0),
           v2t)
        l0z = P.val(vch)


        sp_done = [0]

        def w1_chunk(c, q="sync", sms=None, hi_on_scalar=False):
            slot = C_stage + (c % 2) * 4096
            if _slot_last[c % 2] > 0:
                W(q, vch, _slot_last[c % 2])
            sm = (sms or (spl, spl2))[c % 2]
            vd = dma(q, ap32(slot, 4096), w1_d[c * 128:(c + 1) * 128, :], sm)
            if hi_on_scalar:
                W("scalar", sm, vd)
                if c == 4:
                    W("scalar", ld_c, sp_done[0])
                    W("scalar", spe, 32)
                hv = op("scalar", lambda sc_, c=c, s=slot:
                        sc_.copy(ap16(A_W1h + c * 4096, 4096), ap32(s, 4096)), s_dn)
                W("vector", s_dn, hv)
            else:
                W("vector", sm, vd)
                op("vector", lambda v_, c=c, s=slot:
                   v_.tensor_copy(ap16(A_W1h + c * 4096, 4096), ap32(s, 4096)))
            vv = op("vector", lambda v_, c=c, s=slot:
                    v_.tensor_tensor(ap16(A_W1l + c * 4096, 4096), ap32(s, 4096),
                                     ap16(A_W1h + c * 4096, 4096), ALU.subtract), v2t)
            _slot_last[c % 2] = vv

        W("vector", t2v, vt_mm_done)
        W("vector", spe, 32)
        w1_chunk(0)
        w1_chunk(1)

        op("vector", lambda v_: v_.memset(ap16(A_vwh0 + 4 * 1024, 1024, 64, 128), 0.0))
        op("vector", lambda v_: v_.memset(ap16(A_vwl0 + 4 * 1024, 1024, 64, 128), 0.0))

        W("tensor", v2t, wd_ready)
        W("tensor", v2t, l0z)
        _vw_evac = [0, 0]
        for k in range(5):
            kc = ELC[k]
            base = (k % 2) * 1024
            if _vw_evac[k % 2] > 0:
                W("tensor", v2t, _vw_evac[k % 2])
            for p in range(3):
                lo = (A_vth, A_vth, A_vtl)[p]
                ro = (A_wdh0, A_wdh0 + 8192, A_wdh0)[p]
                for dd in range(8):
                    for s in range(2):
                        lhs = ap16(lo + dd * EL + k * 128, kc)
                        rhs = ap16(ro + dd * 1024 + s * 512, 512)
                        o = psA[0:kc, base + s * 512:base + (s + 1) * 512]
                        stt = (p == 0 and dd == 0)
                        stp = (p == 2 and dd == 7)
                        last = stp and s == 1
                        v = op("tensor", lambda t, o=o, l=lhs, r=rhs, a=stt, b=stp:
                               t.matmul(o, l, r, start=a, stop=b), t2v if last else None)
            W("vector", t2v, v)
            op("vector", lambda v_, k=k, kc=kc, b=base:
               v_.tensor_copy(ap16(A_vwh0 + k * 1024, 1024, 0, kc), psA[0:kc, b:b + 1024]))
            vv = op("vector", lambda v_, k=k, kc=kc, b=base:
                    v_.tensor_tensor(ap16(A_vwl0 + k * 1024, 1024, 0, kc),
                                     psA[0:kc, b:b + 1024],
                                     ap16(A_vwh0 + k * 1024, 1024, 0, kc), ALU.subtract), v2t)
            _vw_evac[k % 2] = vv
        vw_mm_done = P.val(t2v)

        W("vector", t2v, vw_mm_done)
        w1_chunk(2)
        w1_chunk(3)

        W("sync", v2t, P.val(v2t))
        dma("sync", spwd_d, ap16(A_wdh0, 16384), ld_c)
        sp_done[0] = dma("sync", spvw_d, ap16(A_vwh0, 10240), ld_c)

        W("vector", ld_c, sp_done[0])
        W("vector", spe, 32)
        for c in range(4, 8):
            w1_chunk(c, q="scalar", sms=(spf, spf2), hi_on_scalar=True)
        op("vector", lambda v_: v_.memset(carry[:], 0.0), v2t)

        bA_v, bA_s, bA_t = P.val(vch), P.val(sch), P.val(t2v)

        for _e in ("sync", "gpsimd", "scalar", "tensor"):
            W(_e, vch, bA_v)
            W(_e, sch, bA_s)
            W(_e, t2v, bA_t)
        W("scalar", ld_c, P.val(ld_c))
        W("tensor", ld_c, sp_done[0])

        def xslot(i):
            return ap32(C_xb + (i % 2) * 1024, 1024)

        lm = {}
        lm[("ldx", 0)] = dma("sync", xslot(0), x_d[0:128, :], xsem(0))
        if NT > 1:
            lm[("ldx", 1)] = dma("sync", xslot(1), x_d[128:256, :], xsem(1))

        W("scalar", xsem(0), lm[("ldx", 0)])
        v = op("scalar", lambda s: s.activation(ap32(C_sq, 1024), xslot(0), AF.Square,
                                                accum_out=st[:, 1:2]), s_dn)
        lm[("sqx", 0)] = v
        W("vector", xsem(0), lm[("ldx", 0)])
        op("vector", lambda v_: v_.reduce_sum(st[:, 0:1], xslot(0), axis=AXX))
        W("vector", s_dn, lm[("sqx", 0)])
        v = ln_tiny(0)
        W("scalar", v2s, v)
        lm[("istd1", 0)] = ln_istd(0)
        W("vector", s_dn, lm[("istd1", 0)])
        vsp()
        v = op("vector", lambda v_: v_.tensor_scalar(ap32(C_xn, 1024), xslot(0), st[:, 2:3],
                                                     st[:, 7:8], ALU.subtract, ALU.mult), v2t)
        lm[("xn", 0)] = v
        W("tensor", v2t, lm[("xn", 0)])
        for c in range(8):
            v = op("tensor", lambda t, o=psA[:, c * 128:(c + 1) * 128],
                   s=ap32(C_xn + c * 128, 128): t.transpose(o, s, ident[:]),
                   t2v if c == 7 else None)
        lm[("tp", 0)] = v
        W("vector", t2v, lm[("tp", 0)])
        op("vector", lambda v_: v_.tensor_copy(ap16(A_xnTh, 512), psA[:, 0:512]))
        v = op("vector", lambda v_: v_.tensor_tensor(ap16(A_xnTl, 512), psA[:, 0:512],
                                                     ap16(A_xnTh, 512), ALU.subtract), v2t)
        lm[("xnT1", 0)] = v
        op("vector", lambda v_: v_.tensor_copy(ap16(A_xnTh + 512, 512), psA[:, 512:1024]))
        v = op("vector", lambda v_: v_.tensor_tensor(ap16(A_xnTl + 512, 512),
                                                     psA[:, 512:1024],
                                                     ap16(A_xnTh + 512, 512),
                                                     ALU.subtract), v2t)
        lm[("xnT", 0)] = v

        for i in range(NT):
            xb = xslot(i)
            hb = ap32(C_hb + (i % 2) * 1024, 1024)

            if 0 < i and i + 1 < NT:
                W("sync", v2t, lm[("h", i - 1)])
                lm[("ldx", i + 1)] = dma("sync", xslot(i + 1),
                                         x_d[(i + 1) * 128:(i + 2) * 128, :], xsem(i + 1))

            if i + 1 < NT:
                W("scalar", xsem(i + 1), lm[("ldx", i + 1)])
                if i > 0:
                    W("scalar", v2t, lm[("h", i - 1)])
                v = op("scalar", lambda s, j=i + 1: s.activation(ap32(C_sq, 1024), xslot(j),
                                                                 AF.Square,
                                                                 accum_out=st[:, 1:2]), s_dn)
                lm[("sqx", i + 1)] = v

            if i + 1 < NT:
                W("vector", xsem(i + 1), lm[("ldx", i + 1)])
                op("vector", lambda v_, j=i + 1: v_.reduce_sum(st[:, 0:1], xslot(j), axis=AXX))
                W("vector", s_dn, lm[("sqx", i + 1)])
                lm[("var1", i + 1)] = ln_tiny(0)
                W("scalar", v2s, lm[("var1", i + 1)])
                lm[("istd1", i + 1)] = ln_istd(0)
                W("vector", s_dn, lm[("istd1", i + 1)])
                v = op("vector", lambda v_, j=i + 1: v_.tensor_scalar(ap32(C_xn, 1024),
                                                                      xslot(j), st[:, 2:3],
                                                                      st[:, 7:8], ALU.subtract,
                                                                      ALU.mult), v2t)
                lm[("xn", i + 1)] = v

            if i > 0:
                W("tensor", s_dn, lm[("kvf", i - 1)])
            for s in range(2, 6):
                op("tensor", lambda t, o=psA[:, s * 512:(s + 1) * 512],
                   r=a16[0:2, X_b1h + s * 512: X_b1h + (s + 1) * 512]:
                   t.matmul(o, ones2b[:], r, start=True, stop=False))
            W("tensor", v2t, lm[("xnT1", i)])
            for c in range(4):
                for p in range(3):
                    lo = (A_xnTh, A_xnTh, A_xnTl)[p]
                    ro = (A_W1h, A_W1l, A_W1h)[p]
                    lhs = ap16(lo + c * 128, 128)
                    for s in range(2, 6):
                        rhs = ap16(ro + c * 4096 + s * 512, 512)
                        op("tensor", lambda t, o=psA[:, s * 512:(s + 1) * 512], l=lhs, r=rhs:
                           t.matmul(o, l, r, start=False, stop=False))
            W("tensor", v2t, lm[("xnT", i)])
            for sb_ in range(6, 8):
                op("tensor", lambda t, o=psB[:, (sb_ - 6) * 512:(sb_ - 5) * 512],
                   r=a16[0:2, X_b1h + sb_ * 512: X_b1h + (sb_ + 1) * 512]:
                   t.matmul(o, ones2b[:], r, start=True, stop=False))
            for c in range(4):
                for p in range(3):
                    lo = (A_xnTh, A_xnTh, A_xnTl)[p]
                    lhs = ap16(lo + c * 128, 128)
                    for s in range(6, 8):
                        rhs = ap16((A_W1h, A_W1l, A_W1h)[p] + c * 4096 + s * 512, 512)
                        op("tensor", lambda t, o=psB[:, (s - 6) * 512:(s - 5) * 512], l=lhs,
                           r=rhs: t.matmul(o, l, r, start=False, stop=False))
            for c in range(4, 8):
                for p in range(3):
                    lo = (A_xnTh, A_xnTh, A_xnTl)[p]
                    ro = (A_W1h, A_W1l, A_W1h)[p]
                    lhs = ap16(lo + c * 128, 128)
                    for s in range(2, 8):
                        rhs = ap16(ro + c * 4096 + s * 512, 512)
                        o = (psA[:, s * 512:(s + 1) * 512] if s < 6
                             else psB[:, (s - 6) * 512:(s - 5) * 512])
                        stp = (c == 7 and p == 2)
                        last = stp and s == 7
                        v = op("tensor", lambda t, o=o, l=lhs, r=rhs, b=stp:
                               t.matmul(o, l, r, start=False, stop=b), t2v if last else None)
            lm[("qkv1", i)] = v
            if i > 0:
                W("tensor", s_dn, lm[("q", i - 1)])
            for s in range(2):
                op("tensor", lambda t, o=psA[:, s * 512:(s + 1) * 512],
                   r=a16[0:2, X_b1h + s * 512: X_b1h + (s + 1) * 512]:
                   t.matmul(o, ones2b[:], r, start=True, stop=False))
            for c in range(8):
                for p in range(3):
                    lo = (A_xnTh, A_xnTh, A_xnTl)[p]
                    ro = (A_W1h, A_W1l, A_W1h)[p]
                    lhs = ap16(lo + c * 128, 128)
                    for s in range(2):
                        rhs = ap16(ro + c * 4096 + s * 512, 512)
                        stp = (c == 7 and p == 2)
                        last = stp and s == 1
                        v = op("tensor", lambda t, o=psA[:, s * 512:(s + 1) * 512], l=lhs,
                               r=rhs, b=stp: t.matmul(o, l, r, start=False, stop=b),
                               t2v if last else None)
            lm[("qkv", i)] = v

            W("scalar", t2v, lm[("qkv1", i)])
            v = op("scalar", lambda s: s.copy(ap32(C_ku, 1024), psA[:, 1024:2048]), s_dn)
            lm[("k", i)] = v
            v = op("scalar", lambda s: s.activation(ap32(C_g2, 1024), psB[:, 0:1024], AF.Exp,
                                                    scale=-1.0), s_dn)
            lm[("eg", i)] = v
            W("scalar", t2v, lm[("qkv", i)])
            v = op("scalar", lambda s: s.copy(ap32(C_q, 1024), psA[:, 0:1024]), s_dn)
            lm[("q", i)] = v

            W("vector", t2v, lm[("qkv1", i)])
            W("vector", s_dn, lm[("k", i)])
            if i > 0:
                W("vector", c_dma, 16 * i)
            v = op("vector", lambda v_: v_.tensor_tensor(ap32(C_kv, 1024), ap32(C_ku, 1024),
                                                         psA[:, 2048:3072], ALU.mult), v2t)
            lm[("u", i)] = v

            W("tensor", v2t, lm[("u", i)])
            W("tensor", s_dn, lm[("eg", i)])
            if i > 0:
                W("tensor", c_dma, 16 * i)
            for s in range(2):
                oo = psB[:, s * 512:(s + 1) * 512]
                op("tensor", lambda t, o=oo, s=s: t.matmul(o, uones32[:],
                                                           ap32(C_kv + s * 512, 512),
                                                           start=True, stop=False))
                v = op("tensor", lambda t, o=oo, s=s: t.matmul(o, ones1f[:],
                                                               carry[0:1, s * 512:(s + 1) * 512],
                                                               start=False, stop=True),
                       t2v if s == 1 else None)
            lm[("cs", i)] = v

            W("scalar", t2v, lm[("cs", i)])
            v = op("scalar", lambda s: s.copy(ap32(C_kv, 1024), psB[:, 0:1024]), s_dn)
            lm[("kvf", i)] = v

            W("gpsimd", s_dn, lm[("kvf", i)])
            dma("gpsimd", carry[:], ap32(C_kv, 1024, 127, 128), c_dma)

            if i + 1 < NT:
                W("tensor", v2t, lm[("xn", i + 1)])
                W("tensor", s_dn, lm[("q", i)])
                for c in range(8):
                    v = op("tensor", lambda t, o=psA[:, c * 128:(c + 1) * 128],
                           s=ap32(C_xn + c * 128, 128): t.transpose(o, s, ident[:]),
                           t2v if c == 7 else None)
                lm[("tp", i + 1)] = v
                W("vector", t2v, lm[("tp", i + 1)])
                op("vector", lambda v_: v_.tensor_copy(ap16(A_xnTh, 512), psA[:, 0:512]))
                v = op("vector", lambda v_: v_.tensor_tensor(ap16(A_xnTl, 512), psA[:, 0:512],
                                                             ap16(A_xnTh, 512),
                                                             ALU.subtract), v2t)
                lm[("xnT1", i + 1)] = v
                op("vector", lambda v_: v_.tensor_copy(ap16(A_xnTh + 512, 512),
                                                       psA[:, 512:1024]))
                v = op("vector", lambda v_: v_.tensor_tensor(ap16(A_xnTl + 512, 512),
                                                             psA[:, 512:1024],
                                                             ap16(A_xnTh + 512, 512),
                                                             ALU.subtract), v2t)
                lm[("xnT", i + 1)] = v

            W("vector", s_dn, lm[("q", i)])
            op("vector", lambda v_: v_.tensor_scalar(ap32(C_g, 1024), ap32(C_g2, 1024), 1.0,
                                                     None, ALU.add))
            op("vector", lambda v_: v_.reciprocal(ap32(C_g2, 1024), ap32(C_g, 1024)))
            rsp()
            rsp()
            if i + 1 < NT:
                W("vector", s_dn, lm[("sqx", i + 1)])
            W("vector", s_dn, lm[("kvf", i)])
            op("vector", lambda v_: v_.tensor_tensor(ap32(C_sq, 1024), ap32(C_q, 1024),
                                                     ap32(C_kv, 1024), ALU.mult))
            v = op("vector", lambda v_: v_.tensor_tensor(ap32(C_g, 1024), ap32(C_sq, 1024),
                                                         ap32(C_g2, 1024), ALU.mult), v2s)
            lm[("a", i)] = v
            W("scalar", v2s, lm[("a", i)])
            v = op("scalar", lambda s: s.activation(ap32(C_sq, 1024), ap32(C_g, 1024),
                                                    AF.Square, accum_out=st[:, 9:10]), s_dn)
            lm[("sqa", i)] = v
            op("vector", lambda v_: v_.reduce_sum(st[:, 8:9], ap32(C_g, 1024), axis=AXX))
            W("vector", s_dn, lm[("sqa", i)])
            lm[("var2", i)] = ln_tiny(8)
            W("scalar", v2s, lm[("var2", i)])
            lm[("istd2", i)] = ln_istd(8)
            W("vector", s_dn, lm[("istd2", i)])
            vsp()
            if i >= 2:
                W("vector", shsem(i - 2), lm[("sth", i - 2)])
            op("vector", lambda v_: v_.tensor_scalar(ap32(C_sq, 1024), ap32(C_g, 1024),
                                                     st[:, 10:11], st[:, 15:16],
                                                     ALU.subtract, ALU.mult))
            v = op("vector", lambda v_, x=xb, h=hb: v_.tensor_tensor(h, x, ap32(C_sq, 1024),
                                                                     ALU.add), v2t)
            lm[("h", i)] = v

            W("gpsimd", v2t, lm[("h", i)])
            lm[("sth", i)] = dma("gpsimd", h_d[i * 128:(i + 1) * 128, :], hb, shsem(i))

            op("vector", lambda v_, h=hb, j=i: v_.reduce_sum(lnst[:, 2 * j:2 * j + 1], h,
                                                             axis=AXX))
            W("scalar", v2t, lm[("h", i)])
            v = op("scalar", lambda s, h=hb, j=i: s.activation(ap32(C_sq, 1024), h, AF.Square,
                                                               accum_out=lnst[:, 2 * j + 1:
                                                                              2 * j + 2]),
                   s_dn)
            lm[("sqh", i)] = v

        bB_v, bB_s, bB_t = P.val(vch), P.val(sch), P.val(t2v)

        def hslot(i):
            return ap32(C_hb2 + (i % 3) * 1024, 1024)

        def hnslot(i):
            return ap32(C_hn2 + (i % 3) * 1024, 1024)

        W("sync", vch, lm[("h", NT - 2)])
        W("sync", shsem(0), lm[("sth", 0)])
        lm[("ldh", 0)] = dma("sync", hslot(0), h_d[0:128, :], hsem(0))
        W("sync", t2v, lm[("qkv", NT - 1)])
        W("sync", spe, 32)
        dma("sync", ap16(B_tkh, 9600), sptk_d, spl)
        rl_a = dma("sync", ap16(B_wgh, 96), spwg_d, spl)
        dma("sync", ap16(B_wdh, 16384), spwd_d, spl2)
        rl_b = dma("sync", ap16(B_vwh, 10240), spvw_d, spl2)

        for _e in ("sync", "gpsimd", "scalar", "tensor"):
            W(_e, vch, bB_v)
            W(_e, sch, bB_s)
            W(_e, t2v, bB_t)
        for _e in ("vector", "scalar", "sync", "gpsimd"):
            W(_e, c_dma, P.val(c_dma))
            W(_e, ld_x, P.val(ld_x))
            W(_e, ld_x2, P.val(ld_x2))
            W(_e, st_h, P.val(st_h))
            W(_e, st_h2, P.val(st_h2))

        if NT > 1:
            W("sync", shsem(1), lm[("sth", 1)])
            lm[("ldh", 1)] = dma("sync", hslot(1), h_d[128:256, :], hsem(1))
        op("vector", lambda v_: v_.tensor_copy(st[:, 0:2], lnst[:, 0:2]))
        v = ln_tiny(0)
        W("scalar", v2s, v)
        lm[("istd3", 0)] = ln_istd(0)
        W("vector", s_dn, lm[("istd3", 0)])
        vsp()
        W("vector", hsem(0), lm[("ldh", 0)])
        v = op("vector", lambda v_: v_.tensor_scalar(hnslot(0), hslot(0), st[:, 2:3],
                                                     st[:, 7:8], ALU.subtract, ALU.mult), v2t)
        lm[("hn", 0)] = v
        W("tensor", v2t, lm[("hn", 0)])
        for c in range(8):
            v = op("tensor", lambda t, o=psA[:, 1536 + c * 128:1664 + c * 128],
                   s=ap32(C_hn2 + c * 128, 128): t.transpose(o, s, ident[:]),
                   t2v if c == 7 else None)
        lm[("tpB", 0)] = v
        W("vector", t2v, lm[("tpB", 0)])
        op("vector", lambda v_: v_.tensor_copy(ap16(B_hnTh, 512), psA[:, 1536:2048]))
        v = op("vector", lambda v_: v_.tensor_tensor(ap16(B_hnTl, 512), psA[:, 1536:2048],
                                                     ap16(B_hnTh, 512), ALU.subtract), v2t)
        lm[("hnT1", 0)] = v
        op("vector", lambda v_: v_.tensor_copy(ap16(B_hnTh + 512, 512), psA[:, 2048:2560]))
        v = op("vector", lambda v_: v_.tensor_tensor(ap16(B_hnTl + 512, 512),
                                                     psA[:, 2048:2560],
                                                     ap16(B_hnTh + 512, 512),
                                                     ALU.subtract), v2t)
        lm[("hnT", 0)] = v

        for i in range(NT):
            hb = hslot(i)

            if i >= 1 and i + 1 < NT:
                if i >= 2:
                    W("sync", sosem(i - 2), lm[("sto", i - 2)])
                W("sync", vch, lm[("hn", i - 1)])
                W("sync", shsem(i + 1), lm[("sth", i + 1)])
                lm[("ldh", i + 1)] = dma("sync", hslot(i + 1),
                                         h_d[(i + 1) * 128:(i + 2) * 128, :], hsem(i + 1))

            if i == 0:
                W("tensor", spl, rl_a)
            W("tensor", v2t, lm[("hnT", i)])
            for c in range(8):
                for p in range(3):
                    lo = (B_hnTh, B_hnTh, B_hnTl)[p]
                    lhs = ap16(lo + c * 128, 128)
                    stt = (c == 0 and p == 0)
                    stp = (c == 7 and p == 2)
                    rhs = ap16((B_wgh, B_wgl, B_wgh)[p] + c * 6, 6)
                    op("tensor", lambda t, o=psA[:, 1024:1030], l=lhs, r=rhs, a=stt, b=stp:
                       t.matmul(o, l, r, start=a, stop=b))
                    for s0, s1 in ((0, 512), (512, EL)):
                        rhs = ap16((B_tkh, B_tkl, B_tkh)[p] + c * EL + s0, s1 - s0)
                        last = stp and s1 == EL
                        v = op("tensor", lambda t, o=psA[:, 1536 + s0:1536 + s1], l=lhs, r=rhs,
                               a=stt, b=stp: t.matmul(o, l, r, start=a, stop=b),
                               t2v if last else None)
            lm[("sc", i)] = v

            if i > 0:
                W("scalar", t2v, lm[("dl", i - 1)])
                if i > 1:
                    W("scalar", v2t, lm[("ot", i - 2)])
                v = op("scalar", lambda s: s.activation(ap32(C_s2, 1024), psA[:, 0:1024],
                                                        AF.Copy, scale=scv[:]), s_dn)
                lm[("s2", i - 1)] = v

            if i + 1 < NT:
                op("vector", lambda v_, j=i + 1: v_.tensor_copy(st[:, 0:2],
                                                                lnst[:, 2 * j:2 * j + 2]))
                lm[("var3", i + 1)] = ln_tiny(0)
                W("scalar", v2s, lm[("var3", i + 1)])
                lm[("istd3", i + 1)] = ln_istd(0)

            W("vector", t2v, lm[("sc", i)])
            op("vector", lambda v_: v_.tensor_copy(st[:, 32:38], psA[:, 1024:1030]))
            for e in range(NEXP):
                op("vector", lambda v_, e=e: v_.reduce_max(st[:, 54 + e:55 + e],
                                                           psA[:, 1536 + e * LT:
                                                               1536 + (e + 1) * LT],
                                                           axis=AXX))
            op("vector", lambda v_: v_.reduce_max(st[:, 38:39], st[:, 32:38], axis=AXX))
            vsp()
            op("vector", lambda v_: v_.tensor_scalar(st[:, 60:66], st[:, 54:60], -ISC, 1000.0,
                                                     ALU.mult, ALU.subtract))
            vsp()
            op("vector", lambda v_: v_.tensor_scalar(st[:, 40:46], st[:, 32:38], st[:, 38:39],
                                                     1000.0, ALU.is_equal, ALU.mult))
            vsp()
            vsp()
            op("vector", lambda v_: v_.tensor_tensor(st[:, 48:54], st[:, 40:46], iota[:],
                                                     ALU.subtract))
            vsp()
            vsp()
            op("vector", lambda v_: v_.reduce_max(st[:, 46:47], st[:, 48:54], axis=AXX))
            vsp()
            vsp()
            op("vector", lambda v_: v_.tensor_scalar(st[:, 84:90], st[:, 48:54], st[:, 46:47],
                                                     1000.0, ALU.is_equal, ALU.mult))
            rsp()
            rsp()
            op("vector", lambda v_: v_.tensor_tensor(st[:, 54:60], st[:, 60:66], st[:, 84:90],
                                                     ALU.add))
            rsp()
            rsp()
            lm[("am", i)] = vsp(v2s)

            if i + 1 < NT:
                W("vector", s_dn, lm[("istd3", i + 1)])
                W("vector", hsem(i + 1), lm[("ldh", i + 1)])
                v = op("vector", lambda v_, j=i + 1: v_.tensor_scalar(
                    hnslot(j), hslot(j), st[:, 2:3], st[:, 7:8], ALU.subtract, ALU.mult), v2t)
                lm[("hn", i + 1)] = v

            if i == 0:
                W("tensor", spl2, rl_b)
            if i > 0:
                W("tensor", s_dn, lm[("s2", i - 1)])
            for c in range(8):
                for p in range(3):
                    lo = (B_hnTh, B_hnTh, B_hnTl)[p]
                    lhs = ap16(lo + c * 128, 128)
                    stt = (c == 0 and p == 0)
                    stp = (c == 7 and p == 2)
                    for s in range(2):
                        rhs = ap16((B_wdh, B_wdl, B_wdh)[p] + c * 1024 + s * 512, 512)
                        last = stp and s == 1
                        v = op("tensor", lambda t, o=psA[:, s * 512:(s + 1) * 512], l=lhs,
                               r=rhs, a=stt: t.matmul(o, l, r, start=a, stop=False),
                               t2v if last else None)
            lm[("xw", i)] = v

            if i > 0:
                v = op("scalar", lambda s: s.activation(ap32(C_s3, 1024), ap32(C_s2, 1024),
                                                        AF.Exp), s_dn)
                lm[("expd", i - 1)] = v
            W("scalar", v2s, lm[("am", i)])
            for e in range(NEXP):
                v = op("scalar", lambda s, e=e: s.activation(ap32(C_exps + e * LT, LT),
                                                             psA[:, 1536 + e * LT:
                                                                 1536 + (e + 1) * LT],
                                                             AF.Exp,
                                                             bias=st[:, 54 + e:55 + e],
                                                             scale=ISC),
                       s_dn if e == NEXP - 1 else None)
            lm[("exp", i)] = v

            if i > 0:
                W("vector", s_dn, lm[("expd", i - 1)])
                op("vector", lambda v_: v_.tensor_scalar(ap32(C_s1, 1024), ap32(C_s3, 1024),
                                                         0.0, FEPS, ALU.is_equal, ALU.mult))
                v = op("vector", lambda v_: v_.tensor_tensor(ap32(C_s2, 1024),
                                                             ap32(C_s3, 1024),
                                                             ap32(C_s1, 1024), ALU.add), v2s)
                lm[("eadj", i - 1)] = v

            op("scalar", lambda s: s.activation(ap32(C_mask + 32, 600), ap32(C_exps, 600),
                                                AF.Copy, accum_out=ap32(C_mask + 8, 1)))
            op("scalar", lambda s: s.activation(ap32(C_mask + 16, 1), ap32(C_mask + 8, 1),
                                                AF.Ln))
            op("scalar", lambda s: s.activation(ap32(C_mask + 24, 1), ap32(C_mask + 16, 1),
                                                AF.Exp, scale=-1.0))
            v = op("scalar", lambda s: s.activation(ap32(C_mask + 32, 600), ap32(C_exps, 600),
                                                    AF.Copy, scale=ap32(C_mask + 24, 1)),
                   s_dn)
            lm[("mask", i)] = v

            if i > 0:
                W("tensor", s_dn, lm[("maTh", i - 1)])
                W("tensor", v2t, lm[("maT", i - 1)])
            W("tensor", s_dn, lm[("mask", i)])
            for k in range(5):
                kc = ELC[k]
                v = op("tensor", lambda t, o=psB[0:kc, k * 128:k * 128 + 128],
                       s=ap32(C_mask + 32 + k * 128, kc): t.transpose(o, s, ident[:]),
                       t2v if k == 4 else None)
            lm[("tpM", i)] = v

            W("scalar", t2v, lm[("tpM", i)])
            v = op("scalar", lambda s: s.copy(ap16(B_maTh, 640), psB[:, 0:640]), s_dn)
            lm[("maTh", i)] = v
            if i > 0:
                W("scalar", v2s, lm[("eadj", i - 1)])
                v = op("scalar", lambda s: s.activation(ap32(C_s3, 1024), ap32(C_s2, 1024),
                                                        AF.Ln), s_dn)
                lm[("lc", i - 1)] = v

            W("vector", t2v, lm[("tpM", i)])
            W("vector", s_dn, lm[("maTh", i)])
            v = op("vector", lambda v_: v_.tensor_tensor(ap16(B_maTl, 640), psB[:, 0:640],
                                                         ap16(B_maTh, 640), ALU.subtract), v2t)
            lm[("maT", i)] = v
            lm[("maT1", i)] = v

            if i + 1 < NT:
                W("tensor", v2t, lm[("hn", i + 1)])
                W("tensor", s_dn, lm[("exp", i)])
                for c in range(8):
                    v = op("tensor", lambda t, o=psA[:, 1536 + c * 128:1664 + c * 128],
                           s=ap32(C_hn2 + ((i + 1) % 3) * 1024 + c * 128, 128):
                           t.transpose(o, s, ident[:]),
                           t2v if c in (3, 7) else None)
                    if c == 3:
                        lm[("tpB1", i + 1)] = v
                lm[("tpB", i + 1)] = v
                W("vector", t2v, lm[("tpB1", i + 1)])
                op("vector", lambda v_: v_.tensor_copy(ap16(B_hnTh, 512),
                                                       psA[:, 1536:2048]))
                v = op("vector", lambda v_: v_.tensor_tensor(ap16(B_hnTl, 512),
                                                             psA[:, 1536:2048],
                                                             ap16(B_hnTh, 512),
                                                             ALU.subtract), v2t)
                lm[("hnT1", i + 1)] = v
                W("vector", t2v, lm[("tpB", i + 1)])
                op("vector", lambda v_: v_.tensor_copy(ap16(B_hnTh + 512, 512),
                                                       psA[:, 2048:2560]))
                v = op("vector", lambda v_: v_.tensor_tensor(ap16(B_hnTl + 512, 512),
                                                             psA[:, 2048:2560],
                                                             ap16(B_hnTh + 512, 512),
                                                             ALU.subtract), v2t)
                lm[("hnT", i + 1)] = v

            if i > 0:
                W("vector", s_dn, lm[("lc", i - 1)])
                v = op("vector", lambda v_, j=i - 1: v_.tensor_tensor(ap32(C_s1, 1024),
                                                                      ap32(C_s3, 1024),
                                                                      hnslot(j), ALU.add), v2s)
                lm[("fr", i - 1)] = v
                op("vector", lambda v_: v_.reduce_sum(st[:, 8:9], ap32(C_s1, 1024), axis=AXX))

            if i > 0:
                W("scalar", v2s, lm[("fr", i - 1)])
                v = op("scalar", lambda s: s.activation(ap32(C_s2, 1024), ap32(C_s1, 1024),
                                                        AF.Square, accum_out=st[:, 9:10]), s_dn)
                lm[("sqf", i - 1)] = v

            if i > 0:
                W("vector", s_dn, lm[("sqf", i - 1)])
                lm[("var4", i - 1)] = ln_tiny(8)
                W("scalar", v2s, lm[("var4", i - 1)])
                lm[("istd4", i - 1)] = ln_istd(8)

            W("tensor", s_dn, lm[("maTh", i)])
            for p in range(3):
                if p == 2:
                    W("tensor", v2t, lm[("maT", i)])
                lo = (B_maTh, B_maTh, B_maTl)[p]
                ro = (B_vwh, B_vwl, B_vwh)[p]
                for k in range(5):
                    kc = ELC[k]
                    lhs = ap16(lo + k * 128, 128, 0, kc)
                    for s in range(2):
                        rhs = ap16(ro + k * 1024 + s * 512, 512, 0, kc)
                        op("tensor", lambda t, o=psA[:, s * 512:(s + 1) * 512], l=lhs, r=rhs:
                           t.matmul(o, l, r, start=False, stop=False))
            for s in range(2):
                v = op("tensor", lambda t, o=psA[:, s * 512:(s + 1) * 512],
                       r=a16[0:2, X_wdbh + s * 512: X_wdbh + (s + 1) * 512]:
                       t.matmul(o, ones2b[:], r, start=False, stop=True),
                       t2v if s == 1 else None)
            lm[("dl", i)] = v

            if i > 0:
                W("vector", s_dn, lm[("istd4", i - 1)])
                op("vector", lambda v_: v_.tensor_scalar(ap32(C_s2, 1024), ap32(C_s1, 1024),
                                                         st[:, 10:11], st[:, 15:16],
                                                         ALU.subtract, ALU.mult))
                v = op("vector", lambda v_, j=i - 1:
                       v_.tensor_tensor(hslot(j), hslot(j), ap32(C_s2, 1024), ALU.add), v2t)
                lm[("ot", i - 1)] = v
                W("gpsimd", v2t, lm[("ot", i - 1)])
                if i >= 3:
                    W("gpsimd", sosem(i - 3), lm[("sto", i - 3)])
                lm[("sto", i - 1)] = dma("gpsimd", out_d[(i - 1) * 128:i * 128, :],
                                         hslot(i - 1), sosem(i - 1))

        i = NT
        W("scalar", t2v, lm[("dl", i - 1)])
        if i > 1:
            W("scalar", v2t, lm[("ot", i - 2)])
        v = op("scalar", lambda s: s.activation(ap32(C_s2, 1024), psA[:, 0:1024],
                                                AF.Copy, scale=scv[:]), s_dn)
        lm[("s2", i - 1)] = v
        v = op("scalar", lambda s: s.activation(ap32(C_s3, 1024), ap32(C_s2, 1024), AF.Exp),
               s_dn)
        lm[("expd", i - 1)] = v
        W("vector", s_dn, lm[("expd", i - 1)])
        op("vector", lambda v_: v_.tensor_scalar(ap32(C_s1, 1024), ap32(C_s3, 1024),
                                                 0.0, FEPS, ALU.is_equal, ALU.mult))
        v = op("vector", lambda v_: v_.tensor_tensor(ap32(C_s2, 1024), ap32(C_s3, 1024),
                                                     ap32(C_s1, 1024), ALU.add), v2s)
        lm[("eadj", i - 1)] = v
        W("scalar", v2s, lm[("eadj", i - 1)])
        v = op("scalar", lambda s: s.activation(ap32(C_s3, 1024), ap32(C_s2, 1024), AF.Ln),
               s_dn)
        lm[("lc", i - 1)] = v
        W("vector", s_dn, lm[("lc", i - 1)])
        v = op("vector", lambda v_, j=i - 1: v_.tensor_tensor(ap32(C_s1, 1024),
                                                              ap32(C_s3, 1024), hnslot(j),
                                                              ALU.add), v2s)
        lm[("fr", i - 1)] = v
        op("vector", lambda v_: v_.reduce_sum(st[:, 8:9], ap32(C_s1, 1024), axis=AXX))
        W("scalar", v2s, lm[("fr", i - 1)])
        v = op("scalar", lambda s: s.activation(ap32(C_s2, 1024), ap32(C_s1, 1024),
                                                AF.Square, accum_out=st[:, 9:10]), s_dn)
        lm[("sqf", i - 1)] = v
        W("vector", s_dn, lm[("sqf", i - 1)])
        lm[("var4", i - 1)] = ln_tiny(8)
        W("scalar", v2s, lm[("var4", i - 1)])
        lm[("istd4", i - 1)] = ln_istd(8)
        W("vector", s_dn, lm[("istd4", i - 1)])
        vsp()
        op("vector", lambda v_: v_.tensor_scalar(ap32(C_s2, 1024), ap32(C_s1, 1024),
                                                 st[:, 10:11], st[:, 15:16],
                                                 ALU.subtract, ALU.mult))
        v = op("vector", lambda v_, j=i - 1:
               v_.tensor_tensor(hslot(j), hslot(j), ap32(C_s2, 1024), ALU.add), v2t)
        lm[("ot", i - 1)] = v
        W("gpsimd", v2t, lm[("ot", i - 1)])
        if i >= 3:
            W("gpsimd", sosem(i - 3), lm[("sto", i - 3)])
        lm[("sto", i - 1)] = dma("gpsimd", out_d[(i - 1) * 128:i * 128, :],
                                 hslot(i - 1), sosem(i - 1))


        W("gpsimd", st_o, P.val(st_o))
        W("gpsimd", st_o2, P.val(st_o2))

        @block.sync
        def _(e):
            for f in P.ops["sync"]:
                f(e)

        @block.vector
        def _(e):
            for f in P.ops["vector"]:
                f(e)

        @block.scalar
        def _(e):
            for f in P.ops["scalar"]:
                f(e)

        @block.tensor
        def _(e):
            for f in P.ops["tensor"]:
                f(e)

        @block.gpsimd
        def _(e):
            for f in P.ops["gpsimd"]:
                f(e)

    return nc


def _prep(inputs):
    f32 = np.float32
    Wq = np.asarray(inputs["L0_attn_W_qkv"], f32)
    Wg = np.asarray(inputs["L0_attn_W_gate"], f32)
    w1 = np.ascontiguousarray(np.concatenate([Wq, Wg], axis=0).T)
    b1 = np.ascontiguousarray(np.concatenate([np.asarray(inputs["L0_attn_b_qkv"], f32),
                                              np.asarray(inputs["L0_attn_b_gate"], f32)]))
    at = np.ascontiguousarray(np.transpose(np.asarray(inputs["L0_ffn_A"], f32), (0, 2, 1)))
    bm = np.ascontiguousarray(np.asarray(inputs["L0_ffn_B"], f32))
    wg = np.ascontiguousarray(np.asarray(inputs["L0_ffn_w_gate"], f32))
    wt = np.ascontiguousarray(np.asarray(inputs["L0_ffn_wt_w"], f32).T)
    wd = np.ascontiguousarray(np.asarray(inputs["L0_ffn_wd_w"], f32).T)
    return {
        "w1": w1, "b1": b1, "at": at, "bm": bm, "wg": wg, "wt": wt,
        "wtb": np.ascontiguousarray(np.asarray(inputs["L0_ffn_wt_b"], f32)),
        "wd": wd,
        "wdb": np.ascontiguousarray(np.asarray(inputs["L0_ffn_wd_b"], f32)),
        "sc": np.ascontiguousarray(np.asarray(inputs["L0_ffn_scale"], f32)),
        "ident": np.eye(128, dtype=f32),
        "uones": np.triu(np.ones((128, 128), dtype=f32)).astype(ml_dtypes.bfloat16),
        "uones32": np.triu(np.ones((128, 128), dtype=f32)),
        "iota6": np.tile(np.arange(6, dtype=f32), (128, 1)),
        "ones1f": np.ones((1, 128), dtype=f32),
        "onesb": np.ones((1, 128), dtype=ml_dtypes.bfloat16),
        "ones2b": np.ones((2, 128), dtype=ml_dtypes.bfloat16),
        "cvals": np.tile(np.array([1e-5, -0.5, -1.0, 1024 ** -0.5, -(1024 ** -0.5),
                                   1000.0, np.finfo(np.float32).eps, 0.0], dtype=f32),
                         (128, 1)),
    }


def kernel(**inputs):
    x = np.asarray(inputs["L0_attn_x"], np.float32)
    base = _prep(inputs)
    S = x.shape[1]
    nc = build(S)
    in_maps = []
    for i in range(x.shape[0]):
        m = dict(base)
        m["x"] = np.ascontiguousarray(x[i])
        in_maps.append(m)
    res = run_bass_kernel_spmd(nc, in_maps, core_ids=list(range(x.shape[0])))
    return np.stack([np.asarray(res.results[i]["out"]) for i in range(x.shape[0])], axis=0)


# version 36
# speedup vs baseline: 1.0780x; 1.0780x over previous
import numpy as np
import ml_dtypes
import concourse.bass as bass
import concourse.mybir as mybir
from concourse.bass_utils import run_bass_kernel_spmd

F32 = mybir.dt.float32
BF16 = mybir.dt.bfloat16
AF = mybir.ActivationFunctionType
ALU = mybir.AluOpType
AXX = mybir.AxisListType.X

D = 1024
W1N = 4096
EL = 600
ELC = [128, 128, 128, 128, 88]
NEXP = 6
LT = 100
LN_EPS = 1e-5
FEPS = float(np.finfo(np.float32).eps)
ISC = float(1024 ** -0.5)

ENGINES = ("sync", "vector", "scalar", "tensor", "gpsimd")


class Plan:
    def __init__(self):
        self.ops = {e: [] for e in ENGINES}
        self.cnt = {}

    def inc(self, sem, by=1):
        k = id(sem)
        self.cnt[k] = self.cnt.get(k, 0) + by
        return self.cnt[k]

    def val(self, sem):
        return self.cnt.get(id(sem), 0)


def build(S, dbg=False):
    NT = S // 128
    nc = bass.Bass("TRN2", target_bir_lowering=False)

    x_d = nc.dram_tensor("x", [S, D], F32, kind="ExternalInput").ap()
    w1_d = nc.dram_tensor("w1", [D, W1N], F32, kind="ExternalInput").ap()
    b1_d = nc.dram_tensor("b1", [W1N], F32, kind="ExternalInput").ap()
    at_d = nc.dram_tensor("at", [NEXP, 16, LT], F32, kind="ExternalInput").ap()
    bm_d = nc.dram_tensor("bm", [NEXP, 16, D], F32, kind="ExternalInput").ap()
    wg_d = nc.dram_tensor("wg", [D, NEXP], F32, kind="ExternalInput").ap()
    wt_d = nc.dram_tensor("wt", [D, D], F32, kind="ExternalInput").ap()
    wtb_d = nc.dram_tensor("wtb", [D], F32, kind="ExternalInput").ap()
    wd_d = nc.dram_tensor("wd", [D, D], F32, kind="ExternalInput").ap()
    wdb_d = nc.dram_tensor("wdb", [D], F32, kind="ExternalInput").ap()
    sc_d = nc.dram_tensor("sc", [1], F32, kind="ExternalInput").ap()
    id_d = nc.dram_tensor("ident", [128, 128], F32, kind="ExternalInput").ap()
    uo_d = nc.dram_tensor("uones", [128, 128], BF16, kind="ExternalInput").ap()
    uo32_d = nc.dram_tensor("uones32", [128, 128], F32, kind="ExternalInput").ap()
    io_d = nc.dram_tensor("iota6", [128, NEXP], F32, kind="ExternalInput").ap()
    o1_d = nc.dram_tensor("ones1f", [1, 128], F32, kind="ExternalInput").ap()
    ob_d = nc.dram_tensor("onesb", [1, 128], BF16, kind="ExternalInput").ap()
    o2_d = nc.dram_tensor("ones2b", [2, 128], BF16, kind="ExternalInput").ap()
    cv_d = nc.dram_tensor("cvals", [128, 8], F32, kind="ExternalInput").ap()
    out_d = nc.dram_tensor("out", [S, D], F32, kind="ExternalOutput").ap()
    h_d = nc.dram_tensor("hscr", [S, D], F32,
                         kind="ExternalOutput" if dbg else "Internal").ap()
    spwd_d = nc.dram_tensor("spwd", [128, 16384], BF16, kind="Internal").ap()
    sptk_d = nc.dram_tensor("sptk", [128, 9600], BF16, kind="Internal").ap()
    spvw_d = nc.dram_tensor("spvw", [128, 10240], BF16, kind="Internal").ap()
    spwg_d = nc.dram_tensor("spwg", [128, 96], BF16, kind="Internal").ap()

    from contextlib import ExitStack
    with ExitStack() as _es:
        def _sb(*a):
            return _es.enter_context(nc.sbuf_tensor(*a))

        def _ps(*a):
            return _es.enter_context(nc.psum_tensor(*a))

        def _sem(n):
            return _es.enter_context(nc.semaphore(n))

        a16 = _sb("a16", [128, 79872], BF16)
        a32 = _sb("a32", [128, 11264], F32)
        st = _sb("stats", [128, 96], F32)
        wtbs = _sb("wtbs", [128, 8], F32)
        iota = _sb("iota_sb", [128, NEXP], F32)
        scin = _sb("scin", [1, 1], F32)
        scv = _sb("scv", [128, 1], F32)
        ident = _sb("ident_sb", [128, 128], F32)
        uones = _sb("uones_sb", [128, 128], BF16)
        uones32 = _sb("uones32_sb", [128, 128], F32)
        ones1f = _sb("ones1f_sb", [1, 128], F32)
        onesb = _sb("onesb_sb", [1, 128], BF16)
        ones2b = _sb("ones2b_sb", [2, 128], BF16)
        lnst = _sb("lnstats", [128, 96], F32)
        carry = _sb("carry", [1, D], F32)
        cst = _sb("cvals_sb", [128, 8], F32)
        for _k, _v in enumerate([LN_EPS, -0.5, -1.0, ISC, -ISC, 1000.0, FEPS, 0.0]):
            nc.const_aps.register(F32, _v, cst[:, _k:_k + 1])
        psA = _ps("psA", [128, 3072], F32)
        psB = _ps("psB", [128, 1024], F32)
        ld_c = _sem("ld_c")
        t2v = _sem("t2v")
        vch = _sem("vch")
        sch = _sem("sch")
        v2t = vch
        v2s = vch
        s_dn = sch
        ld_x = _sem("ld_x")
        c_dma = _sem("c_dma")
        st_h = _sem("st_h")
        ld_h = _sem("ld_h")
        st_o = _sem("st_o")
        ld_x2 = _sem("ld_x2")
        ld_h2 = _sem("ld_h2")
        st_h2 = _sem("st_h2")
        st_o2 = _sem("st_o2")
        spl = _sem("spl")
        spl2 = _sem("spl2")
        spe = _sem("spe")
        spf = _sem("spf")
        spf2 = _sem("spf2")
        block = _es.enter_context(nc.Block())
        P = Plan()

        A_wth, A_wtl = 0, 8192
        A_wdh0 = 16384
        A_tkh0 = 32768
        A_tkl0 = 32768 + 4800
        A_vth, A_vtl = 42368, 47168
        A_vwh0 = 51968
        A_vwl0 = 51968 + 5120
        A_wgh0 = 62208
        A_wgl0 = 62208 + 48
        A_W1h, A_W1l = 0, 32768
        A_xnTh, A_xnTl = 65536, 66560
        A_uh, A_ul = 67584, 68608
        B_wdh, B_wdl = 0, 8192
        B_tkh, B_tkl = 16384, 21184
        B_vwh, B_vwl = 25984, 31104
        B_wgh, B_wgl = 36224, 36272
        B_hnTh, B_hnTl = 36352, 37376
        B_maTh, B_maTl = 38400, 39040
        X_b1h, X_b1l = 69632, 73728
        X_wdbh, X_wdbl = 77824, 78848

        C_xb, C_xn, C_ku, C_q, C_g, C_g2, C_kv, C_hb = \
            0, 2048, 3072, 4096, 5120, 6144, 7168, 8192
        C_sq = 10240
        C_hb2, C_hn2, C_exps, C_mask, C_s1, C_s2 = \
            0, 3072, 6144, 6784, 7424, 8448
        C_s3 = 9472
        C_stage = 0
        C_vts = 8192
        C_wgs = 8792

        def ap16(off, n, p0=0, p1=128):
            return a16[p0:p1, off:off + n]

        def ap32(off, n, p0=0, p1=128):
            return a32[p0:p1, off:off + n]

        def W(eng, sem, val):
            if val > 0:
                P.ops[eng].append(lambda e, s=sem, v=val: e.wait_ge(s, v))

        def dma(eng, out, in_, sem, slow=False):
            v = P.inc(sem, 16)
            P.ops[eng].append(lambda e, o=out, i=in_, s=sem, sl=slow:
                              e.dma_start(out=o, in_=i,
                                          allow_slow_non_contiguous=sl).then_inc(s, 16))
            return v

        def op(eng, fn, sem=None):
            if eng in ("vector", "scalar"):
                ch = vch if eng == "vector" else sch
                prev = P.val(ch)
                if prev > 0:
                    P.ops[eng].append(lambda e, s=ch, v=prev: e.wait_ge(s, v))
                v = P.inc(ch, 1)
                P.ops[eng].append(lambda e, f=fn, s=ch: f(e).then_inc(s, 1))
                return v
            if sem is None:
                P.ops[eng].append(fn)
                return None
            v = P.inc(sem, 1)
            P.ops[eng].append(lambda e, f=fn, s=sem: f(e).then_inc(s, 1))
            return v

        def vsp(sem=None):
            return P.val(vch)

        def rsp():
            return op("vector", lambda v_: v_.tensor_copy(st[:, 94:95], st[:, 88:89]))

        def xsem(i):
            return ld_x if i % 2 == 0 else ld_x2

        def hsem(i):
            return ld_h if i % 2 == 0 else ld_h2

        def shsem(i):
            return st_h if i % 2 == 0 else st_h2

        def sosem(i):
            return st_o if i % 2 == 0 else st_o2

        def nop(eng, sem):
            v = P.inc(sem, 1)
            P.ops[eng].append(lambda e, s=sem: e.nop().then_inc(s, 1))
            return v

        def ln_tiny(o):
            vsp()
            vsp()
            op("vector", lambda v_: v_.tensor_scalar(st[:, o + 2:o + 3], st[:, o:o + 1],
                                                     1.0 / D, None, ALU.mult))
            op("vector", lambda v_: v_.tensor_scalar(st[:, o + 4:o + 5], st[:, o + 1:o + 2],
                                                     1.0 / D, None, ALU.mult))
            vsp()
            op("vector", lambda v_: v_.tensor_tensor(st[:, o + 3:o + 4], st[:, o + 2:o + 3],
                                                     st[:, o + 2:o + 3], ALU.mult))
            vsp()
            vsp()
            op("vector", lambda v_: v_.tensor_tensor(st[:, o + 5:o + 6], st[:, o + 4:o + 5],
                                                     st[:, o + 3:o + 4], ALU.subtract))
            vsp()
            return vsp(v2s)

        def ln_istd(o):
            op("scalar", lambda s, o=o: s.activation(st[:, o + 6:o + 7], st[:, o + 5:o + 6],
                                                     AF.Ln, bias=LN_EPS))
            return op("scalar", lambda s, o=o: s.activation(st[:, o + 7:o + 8],
                                                            st[:, o + 6:o + 7],
                                                            AF.Exp, scale=-0.5), s_dn)

        dma("sync", ident[:], id_d, ld_c)
        dma("sync", uones[:], uo_d, ld_c)
        dma("sync", uones32[:], uo32_d, ld_c)
        dma("sync", iota[:], io_d, ld_c)
        dma("sync", ones1f[:], o1_d, ld_c)
        dma("sync", onesb[:], ob_d, ld_c)
        dma("sync", ones2b[:], o2_d, ld_c)
        dma("sync", cst[:], cv_d, ld_c)
        dma("sync", scin[:], sc_d.rearrange("(a n) -> a n", a=1), ld_c)
        dma("sync", ap32(C_wgs, 48).rearrange("p (c e) -> p c e", e=6),
            wg_d.rearrange("(c p) e -> p c e", p=128), ld_c)
        dma("sync", a32[0:1, 0:4096], b1_d.rearrange("(a n) -> a n", a=1), ld_c)
        dma("sync", wtbs[:], wtb_d.rearrange("(c p) -> p c", p=128), ld_c, slow=True)
        v_bias = dma("sync", a32[0:1, 4096:5120], wdb_d.rearrange("(a n) -> a n", a=1), ld_c)

        op("vector", lambda v_: v_.memset(st[:, 84:96], 0.0))
        W("vector", ld_c, v_bias)
        op("vector", lambda v_: v_.tensor_copy(ap16(X_b1h, 4096, 0, 1), a32[0:1, 0:4096]))
        op("vector", lambda v_: v_.tensor_tensor(ap16(X_b1l, 4096, 0, 1), a32[0:1, 0:4096],
                                                 ap16(X_b1h, 4096, 0, 1), ALU.subtract))
        op("vector", lambda v_: v_.tensor_copy(ap16(X_wdbh, 1024, 0, 1), a32[0:1, 4096:5120]))
        vb1 = op("vector", lambda v_: v_.tensor_tensor(ap16(X_wdbl, 1024, 0, 1),
                                                       a32[0:1, 4096:5120],
                                                       ap16(X_wdbh, 1024, 0, 1),
                                                       ALU.subtract), v2t)

        W("sync", v2t, vb1)
        dma("sync", a16[1:2, X_b1h:X_b1h + 4096], ap16(X_b1l, 4096, 0, 1), ld_c)
        v_brow = dma("sync", a16[1:2, X_wdbh:X_wdbh + 1024], ap16(X_wdbl, 1024, 0, 1), ld_c)
        dma("sync", a32[0:16, 6144:6744].rearrange("r (e l) -> r e l", l=LT),
            at_d.rearrange("e r l -> r e l"), ld_c)
        v_ld = dma("sync", a32[0:16, 0:6144].rearrange("r (e c) -> r e c", c=D),
                   bm_d.rearrange("e r c -> r e c"), ld_c)

        _wt_slot_last = [0, 0]

        def wt_chunk(c):
            slot = 8840 + (c % 2) * 1024
            if _wt_slot_last[c % 2] > 0:
                W("sync", vch, _wt_slot_last[c % 2])
            sm = spl if c % 2 == 0 else spl2
            vd = dma("sync", ap32(slot, 1024), wt_d[c * 128:(c + 1) * 128, :], sm)
            W("vector", sm, vd)
            op("vector", lambda v_, c=c, sl=slot:
               v_.tensor_copy(ap16(A_wth + c * 1024, 1024), ap32(sl, 1024)))
            vv = op("vector", lambda v_, c=c, sl=slot:
                    v_.tensor_tensor(ap16(A_wtl + c * 1024, 1024), ap32(sl, 1024),
                                     ap16(A_wth + c * 1024, 1024), ALU.subtract), v2t)
            _wt_slot_last[c % 2] = vv
            return vv

        W("tensor", ld_c, v_ld)
        tok_mm_done = None
        _tk_evac = [0, 0]
        for c in range(8):
            pso = psB if c % 2 == 0 else psA
            if _tk_evac[c % 2] > 0:
                W("tensor", v2t, _tk_evac[c % 2])
            for e in range(NEXP):
                lhs = a32[0:16, e * 1024 + c * 128: e * 1024 + (c + 1) * 128]
                rhs = a32[0:16, 6144 + e * LT: 6144 + (e + 1) * LT]
                o = pso[:, e * 128: e * 128 + LT]
                last = (e == NEXP - 1)
                v = op("tensor", lambda t, o=o, l=lhs, r=rhs: t.matmul(o, l, r, start=True,
                                                                      stop=True),
                       t2v if last else None)
            tok_mm_done = v
            W("vector", t2v, v)
            src = pso[:, 0:768].rearrange("p (e x) -> p e x", x=128)[:, :, 0:LT]
            dsth = ap16(A_tkh0 + c * EL, EL).rearrange("p (e l) -> p e l", l=LT)
            dstl = ap16(A_tkl0 + c * EL, EL).rearrange("p (e l) -> p e l", l=LT)
            op("vector", lambda v_, d=dsth, s=src: v_.tensor_copy(d, s))
            v = op("vector", lambda v_, d=dstl, s=src, hh=dsth:
                   v_.tensor_tensor(d, s, hh, ALU.subtract), v2t)
            _tk_evac[c % 2] = v
            wt_chunk(c)
        tok_evac_done = P.val(v2t)
        wt_ready = tok_evac_done

        W("tensor", v2t, tok_evac_done)
        W("tensor", ld_c, v_ld)
        v = op("tensor", lambda t: t.matmul(psB[:, 0:1], ones1f[:], scin[:], start=True,
                                            stop=True), t2v)
        W("vector", t2v, v)
        op("vector", lambda v_: v_.tensor_copy(scv[:], psB[:, 0:1]))
        op("vector", lambda v_: v_.tensor_copy(ap16(A_wgh0, 48), ap32(C_wgs, 48)))
        op("vector", lambda v_: v_.tensor_tensor(ap16(A_wgl0, 48), ap32(C_wgs, 48),
                                                 ap16(A_wgh0, 48), ALU.subtract), v2t)
        spill_ready = P.val(v2t)

        _slot_last = [0, 0]

        def split_weight(dram_ap, ncols, offh, offl, pre_wait=0):
            splits = []
            for c in range(8):
                slot = C_stage + (c % 2) * 4096
                if _slot_last[c % 2] > 0:
                    W("sync", vch, _slot_last[c % 2])
                sm = spl if c % 2 == 0 else spl2
                vd = dma("sync", ap32(slot, ncols), dram_ap[c * 128:(c + 1) * 128, :], sm)
                W("vector", sm, vd)
                if c == 0 and pre_wait:
                    W("vector", ld_c, pre_wait)
                op("vector", lambda v_, c=c, s=slot, n=ncols, oh=offh:
                   v_.tensor_copy(ap16(oh + c * n, n), ap32(s, n)))
                vv = op("vector", lambda v_, c=c, s=slot, n=ncols, oh=offh, ol=offl:
                        v_.tensor_tensor(ap16(ol + c * n, n), ap32(s, n), ap16(oh + c * n, n),
                                         ALU.subtract), v2t)
                splits.append(vv)
                _slot_last[c % 2] = vv
            return P.val(v2t)

        W("sync", t2v, tok_mm_done)
        W("sync", v2t, vb1)
        wd_splits = []

        def wd_chunk_stage(c):
            slot = C_stage + (c % 2) * 4096
            if _slot_last[c % 2] > 0:
                W("sync", vch, _slot_last[c % 2])
            else:
                W("sync", vch, _wt_slot_last[c % 2])
            sm = spl if c % 2 == 0 else spl2
            vd = dma("sync", ap32(slot, 1024), wd_d[c * 128:(c + 1) * 128, :], sm)
            W("vector", sm, vd)
            op("vector", lambda v_, c=c, sl=slot:
               v_.tensor_copy(ap16(A_wdh0 + c * 1024, 1024), ap32(sl, 1024)))
            vv = op("vector", lambda v_, c=c, sl=slot:
                    v_.tensor_tensor(ap16(A_wdh0 + 8192 + c * 1024, 1024), ap32(sl, 1024),
                                     ap16(A_wdh0 + c * 1024, 1024), ALU.subtract), v2t)
            _slot_last[c % 2] = vv
            return vv

        W("tensor", v2t, wt_ready)
        _vt_evac = [0, 0]
        for d in range(8):
            base = (d % 2) * 1024
            if _vt_evac[d % 2] > 0:
                W("tensor", v2t, _vt_evac[d % 2])
            for p in range(3):
                lo = (A_wth, A_wth, A_wtl)[p]
                ro = (A_tkh0, A_tkl0, A_tkh0)[p]
                for c in range(8):
                    for s0, s1 in ((0, 512), (512, EL)):
                        lhs = ap16(lo + c * 1024 + d * 128, 128)
                        rhs = ap16(ro + c * EL + s0, s1 - s0)
                        o = psA[:, base + s0:base + s1]
                        stt = (p == 0 and c == 0)
                        stp = (p == 2 and c == 7)
                        last = stp and s1 == EL
                        v = op("tensor", lambda t, o=o, l=lhs, r=rhs, a=stt, b=stp:
                               t.matmul(o, l, r, start=a, stop=b), t2v if last else None)
            W("vector", t2v, v)
            op("vector", lambda v_, d=d, b=base: v_.tensor_scalar(ap32(C_vts, EL),
                                                                  psA[:, b:b + EL],
                                                                  wtbs[:, d:d + 1], None,
                                                                  ALU.add))
            op("vector", lambda v_, d=d: v_.tensor_copy(ap16(A_vth + d * EL, EL),
                                                        ap32(C_vts, EL)))
            vv = op("vector", lambda v_, d=d:
                    v_.tensor_tensor(ap16(A_vtl + d * EL, EL), ap32(C_vts, EL),
                                     ap16(A_vth + d * EL, EL), ALU.subtract), v2t)
            _vt_evac[d % 2] = vv
            wd_splits.append(wd_chunk_stage(d))
        vt_mm_done = P.val(t2v)
        wd_ready = P.val(vch)

        W("gpsimd", v2t, spill_ready)
        dma("gpsimd", sptk_d, ap16(A_tkh0, 9600), spe)
        dma("gpsimd", spwg_d, ap16(A_wgh0, 96), spe)

        op("vector", lambda v_: v_.memset(
            ap16(A_vth, 4800).rearrange("p (d e l) -> p d e l", e=6, l=LT)[:, :, :, 0:1], 0.0))
        op("vector", lambda v_: v_.memset(
            ap16(A_vtl, 4800).rearrange("p (d e l) -> p d e l", e=6, l=LT)[:, :, :, 0:1], 0.0),
           v2t)
        l0z = P.val(vch)


        sp_done = [0]

        def w1_chunk(c, q="sync", sms=None, hi_on_scalar=False):
            slot = C_stage + (c % 2) * 4096
            if _slot_last[c % 2] > 0:
                W(q, vch, _slot_last[c % 2])
            sm = (sms or (spl, spl2))[c % 2]
            vd = dma(q, ap32(slot, 4096), w1_d[c * 128:(c + 1) * 128, :], sm)
            if hi_on_scalar:
                W("scalar", sm, vd)
                if c == 4:
                    W("scalar", ld_c, sp_done[0])
                    W("scalar", spe, 32)
                hv = op("scalar", lambda sc_, c=c, s=slot:
                        sc_.copy(ap16(A_W1h + c * 4096, 4096), ap32(s, 4096)), s_dn)
                W("vector", s_dn, hv)
            else:
                W("vector", sm, vd)
                op("vector", lambda v_, c=c, s=slot:
                   v_.tensor_copy(ap16(A_W1h + c * 4096, 4096), ap32(s, 4096)))
            vv = op("vector", lambda v_, c=c, s=slot:
                    v_.tensor_tensor(ap16(A_W1l + c * 4096, 4096), ap32(s, 4096),
                                     ap16(A_W1h + c * 4096, 4096), ALU.subtract), v2t)
            _slot_last[c % 2] = vv

        W("vector", t2v, vt_mm_done)
        W("vector", spe, 32)
        w1_chunk(0)
        w1_chunk(1)

        op("vector", lambda v_: v_.memset(ap16(A_vwh0 + 4 * 1024, 1024, 64, 128), 0.0))
        op("vector", lambda v_: v_.memset(ap16(A_vwl0 + 4 * 1024, 1024, 64, 128), 0.0))

        W("tensor", v2t, wd_ready)
        W("tensor", v2t, l0z)
        _vw_evac = [0, 0]
        for k in range(5):
            kc = ELC[k]
            base = (k % 2) * 1024
            if _vw_evac[k % 2] > 0:
                W("tensor", v2t, _vw_evac[k % 2])
            for p in range(3):
                lo = (A_vth, A_vth, A_vtl)[p]
                ro = (A_wdh0, A_wdh0 + 8192, A_wdh0)[p]
                for dd in range(8):
                    for s in range(2):
                        lhs = ap16(lo + dd * EL + k * 128, kc)
                        rhs = ap16(ro + dd * 1024 + s * 512, 512)
                        o = psA[0:kc, base + s * 512:base + (s + 1) * 512]
                        stt = (p == 0 and dd == 0)
                        stp = (p == 2 and dd == 7)
                        last = stp and s == 1
                        v = op("tensor", lambda t, o=o, l=lhs, r=rhs, a=stt, b=stp:
                               t.matmul(o, l, r, start=a, stop=b), t2v if last else None)
            W("vector", t2v, v)
            op("vector", lambda v_, k=k, kc=kc, b=base:
               v_.tensor_copy(ap16(A_vwh0 + k * 1024, 1024, 0, kc), psA[0:kc, b:b + 1024]))
            vv = op("vector", lambda v_, k=k, kc=kc, b=base:
                    v_.tensor_tensor(ap16(A_vwl0 + k * 1024, 1024, 0, kc),
                                     psA[0:kc, b:b + 1024],
                                     ap16(A_vwh0 + k * 1024, 1024, 0, kc), ALU.subtract), v2t)
            _vw_evac[k % 2] = vv
        vw_evac_done = P.val(vch)
        vw_mm_done = P.val(t2v)

        W("vector", t2v, vw_mm_done)
        w1_chunk(2)
        w1_chunk(3)

        W("sync", t2v, vw_mm_done)
        W("sync", vch, vw_evac_done)
        dma("sync", spwd_d, ap16(A_wdh0, 16384), ld_c)
        sp_done[0] = dma("sync", spvw_d, ap16(A_vwh0, 10240), ld_c)

        W("vector", ld_c, sp_done[0])
        W("vector", spe, 32)
        for c in range(4, 8):
            w1_chunk(c, q="scalar", sms=(spf, spf2), hi_on_scalar=True)
        op("vector", lambda v_: v_.memset(carry[:], 0.0), v2t)

        bA_v, bA_s, bA_t = P.val(vch), P.val(sch), P.val(t2v)

        for _e in ("sync", "gpsimd", "scalar", "tensor"):
            W(_e, vch, bA_v)
            W(_e, sch, bA_s)
            W(_e, t2v, bA_t)
        W("scalar", ld_c, P.val(ld_c))
        W("tensor", ld_c, sp_done[0])

        def xslot(i):
            return ap32(C_xb + (i % 2) * 1024, 1024)

        lm = {}
        lm[("ldx", 0)] = dma("sync", xslot(0), x_d[0:128, :], xsem(0))
        if NT > 1:
            lm[("ldx", 1)] = dma("sync", xslot(1), x_d[128:256, :], xsem(1))

        W("scalar", xsem(0), lm[("ldx", 0)])
        v = op("scalar", lambda s: s.activation(ap32(C_sq, 1024), xslot(0), AF.Square,
                                                accum_out=st[:, 1:2]), s_dn)
        lm[("sqx", 0)] = v
        W("vector", xsem(0), lm[("ldx", 0)])
        op("vector", lambda v_: v_.reduce_sum(st[:, 0:1], xslot(0), axis=AXX))
        W("vector", s_dn, lm[("sqx", 0)])
        v = ln_tiny(0)
        W("scalar", v2s, v)
        lm[("istd1", 0)] = ln_istd(0)
        W("vector", s_dn, lm[("istd1", 0)])
        vsp()
        v = op("vector", lambda v_: v_.tensor_scalar(ap32(C_xn, 1024), xslot(0), st[:, 2:3],
                                                     st[:, 7:8], ALU.subtract, ALU.mult), v2t)
        lm[("xn", 0)] = v
        W("tensor", v2t, lm[("xn", 0)])
        for c in range(8):
            v = op("tensor", lambda t, o=psA[:, c * 128:(c + 1) * 128],
                   s=ap32(C_xn + c * 128, 128): t.transpose(o, s, ident[:]),
                   t2v if c == 7 else None)
        lm[("tp", 0)] = v
        W("vector", t2v, lm[("tp", 0)])
        op("vector", lambda v_: v_.tensor_copy(ap16(A_xnTh, 512), psA[:, 0:512]))
        v = op("vector", lambda v_: v_.tensor_tensor(ap16(A_xnTl, 512), psA[:, 0:512],
                                                     ap16(A_xnTh, 512), ALU.subtract), v2t)
        lm[("xnT1", 0)] = v
        op("vector", lambda v_: v_.tensor_copy(ap16(A_xnTh + 512, 512), psA[:, 512:1024]))
        v = op("vector", lambda v_: v_.tensor_tensor(ap16(A_xnTl + 512, 512),
                                                     psA[:, 512:1024],
                                                     ap16(A_xnTh + 512, 512),
                                                     ALU.subtract), v2t)
        lm[("xnT", 0)] = v

        for i in range(NT):
            xb = xslot(i)
            hb = ap32(C_hb + (i % 2) * 1024, 1024)

            if 0 < i and i + 1 < NT:
                W("sync", v2t, lm[("h", i - 1)])
                lm[("ldx", i + 1)] = dma("sync", xslot(i + 1),
                                         x_d[(i + 1) * 128:(i + 2) * 128, :], xsem(i + 1))

            if i + 1 < NT:
                W("scalar", xsem(i + 1), lm[("ldx", i + 1)])
                if i > 0:
                    W("scalar", v2t, lm[("h", i - 1)])
                v = op("scalar", lambda s, j=i + 1: s.activation(ap32(C_sq, 1024), xslot(j),
                                                                 AF.Square,
                                                                 accum_out=st[:, 1:2]), s_dn)
                lm[("sqx", i + 1)] = v

            if i + 1 < NT:
                W("vector", xsem(i + 1), lm[("ldx", i + 1)])
                op("vector", lambda v_, j=i + 1: v_.reduce_sum(st[:, 0:1], xslot(j), axis=AXX))
                W("vector", s_dn, lm[("sqx", i + 1)])
                lm[("var1", i + 1)] = ln_tiny(0)
                W("scalar", v2s, lm[("var1", i + 1)])
                lm[("istd1", i + 1)] = ln_istd(0)
                W("vector", s_dn, lm[("istd1", i + 1)])
                v = op("vector", lambda v_, j=i + 1: v_.tensor_scalar(ap32(C_xn, 1024),
                                                                      xslot(j), st[:, 2:3],
                                                                      st[:, 7:8], ALU.subtract,
                                                                      ALU.mult), v2t)
                lm[("xn", i + 1)] = v

            if i > 0:
                W("tensor", s_dn, lm[("kvf", i - 1)])
            for s in range(2, 6):
                op("tensor", lambda t, o=psA[:, s * 512:(s + 1) * 512],
                   r=a16[0:2, X_b1h + s * 512: X_b1h + (s + 1) * 512]:
                   t.matmul(o, ones2b[:], r, start=True, stop=False))
            W("tensor", v2t, lm[("xnT1", i)])
            for c in range(4):
                for p in range(3):
                    lo = (A_xnTh, A_xnTh, A_xnTl)[p]
                    ro = (A_W1h, A_W1l, A_W1h)[p]
                    lhs = ap16(lo + c * 128, 128)
                    for s in range(2, 6):
                        rhs = ap16(ro + c * 4096 + s * 512, 512)
                        op("tensor", lambda t, o=psA[:, s * 512:(s + 1) * 512], l=lhs, r=rhs:
                           t.matmul(o, l, r, start=False, stop=False))
            W("tensor", v2t, lm[("xnT", i)])
            for sb_ in range(6, 8):
                op("tensor", lambda t, o=psB[:, (sb_ - 6) * 512:(sb_ - 5) * 512],
                   r=a16[0:2, X_b1h + sb_ * 512: X_b1h + (sb_ + 1) * 512]:
                   t.matmul(o, ones2b[:], r, start=True, stop=False))
            for c in range(4):
                for p in range(3):
                    lo = (A_xnTh, A_xnTh, A_xnTl)[p]
                    lhs = ap16(lo + c * 128, 128)
                    for s in range(6, 8):
                        rhs = ap16((A_W1h, A_W1l, A_W1h)[p] + c * 4096 + s * 512, 512)
                        op("tensor", lambda t, o=psB[:, (s - 6) * 512:(s - 5) * 512], l=lhs,
                           r=rhs: t.matmul(o, l, r, start=False, stop=False))
            for c in range(4, 8):
                for p in range(3):
                    lo = (A_xnTh, A_xnTh, A_xnTl)[p]
                    ro = (A_W1h, A_W1l, A_W1h)[p]
                    lhs = ap16(lo + c * 128, 128)
                    for s in range(2, 8):
                        rhs = ap16(ro + c * 4096 + s * 512, 512)
                        o = (psA[:, s * 512:(s + 1) * 512] if s < 6
                             else psB[:, (s - 6) * 512:(s - 5) * 512])
                        stp = (c == 7 and p == 2)
                        last = stp and s == 7
                        v = op("tensor", lambda t, o=o, l=lhs, r=rhs, b=stp:
                               t.matmul(o, l, r, start=False, stop=b), t2v if last else None)
            lm[("qkv1", i)] = v
            if i > 0:
                W("tensor", s_dn, lm[("q", i - 1)])
            for s in range(2):
                op("tensor", lambda t, o=psA[:, s * 512:(s + 1) * 512],
                   r=a16[0:2, X_b1h + s * 512: X_b1h + (s + 1) * 512]:
                   t.matmul(o, ones2b[:], r, start=True, stop=False))
            for c in range(8):
                for p in range(3):
                    lo = (A_xnTh, A_xnTh, A_xnTl)[p]
                    ro = (A_W1h, A_W1l, A_W1h)[p]
                    lhs = ap16(lo + c * 128, 128)
                    for s in range(2):
                        rhs = ap16(ro + c * 4096 + s * 512, 512)
                        stp = (c == 7 and p == 2)
                        last = stp and s == 1
                        v = op("tensor", lambda t, o=psA[:, s * 512:(s + 1) * 512], l=lhs,
                               r=rhs, b=stp: t.matmul(o, l, r, start=False, stop=b),
                               t2v if last else None)
            lm[("qkv", i)] = v

            W("scalar", t2v, lm[("qkv1", i)])
            v = op("scalar", lambda s: s.copy(ap32(C_ku, 1024), psA[:, 1024:2048]), s_dn)
            lm[("k", i)] = v
            v = op("scalar", lambda s: s.activation(ap32(C_g2, 1024), psB[:, 0:1024], AF.Exp,
                                                    scale=-1.0), s_dn)
            lm[("eg", i)] = v
            W("scalar", t2v, lm[("qkv", i)])
            v = op("scalar", lambda s: s.copy(ap32(C_q, 1024), psA[:, 0:1024]), s_dn)
            lm[("q", i)] = v

            W("vector", t2v, lm[("qkv1", i)])
            W("vector", s_dn, lm[("k", i)])
            if i > 0:
                W("vector", c_dma, 16 * i)
            v = op("vector", lambda v_: v_.tensor_tensor(ap32(C_kv, 1024), ap32(C_ku, 1024),
                                                         psA[:, 2048:3072], ALU.mult), v2t)
            lm[("u", i)] = v

            W("tensor", v2t, lm[("u", i)])
            W("tensor", s_dn, lm[("eg", i)])
            if i > 0:
                W("tensor", c_dma, 16 * i)
            for s in range(2):
                oo = psB[:, s * 512:(s + 1) * 512]
                op("tensor", lambda t, o=oo, s=s: t.matmul(o, uones32[:],
                                                           ap32(C_kv + s * 512, 512),
                                                           start=True, stop=False))
                v = op("tensor", lambda t, o=oo, s=s: t.matmul(o, ones1f[:],
                                                               carry[0:1, s * 512:(s + 1) * 512],
                                                               start=False, stop=True),
                       t2v if s == 1 else None)
            lm[("cs", i)] = v

            W("scalar", t2v, lm[("cs", i)])
            v = op("scalar", lambda s: s.copy(ap32(C_kv, 1024), psB[:, 0:1024]), s_dn)
            lm[("kvf", i)] = v

            W("gpsimd", s_dn, lm[("kvf", i)])
            dma("gpsimd", carry[:], ap32(C_kv, 1024, 127, 128), c_dma)

            if i + 1 < NT:
                W("tensor", v2t, lm[("xn", i + 1)])
                W("tensor", s_dn, lm[("q", i)])
                for c in range(8):
                    v = op("tensor", lambda t, o=psA[:, c * 128:(c + 1) * 128],
                           s=ap32(C_xn + c * 128, 128): t.transpose(o, s, ident[:]),
                           t2v if c == 7 else None)
                lm[("tp", i + 1)] = v
                W("vector", t2v, lm[("tp", i + 1)])
                op("vector", lambda v_: v_.tensor_copy(ap16(A_xnTh, 512), psA[:, 0:512]))
                v = op("vector", lambda v_: v_.tensor_tensor(ap16(A_xnTl, 512), psA[:, 0:512],
                                                             ap16(A_xnTh, 512),
                                                             ALU.subtract), v2t)
                lm[("xnT1", i + 1)] = v
                op("vector", lambda v_: v_.tensor_copy(ap16(A_xnTh + 512, 512),
                                                       psA[:, 512:1024]))
                v = op("vector", lambda v_: v_.tensor_tensor(ap16(A_xnTl + 512, 512),
                                                             psA[:, 512:1024],
                                                             ap16(A_xnTh + 512, 512),
                                                             ALU.subtract), v2t)
                lm[("xnT", i + 1)] = v

            W("vector", s_dn, lm[("q", i)])
            op("vector", lambda v_: v_.tensor_scalar(ap32(C_g, 1024), ap32(C_g2, 1024), 1.0,
                                                     None, ALU.add))
            op("vector", lambda v_: v_.reciprocal(ap32(C_g2, 1024), ap32(C_g, 1024)))
            rsp()
            rsp()
            if i + 1 < NT:
                W("vector", s_dn, lm[("sqx", i + 1)])
            W("vector", s_dn, lm[("kvf", i)])
            op("vector", lambda v_: v_.tensor_tensor(ap32(C_sq, 1024), ap32(C_q, 1024),
                                                     ap32(C_kv, 1024), ALU.mult))
            v = op("vector", lambda v_: v_.tensor_tensor(ap32(C_g, 1024), ap32(C_sq, 1024),
                                                         ap32(C_g2, 1024), ALU.mult), v2s)
            lm[("a", i)] = v
            W("scalar", v2s, lm[("a", i)])
            v = op("scalar", lambda s: s.activation(ap32(C_sq, 1024), ap32(C_g, 1024),
                                                    AF.Square, accum_out=st[:, 9:10]), s_dn)
            lm[("sqa", i)] = v
            op("vector", lambda v_: v_.reduce_sum(st[:, 8:9], ap32(C_g, 1024), axis=AXX))
            W("vector", s_dn, lm[("sqa", i)])
            lm[("var2", i)] = ln_tiny(8)
            W("scalar", v2s, lm[("var2", i)])
            lm[("istd2", i)] = ln_istd(8)
            W("vector", s_dn, lm[("istd2", i)])
            vsp()
            if i >= 2:
                W("vector", shsem(i - 2), lm[("sth", i - 2)])
            op("vector", lambda v_: v_.tensor_scalar(ap32(C_sq, 1024), ap32(C_g, 1024),
                                                     st[:, 10:11], st[:, 15:16],
                                                     ALU.subtract, ALU.mult))
            v = op("vector", lambda v_, x=xb, h=hb: v_.tensor_tensor(h, x, ap32(C_sq, 1024),
                                                                     ALU.add), v2t)
            lm[("h", i)] = v

            W("gpsimd", v2t, lm[("h", i)])
            lm[("sth", i)] = dma("gpsimd", h_d[i * 128:(i + 1) * 128, :], hb, shsem(i))

            op("vector", lambda v_, h=hb, j=i: v_.reduce_sum(lnst[:, 2 * j:2 * j + 1], h,
                                                             axis=AXX))
            W("scalar", v2t, lm[("h", i)])
            v = op("scalar", lambda s, h=hb, j=i: s.activation(ap32(C_sq, 1024), h, AF.Square,
                                                               accum_out=lnst[:, 2 * j + 1:
                                                                              2 * j + 2]),
                   s_dn)
            lm[("sqh", i)] = v

        bB_v, bB_s, bB_t = P.val(vch), P.val(sch), P.val(t2v)

        def hslot(i):
            return ap32(C_hb2 + (i % 3) * 1024, 1024)

        def hnslot(i):
            return ap32(C_hn2 + (i % 3) * 1024, 1024)

        W("sync", vch, lm[("h", NT - 2)])
        W("sync", shsem(0), lm[("sth", 0)])
        lm[("ldh", 0)] = dma("sync", hslot(0), h_d[0:128, :], hsem(0))
        W("sync", t2v, lm[("qkv", NT - 1)])
        W("sync", spe, 32)
        dma("sync", ap16(B_tkh, 9600), sptk_d, spl)
        rl_a = dma("sync", ap16(B_wgh, 96), spwg_d, spl)
        dma("sync", ap16(B_wdh, 16384), spwd_d, spl2)
        rl_b = dma("sync", ap16(B_vwh, 10240), spvw_d, spl2)

        for _e in ("sync", "gpsimd", "scalar", "tensor"):
            W(_e, vch, bB_v)
            W(_e, sch, bB_s)
            W(_e, t2v, bB_t)
        for _e in ("vector", "scalar", "sync", "gpsimd"):
            W(_e, c_dma, P.val(c_dma))
            W(_e, ld_x, P.val(ld_x))
            W(_e, ld_x2, P.val(ld_x2))
            W(_e, st_h, P.val(st_h))
            W(_e, st_h2, P.val(st_h2))

        if NT > 1:
            W("sync", shsem(1), lm[("sth", 1)])
            lm[("ldh", 1)] = dma("sync", hslot(1), h_d[128:256, :], hsem(1))
        op("vector", lambda v_: v_.tensor_copy(st[:, 0:2], lnst[:, 0:2]))
        v = ln_tiny(0)
        W("scalar", v2s, v)
        lm[("istd3", 0)] = ln_istd(0)
        W("vector", s_dn, lm[("istd3", 0)])
        vsp()
        W("vector", hsem(0), lm[("ldh", 0)])
        v = op("vector", lambda v_: v_.tensor_scalar(hnslot(0), hslot(0), st[:, 2:3],
                                                     st[:, 7:8], ALU.subtract, ALU.mult), v2t)
        lm[("hn", 0)] = v
        W("tensor", v2t, lm[("hn", 0)])
        for c in range(8):
            v = op("tensor", lambda t, o=psA[:, 1536 + c * 128:1664 + c * 128],
                   s=ap32(C_hn2 + c * 128, 128): t.transpose(o, s, ident[:]),
                   t2v if c == 7 else None)
        lm[("tpB", 0)] = v
        W("vector", t2v, lm[("tpB", 0)])
        op("vector", lambda v_: v_.tensor_copy(ap16(B_hnTh, 512), psA[:, 1536:2048]))
        v = op("vector", lambda v_: v_.tensor_tensor(ap16(B_hnTl, 512), psA[:, 1536:2048],
                                                     ap16(B_hnTh, 512), ALU.subtract), v2t)
        lm[("hnT1", 0)] = v
        op("vector", lambda v_: v_.tensor_copy(ap16(B_hnTh + 512, 512), psA[:, 2048:2560]))
        v = op("vector", lambda v_: v_.tensor_tensor(ap16(B_hnTl + 512, 512),
                                                     psA[:, 2048:2560],
                                                     ap16(B_hnTh + 512, 512),
                                                     ALU.subtract), v2t)
        lm[("hnT", 0)] = v

        for i in range(NT):
            hb = hslot(i)

            if i >= 1 and i + 1 < NT:
                if i >= 2:
                    W("sync", sosem(i - 2), lm[("sto", i - 2)])
                W("sync", vch, lm[("hn", i - 1)])
                W("sync", shsem(i + 1), lm[("sth", i + 1)])
                lm[("ldh", i + 1)] = dma("sync", hslot(i + 1),
                                         h_d[(i + 1) * 128:(i + 2) * 128, :], hsem(i + 1))

            if i == 0:
                W("tensor", spl, rl_a)
            W("tensor", v2t, lm[("hnT", i)])
            for c in range(8):
                for p in range(3):
                    lo = (B_hnTh, B_hnTh, B_hnTl)[p]
                    lhs = ap16(lo + c * 128, 128)
                    stt = (c == 0 and p == 0)
                    stp = (c == 7 and p == 2)
                    rhs = ap16((B_wgh, B_wgl, B_wgh)[p] + c * 6, 6)
                    op("tensor", lambda t, o=psA[:, 1024:1030], l=lhs, r=rhs, a=stt, b=stp:
                       t.matmul(o, l, r, start=a, stop=b))
                    for s0, s1 in ((0, 512), (512, EL)):
                        rhs = ap16((B_tkh, B_tkl, B_tkh)[p] + c * EL + s0, s1 - s0)
                        last = stp and s1 == EL
                        v = op("tensor", lambda t, o=psA[:, 1536 + s0:1536 + s1], l=lhs, r=rhs,
                               a=stt, b=stp: t.matmul(o, l, r, start=a, stop=b),
                               t2v if last else None)
            lm[("sc", i)] = v

            if i > 0:
                W("scalar", t2v, lm[("dl", i - 1)])
                if i > 1:
                    W("scalar", v2t, lm[("ot", i - 2)])
                v = op("scalar", lambda s: s.activation(ap32(C_s2, 1024), psA[:, 0:1024],
                                                        AF.Copy, scale=scv[:]), s_dn)
                lm[("s2", i - 1)] = v

            if i + 1 < NT:
                op("vector", lambda v_, j=i + 1: v_.tensor_copy(st[:, 0:2],
                                                                lnst[:, 2 * j:2 * j + 2]))
                lm[("var3", i + 1)] = ln_tiny(0)
                W("scalar", v2s, lm[("var3", i + 1)])
                lm[("istd3", i + 1)] = ln_istd(0)

            W("vector", t2v, lm[("sc", i)])
            op("vector", lambda v_: v_.tensor_copy(st[:, 32:38], psA[:, 1024:1030]))
            for e in range(NEXP):
                op("vector", lambda v_, e=e: v_.reduce_max(st[:, 54 + e:55 + e],
                                                           psA[:, 1536 + e * LT:
                                                               1536 + (e + 1) * LT],
                                                           axis=AXX))
            op("vector", lambda v_: v_.reduce_max(st[:, 38:39], st[:, 32:38], axis=AXX))
            vsp()
            op("vector", lambda v_: v_.tensor_scalar(st[:, 60:66], st[:, 54:60], -ISC, 1000.0,
                                                     ALU.mult, ALU.subtract))
            vsp()
            op("vector", lambda v_: v_.tensor_scalar(st[:, 40:46], st[:, 32:38], st[:, 38:39],
                                                     1000.0, ALU.is_equal, ALU.mult))
            vsp()
            vsp()
            op("vector", lambda v_: v_.tensor_tensor(st[:, 48:54], st[:, 40:46], iota[:],
                                                     ALU.subtract))
            vsp()
            vsp()
            op("vector", lambda v_: v_.reduce_max(st[:, 46:47], st[:, 48:54], axis=AXX))
            vsp()
            vsp()
            op("vector", lambda v_: v_.tensor_scalar(st[:, 84:90], st[:, 48:54], st[:, 46:47],
                                                     1000.0, ALU.is_equal, ALU.mult))
            rsp()
            rsp()
            op("vector", lambda v_: v_.tensor_tensor(st[:, 54:60], st[:, 60:66], st[:, 84:90],
                                                     ALU.add))
            rsp()
            rsp()
            lm[("am", i)] = vsp(v2s)

            if i + 1 < NT:
                W("vector", s_dn, lm[("istd3", i + 1)])
                W("vector", hsem(i + 1), lm[("ldh", i + 1)])
                v = op("vector", lambda v_, j=i + 1: v_.tensor_scalar(
                    hnslot(j), hslot(j), st[:, 2:3], st[:, 7:8], ALU.subtract, ALU.mult), v2t)
                lm[("hn", i + 1)] = v

            if i == 0:
                W("tensor", spl2, rl_b)
            if i > 0:
                W("tensor", s_dn, lm[("s2", i - 1)])
            for c in range(8):
                for p in range(3):
                    lo = (B_hnTh, B_hnTh, B_hnTl)[p]
                    lhs = ap16(lo + c * 128, 128)
                    stt = (c == 0 and p == 0)
                    stp = (c == 7 and p == 2)
                    for s in range(2):
                        rhs = ap16((B_wdh, B_wdl, B_wdh)[p] + c * 1024 + s * 512, 512)
                        last = stp and s == 1
                        v = op("tensor", lambda t, o=psA[:, s * 512:(s + 1) * 512], l=lhs,
                               r=rhs, a=stt: t.matmul(o, l, r, start=a, stop=False),
                               t2v if last else None)
            lm[("xw", i)] = v

            if i > 0:
                v = op("scalar", lambda s: s.activation(ap32(C_s3, 1024), ap32(C_s2, 1024),
                                                        AF.Exp), s_dn)
                lm[("expd", i - 1)] = v
            W("scalar", v2s, lm[("am", i)])
            for e in range(NEXP):
                v = op("scalar", lambda s, e=e: s.activation(ap32(C_exps + e * LT, LT),
                                                             psA[:, 1536 + e * LT:
                                                                 1536 + (e + 1) * LT],
                                                             AF.Exp,
                                                             bias=st[:, 54 + e:55 + e],
                                                             scale=ISC),
                       s_dn if e == NEXP - 1 else None)
            lm[("exp", i)] = v

            if i > 0:
                W("vector", s_dn, lm[("expd", i - 1)])
                op("vector", lambda v_: v_.tensor_scalar(ap32(C_s1, 1024), ap32(C_s3, 1024),
                                                         0.0, FEPS, ALU.is_equal, ALU.mult))
                v = op("vector", lambda v_: v_.tensor_tensor(ap32(C_s2, 1024),
                                                             ap32(C_s3, 1024),
                                                             ap32(C_s1, 1024), ALU.add), v2s)
                lm[("eadj", i - 1)] = v

            op("scalar", lambda s: s.activation(ap32(C_mask + 32, 600), ap32(C_exps, 600),
                                                AF.Copy, accum_out=ap32(C_mask + 8, 1)))
            op("scalar", lambda s: s.activation(ap32(C_mask + 16, 1), ap32(C_mask + 8, 1),
                                                AF.Ln))
            op("scalar", lambda s: s.activation(ap32(C_mask + 24, 1), ap32(C_mask + 16, 1),
                                                AF.Exp, scale=-1.0))
            v = op("scalar", lambda s: s.activation(ap32(C_mask + 32, 600), ap32(C_exps, 600),
                                                    AF.Copy, scale=ap32(C_mask + 24, 1)),
                   s_dn)
            lm[("mask", i)] = v

            if i > 0:
                W("tensor", s_dn, lm[("maTh", i - 1)])
                W("tensor", v2t, lm[("maT", i - 1)])
            W("tensor", s_dn, lm[("mask", i)])
            for k in range(5):
                kc = ELC[k]
                v = op("tensor", lambda t, o=psB[0:kc, k * 128:k * 128 + 128],
                       s=ap32(C_mask + 32 + k * 128, kc): t.transpose(o, s, ident[:]),
                       t2v if k == 4 else None)
            lm[("tpM", i)] = v

            W("scalar", t2v, lm[("tpM", i)])
            v = op("scalar", lambda s: s.copy(ap16(B_maTh, 640), psB[:, 0:640]), s_dn)
            lm[("maTh", i)] = v
            if i > 0:
                W("scalar", v2s, lm[("eadj", i - 1)])
                v = op("scalar", lambda s: s.activation(ap32(C_s3, 1024), ap32(C_s2, 1024),
                                                        AF.Ln), s_dn)
                lm[("lc", i - 1)] = v

            W("vector", t2v, lm[("tpM", i)])
            W("vector", s_dn, lm[("maTh", i)])
            v = op("vector", lambda v_: v_.tensor_tensor(ap16(B_maTl, 640), psB[:, 0:640],
                                                         ap16(B_maTh, 640), ALU.subtract), v2t)
            lm[("maT", i)] = v
            lm[("maT1", i)] = v

            if i + 1 < NT:
                W("tensor", v2t, lm[("hn", i + 1)])
                W("tensor", s_dn, lm[("exp", i)])
                for c in range(8):
                    v = op("tensor", lambda t, o=psA[:, 1536 + c * 128:1664 + c * 128],
                           s=ap32(C_hn2 + ((i + 1) % 3) * 1024 + c * 128, 128):
                           t.transpose(o, s, ident[:]),
                           t2v if c in (3, 7) else None)
                    if c == 3:
                        lm[("tpB1", i + 1)] = v
                lm[("tpB", i + 1)] = v
                W("vector", t2v, lm[("tpB1", i + 1)])
                op("vector", lambda v_: v_.tensor_copy(ap16(B_hnTh, 512),
                                                       psA[:, 1536:2048]))
                v = op("vector", lambda v_: v_.tensor_tensor(ap16(B_hnTl, 512),
                                                             psA[:, 1536:2048],
                                                             ap16(B_hnTh, 512),
                                                             ALU.subtract), v2t)
                lm[("hnT1", i + 1)] = v
                W("vector", t2v, lm[("tpB", i + 1)])
                op("vector", lambda v_: v_.tensor_copy(ap16(B_hnTh + 512, 512),
                                                       psA[:, 2048:2560]))
                v = op("vector", lambda v_: v_.tensor_tensor(ap16(B_hnTl + 512, 512),
                                                             psA[:, 2048:2560],
                                                             ap16(B_hnTh + 512, 512),
                                                             ALU.subtract), v2t)
                lm[("hnT", i + 1)] = v

            if i > 0:
                W("vector", s_dn, lm[("lc", i - 1)])
                v = op("vector", lambda v_, j=i - 1: v_.tensor_tensor(ap32(C_s1, 1024),
                                                                      ap32(C_s3, 1024),
                                                                      hnslot(j), ALU.add), v2s)
                lm[("fr", i - 1)] = v
                op("vector", lambda v_: v_.reduce_sum(st[:, 8:9], ap32(C_s1, 1024), axis=AXX))

            if i > 0:
                W("scalar", v2s, lm[("fr", i - 1)])
                v = op("scalar", lambda s: s.activation(ap32(C_s2, 1024), ap32(C_s1, 1024),
                                                        AF.Square, accum_out=st[:, 9:10]), s_dn)
                lm[("sqf", i - 1)] = v

            if i > 0:
                W("vector", s_dn, lm[("sqf", i - 1)])
                lm[("var4", i - 1)] = ln_tiny(8)
                W("scalar", v2s, lm[("var4", i - 1)])
                lm[("istd4", i - 1)] = ln_istd(8)

            W("tensor", s_dn, lm[("maTh", i)])
            for p in range(3):
                if p == 2:
                    W("tensor", v2t, lm[("maT", i)])
                lo = (B_maTh, B_maTh, B_maTl)[p]
                ro = (B_vwh, B_vwl, B_vwh)[p]
                for k in range(5):
                    kc = ELC[k]
                    lhs = ap16(lo + k * 128, 128, 0, kc)
                    for s in range(2):
                        rhs = ap16(ro + k * 1024 + s * 512, 512, 0, kc)
                        op("tensor", lambda t, o=psA[:, s * 512:(s + 1) * 512], l=lhs, r=rhs:
                           t.matmul(o, l, r, start=False, stop=False))
            for s in range(2):
                v = op("tensor", lambda t, o=psA[:, s * 512:(s + 1) * 512],
                       r=a16[0:2, X_wdbh + s * 512: X_wdbh + (s + 1) * 512]:
                       t.matmul(o, ones2b[:], r, start=False, stop=True),
                       t2v if s == 1 else None)
            lm[("dl", i)] = v

            if i > 0:
                W("vector", s_dn, lm[("istd4", i - 1)])
                op("vector", lambda v_: v_.tensor_scalar(ap32(C_s2, 1024), ap32(C_s1, 1024),
                                                         st[:, 10:11], st[:, 15:16],
                                                         ALU.subtract, ALU.mult))
                v = op("vector", lambda v_, j=i - 1:
                       v_.tensor_tensor(hslot(j), hslot(j), ap32(C_s2, 1024), ALU.add), v2t)
                lm[("ot", i - 1)] = v
                W("gpsimd", v2t, lm[("ot", i - 1)])
                if i >= 3:
                    W("gpsimd", sosem(i - 3), lm[("sto", i - 3)])
                lm[("sto", i - 1)] = dma("gpsimd", out_d[(i - 1) * 128:i * 128, :],
                                         hslot(i - 1), sosem(i - 1))

        i = NT
        W("scalar", t2v, lm[("dl", i - 1)])
        if i > 1:
            W("scalar", v2t, lm[("ot", i - 2)])
        v = op("scalar", lambda s: s.activation(ap32(C_s2, 1024), psA[:, 0:1024],
                                                AF.Copy, scale=scv[:]), s_dn)
        lm[("s2", i - 1)] = v
        v = op("scalar", lambda s: s.activation(ap32(C_s3, 1024), ap32(C_s2, 1024), AF.Exp),
               s_dn)
        lm[("expd", i - 1)] = v
        W("vector", s_dn, lm[("expd", i - 1)])
        op("vector", lambda v_: v_.tensor_scalar(ap32(C_s1, 1024), ap32(C_s3, 1024),
                                                 0.0, FEPS, ALU.is_equal, ALU.mult))
        v = op("vector", lambda v_: v_.tensor_tensor(ap32(C_s2, 1024), ap32(C_s3, 1024),
                                                     ap32(C_s1, 1024), ALU.add), v2s)
        lm[("eadj", i - 1)] = v
        W("scalar", v2s, lm[("eadj", i - 1)])
        v = op("scalar", lambda s: s.activation(ap32(C_s3, 1024), ap32(C_s2, 1024), AF.Ln),
               s_dn)
        lm[("lc", i - 1)] = v
        W("vector", s_dn, lm[("lc", i - 1)])
        v = op("vector", lambda v_, j=i - 1: v_.tensor_tensor(ap32(C_s1, 1024),
                                                              ap32(C_s3, 1024), hnslot(j),
                                                              ALU.add), v2s)
        lm[("fr", i - 1)] = v
        op("vector", lambda v_: v_.reduce_sum(st[:, 8:9], ap32(C_s1, 1024), axis=AXX))
        W("scalar", v2s, lm[("fr", i - 1)])
        v = op("scalar", lambda s: s.activation(ap32(C_s2, 1024), ap32(C_s1, 1024),
                                                AF.Square, accum_out=st[:, 9:10]), s_dn)
        lm[("sqf", i - 1)] = v
        W("vector", s_dn, lm[("sqf", i - 1)])
        lm[("var4", i - 1)] = ln_tiny(8)
        W("scalar", v2s, lm[("var4", i - 1)])
        lm[("istd4", i - 1)] = ln_istd(8)
        W("vector", s_dn, lm[("istd4", i - 1)])
        vsp()
        op("vector", lambda v_: v_.tensor_scalar(ap32(C_s2, 1024), ap32(C_s1, 1024),
                                                 st[:, 10:11], st[:, 15:16],
                                                 ALU.subtract, ALU.mult))
        v = op("vector", lambda v_, j=i - 1:
               v_.tensor_tensor(hslot(j), hslot(j), ap32(C_s2, 1024), ALU.add), v2t)
        lm[("ot", i - 1)] = v
        W("gpsimd", v2t, lm[("ot", i - 1)])
        if i >= 3:
            W("gpsimd", sosem(i - 3), lm[("sto", i - 3)])
        lm[("sto", i - 1)] = dma("gpsimd", out_d[(i - 1) * 128:i * 128, :],
                                 hslot(i - 1), sosem(i - 1))


        W("gpsimd", st_o, P.val(st_o))
        W("gpsimd", st_o2, P.val(st_o2))

        @block.sync
        def _(e):
            for f in P.ops["sync"]:
                f(e)

        @block.vector
        def _(e):
            for f in P.ops["vector"]:
                f(e)

        @block.scalar
        def _(e):
            for f in P.ops["scalar"]:
                f(e)

        @block.tensor
        def _(e):
            for f in P.ops["tensor"]:
                f(e)

        @block.gpsimd
        def _(e):
            for f in P.ops["gpsimd"]:
                f(e)

    return nc


def _prep(inputs):
    f32 = np.float32
    Wq = np.asarray(inputs["L0_attn_W_qkv"], f32)
    Wg = np.asarray(inputs["L0_attn_W_gate"], f32)
    w1 = np.ascontiguousarray(np.concatenate([Wq, Wg], axis=0).T)
    b1 = np.ascontiguousarray(np.concatenate([np.asarray(inputs["L0_attn_b_qkv"], f32),
                                              np.asarray(inputs["L0_attn_b_gate"], f32)]))
    at = np.ascontiguousarray(np.transpose(np.asarray(inputs["L0_ffn_A"], f32), (0, 2, 1)))
    bm = np.ascontiguousarray(np.asarray(inputs["L0_ffn_B"], f32))
    wg = np.ascontiguousarray(np.asarray(inputs["L0_ffn_w_gate"], f32))
    wt = np.ascontiguousarray(np.asarray(inputs["L0_ffn_wt_w"], f32).T)
    wd = np.ascontiguousarray(np.asarray(inputs["L0_ffn_wd_w"], f32).T)
    return {
        "w1": w1, "b1": b1, "at": at, "bm": bm, "wg": wg, "wt": wt,
        "wtb": np.ascontiguousarray(np.asarray(inputs["L0_ffn_wt_b"], f32)),
        "wd": wd,
        "wdb": np.ascontiguousarray(np.asarray(inputs["L0_ffn_wd_b"], f32)),
        "sc": np.ascontiguousarray(np.asarray(inputs["L0_ffn_scale"], f32)),
        "ident": np.eye(128, dtype=f32),
        "uones": np.triu(np.ones((128, 128), dtype=f32)).astype(ml_dtypes.bfloat16),
        "uones32": np.triu(np.ones((128, 128), dtype=f32)),
        "iota6": np.tile(np.arange(6, dtype=f32), (128, 1)),
        "ones1f": np.ones((1, 128), dtype=f32),
        "onesb": np.ones((1, 128), dtype=ml_dtypes.bfloat16),
        "ones2b": np.ones((2, 128), dtype=ml_dtypes.bfloat16),
        "cvals": np.tile(np.array([1e-5, -0.5, -1.0, 1024 ** -0.5, -(1024 ** -0.5),
                                   1000.0, np.finfo(np.float32).eps, 0.0], dtype=f32),
                         (128, 1)),
    }


def kernel(**inputs):
    x = np.asarray(inputs["L0_attn_x"], np.float32)
    base = _prep(inputs)
    S = x.shape[1]
    nc = build(S)
    in_maps = []
    for i in range(x.shape[0]):
        m = dict(base)
        m["x"] = np.ascontiguousarray(x[i])
        in_maps.append(m)
    res = run_bass_kernel_spmd(nc, in_maps, core_ids=list(range(x.shape[0])))
    return np.stack([np.asarray(res.results[i]["out"]) for i in range(x.shape[0])], axis=0)


# version 37
# speedup vs baseline: 1.0812x; 1.0030x over previous
import numpy as np
import ml_dtypes
import concourse.bass as bass
import concourse.mybir as mybir
from concourse.bass_utils import run_bass_kernel_spmd

F32 = mybir.dt.float32
BF16 = mybir.dt.bfloat16
AF = mybir.ActivationFunctionType
ALU = mybir.AluOpType
AXX = mybir.AxisListType.X

D = 1024
W1N = 4096
EL = 600
ELC = [128, 128, 128, 128, 88]
NEXP = 6
LT = 100
LN_EPS = 1e-5
FEPS = float(np.finfo(np.float32).eps)
ISC = float(1024 ** -0.5)

ENGINES = ("sync", "vector", "scalar", "tensor", "gpsimd")


class Plan:
    def __init__(self):
        self.ops = {e: [] for e in ENGINES}
        self.cnt = {}

    def inc(self, sem, by=1):
        k = id(sem)
        self.cnt[k] = self.cnt.get(k, 0) + by
        return self.cnt[k]

    def val(self, sem):
        return self.cnt.get(id(sem), 0)


def build(S, dbg=False):
    NT = S // 128
    nc = bass.Bass("TRN2", target_bir_lowering=False)

    x_d = nc.dram_tensor("x", [S, D], F32, kind="ExternalInput").ap()
    w1_d = nc.dram_tensor("w1", [D, W1N], F32, kind="ExternalInput").ap()
    b1_d = nc.dram_tensor("b1", [W1N], F32, kind="ExternalInput").ap()
    at_d = nc.dram_tensor("at", [NEXP, 16, LT], F32, kind="ExternalInput").ap()
    bm_d = nc.dram_tensor("bm", [NEXP, 16, D], F32, kind="ExternalInput").ap()
    wg_d = nc.dram_tensor("wg", [D, NEXP], F32, kind="ExternalInput").ap()
    wt_d = nc.dram_tensor("wt", [D, D], F32, kind="ExternalInput").ap()
    wtb_d = nc.dram_tensor("wtb", [D], F32, kind="ExternalInput").ap()
    wd_d = nc.dram_tensor("wd", [D, D], F32, kind="ExternalInput").ap()
    wdb_d = nc.dram_tensor("wdb", [D], F32, kind="ExternalInput").ap()
    sc_d = nc.dram_tensor("sc", [1], F32, kind="ExternalInput").ap()
    id_d = nc.dram_tensor("ident", [128, 128], F32, kind="ExternalInput").ap()
    uo_d = nc.dram_tensor("uones", [128, 128], BF16, kind="ExternalInput").ap()
    uo32_d = nc.dram_tensor("uones32", [128, 128], F32, kind="ExternalInput").ap()
    io_d = nc.dram_tensor("iota6", [128, NEXP], F32, kind="ExternalInput").ap()
    o1_d = nc.dram_tensor("ones1f", [1, 128], F32, kind="ExternalInput").ap()
    ob_d = nc.dram_tensor("onesb", [1, 128], BF16, kind="ExternalInput").ap()
    o2_d = nc.dram_tensor("ones2b", [2, 128], BF16, kind="ExternalInput").ap()
    cv_d = nc.dram_tensor("cvals", [128, 8], F32, kind="ExternalInput").ap()
    out_d = nc.dram_tensor("out", [S, D], F32, kind="ExternalOutput").ap()
    h_d = nc.dram_tensor("hscr", [S, D], F32,
                         kind="ExternalOutput" if dbg else "Internal").ap()
    spwd_d = nc.dram_tensor("spwd", [128, 16384], BF16, kind="Internal").ap()
    sptk_d = nc.dram_tensor("sptk", [128, 9600], BF16, kind="Internal").ap()
    spvw_d = nc.dram_tensor("spvw", [128, 10240], BF16, kind="Internal").ap()
    spwg_d = nc.dram_tensor("spwg", [128, 96], BF16, kind="Internal").ap()

    from contextlib import ExitStack
    with ExitStack() as _es:
        def _sb(*a):
            return _es.enter_context(nc.sbuf_tensor(*a))

        def _ps(*a):
            return _es.enter_context(nc.psum_tensor(*a))

        def _sem(n):
            return _es.enter_context(nc.semaphore(n))

        a16 = _sb("a16", [128, 79872], BF16)
        a32 = _sb("a32", [128, 11264], F32)
        st = _sb("stats", [128, 96], F32)
        wtbs = _sb("wtbs", [128, 8], F32)
        iota = _sb("iota_sb", [128, NEXP], F32)
        scin = _sb("scin", [1, 1], F32)
        scv = _sb("scv", [128, 1], F32)
        ident = _sb("ident_sb", [128, 128], F32)
        uones = _sb("uones_sb", [128, 128], BF16)
        uones32 = _sb("uones32_sb", [128, 128], F32)
        ones1f = _sb("ones1f_sb", [1, 128], F32)
        onesb = _sb("onesb_sb", [1, 128], BF16)
        ones2b = _sb("ones2b_sb", [2, 128], BF16)
        lnst = _sb("lnstats", [128, 96], F32)
        carry = _sb("carry", [1, D], F32)
        cst = _sb("cvals_sb", [128, 8], F32)
        for _k, _v in enumerate([LN_EPS, -0.5, -1.0, ISC, -ISC, 1000.0, FEPS, 0.0]):
            nc.const_aps.register(F32, _v, cst[:, _k:_k + 1])
        psA = _ps("psA", [128, 3072], F32)
        psB = _ps("psB", [128, 1024], F32)
        ld_c = _sem("ld_c")
        t2v = _sem("t2v")
        vch = _sem("vch")
        sch = _sem("sch")
        v2t = vch
        v2s = vch
        s_dn = sch
        ld_x = _sem("ld_x")
        c_dma = _sem("c_dma")
        st_h = _sem("st_h")
        ld_h = _sem("ld_h")
        st_o = _sem("st_o")
        ld_x2 = _sem("ld_x2")
        ld_h2 = _sem("ld_h2")
        st_h2 = _sem("st_h2")
        st_o2 = _sem("st_o2")
        spl = _sem("spl")
        spl2 = _sem("spl2")
        spe = _sem("spe")
        spf = _sem("spf")
        spf2 = _sem("spf2")
        block = _es.enter_context(nc.Block())
        P = Plan()

        A_wth, A_wtl = 0, 8192
        A_wdh0 = 16384
        A_tkh0 = 32768
        A_tkl0 = 32768 + 4800
        A_vth, A_vtl = 42368, 47168
        A_vwh0 = 51968
        A_vwl0 = 51968 + 5120
        A_wgh0 = 62208
        A_wgl0 = 62208 + 48
        A_W1h, A_W1l = 0, 32768
        A_xnTh, A_xnTl = 65536, 66560
        A_uh, A_ul = 67584, 68608
        B_wdh, B_wdl = 0, 8192
        B_tkh, B_tkl = 16384, 21184
        B_vwh, B_vwl = 25984, 31104
        B_wgh, B_wgl = 36224, 36272
        B_hnTh, B_hnTl = 36352, 37376
        B_maTh, B_maTl = 38400, 39040
        X_b1h, X_b1l = 69632, 73728
        X_wdbh, X_wdbl = 77824, 78848

        C_xb, C_xn, C_ku, C_q, C_g, C_g2, C_kv, C_hb = \
            0, 2048, 3072, 4096, 5120, 6144, 7168, 8192
        C_sq = 10240
        C_hb2, C_hn2, C_exps, C_mask, C_s1, C_s2 = \
            0, 3072, 6144, 6784, 7424, 8448
        C_s3 = 9472
        C_stage = 0
        C_vts = 8192
        C_wgs = 8792

        def ap16(off, n, p0=0, p1=128):
            return a16[p0:p1, off:off + n]

        def ap32(off, n, p0=0, p1=128):
            return a32[p0:p1, off:off + n]

        def W(eng, sem, val):
            if val > 0:
                P.ops[eng].append(lambda e, s=sem, v=val: e.wait_ge(s, v))

        def dma(eng, out, in_, sem, slow=False):
            v = P.inc(sem, 16)
            P.ops[eng].append(lambda e, o=out, i=in_, s=sem, sl=slow:
                              e.dma_start(out=o, in_=i,
                                          allow_slow_non_contiguous=sl).then_inc(s, 16))
            return v

        def op(eng, fn, sem=None):
            if eng in ("vector", "scalar"):
                ch = vch if eng == "vector" else sch
                prev = P.val(ch)
                if prev > 0:
                    P.ops[eng].append(lambda e, s=ch, v=prev: e.wait_ge(s, v))
                v = P.inc(ch, 1)
                P.ops[eng].append(lambda e, f=fn, s=ch: f(e).then_inc(s, 1))
                return v
            if sem is None:
                P.ops[eng].append(fn)
                return None
            v = P.inc(sem, 1)
            P.ops[eng].append(lambda e, f=fn, s=sem: f(e).then_inc(s, 1))
            return v

        def vsp(sem=None):
            return P.val(vch)

        def rsp():
            return op("vector", lambda v_: v_.tensor_copy(st[:, 94:95], st[:, 88:89]))

        def xsem(i):
            return ld_x if i % 2 == 0 else ld_x2

        def hsem(i):
            return ld_h if i % 2 == 0 else ld_h2

        def shsem(i):
            return st_h if i % 2 == 0 else st_h2

        def sosem(i):
            return st_o if i % 2 == 0 else st_o2

        def nop(eng, sem):
            v = P.inc(sem, 1)
            P.ops[eng].append(lambda e, s=sem: e.nop().then_inc(s, 1))
            return v

        def ln_tiny(o):
            vsp()
            vsp()
            op("vector", lambda v_: v_.tensor_scalar(st[:, o + 2:o + 3], st[:, o:o + 1],
                                                     1.0 / D, None, ALU.mult))
            op("vector", lambda v_: v_.tensor_scalar(st[:, o + 4:o + 5], st[:, o + 1:o + 2],
                                                     1.0 / D, None, ALU.mult))
            vsp()
            op("vector", lambda v_: v_.tensor_tensor(st[:, o + 3:o + 4], st[:, o + 2:o + 3],
                                                     st[:, o + 2:o + 3], ALU.mult))
            vsp()
            vsp()
            op("vector", lambda v_: v_.tensor_tensor(st[:, o + 5:o + 6], st[:, o + 4:o + 5],
                                                     st[:, o + 3:o + 4], ALU.subtract))
            vsp()
            return vsp(v2s)

        def ln_istd(o):
            op("scalar", lambda s, o=o: s.activation(st[:, o + 6:o + 7], st[:, o + 5:o + 6],
                                                     AF.Ln, bias=LN_EPS))
            return op("scalar", lambda s, o=o: s.activation(st[:, o + 7:o + 8],
                                                            st[:, o + 6:o + 7],
                                                            AF.Exp, scale=-0.5), s_dn)

        dma("sync", ident[:], id_d, ld_c)
        dma("sync", uones[:], uo_d, ld_c)
        dma("sync", uones32[:], uo32_d, ld_c)
        dma("sync", iota[:], io_d, ld_c)
        dma("sync", ones1f[:], o1_d, ld_c)
        dma("sync", onesb[:], ob_d, ld_c)
        dma("sync", ones2b[:], o2_d, ld_c)
        dma("sync", cst[:], cv_d, ld_c)
        dma("sync", scin[:], sc_d.rearrange("(a n) -> a n", a=1), ld_c)
        dma("sync", ap32(C_wgs, 48).rearrange("p (c e) -> p c e", e=6),
            wg_d.rearrange("(c p) e -> p c e", p=128), ld_c)
        dma("sync", a32[0:1, 0:4096], b1_d.rearrange("(a n) -> a n", a=1), ld_c)
        dma("sync", wtbs[:], wtb_d.rearrange("(c p) -> p c", p=128), ld_c, slow=True)
        v_bias = dma("sync", a32[0:1, 4096:5120], wdb_d.rearrange("(a n) -> a n", a=1), ld_c)

        op("vector", lambda v_: v_.memset(st[:, 84:96], 0.0))
        W("vector", ld_c, v_bias)
        op("vector", lambda v_: v_.tensor_copy(ap16(X_b1h, 4096, 0, 1), a32[0:1, 0:4096]))
        op("vector", lambda v_: v_.tensor_tensor(ap16(X_b1l, 4096, 0, 1), a32[0:1, 0:4096],
                                                 ap16(X_b1h, 4096, 0, 1), ALU.subtract))
        op("vector", lambda v_: v_.tensor_copy(ap16(X_wdbh, 1024, 0, 1), a32[0:1, 4096:5120]))
        vb1 = op("vector", lambda v_: v_.tensor_tensor(ap16(X_wdbl, 1024, 0, 1),
                                                       a32[0:1, 4096:5120],
                                                       ap16(X_wdbh, 1024, 0, 1),
                                                       ALU.subtract), v2t)

        W("sync", v2t, vb1)
        dma("sync", a16[1:2, X_b1h:X_b1h + 4096], ap16(X_b1l, 4096, 0, 1), ld_c)
        v_brow = dma("sync", a16[1:2, X_wdbh:X_wdbh + 1024], ap16(X_wdbl, 1024, 0, 1), ld_c)
        dma("sync", a32[0:16, 6144:6744].rearrange("r (e l) -> r e l", l=LT),
            at_d.rearrange("e r l -> r e l"), ld_c)
        v_ld = dma("sync", a32[0:16, 0:6144].rearrange("r (e c) -> r e c", c=D),
                   bm_d.rearrange("e r c -> r e c"), ld_c)

        _wt_slot_last = [0, 0]

        def wt_chunk(c):
            slot = 8840 + (c % 2) * 1024
            if _wt_slot_last[c % 2] > 0:
                W("sync", vch, _wt_slot_last[c % 2])
            sm = spl if c % 2 == 0 else spl2
            vd = dma("sync", ap32(slot, 1024), wt_d[c * 128:(c + 1) * 128, :], sm)
            W("vector", sm, vd)
            op("vector", lambda v_, c=c, sl=slot:
               v_.tensor_copy(ap16(A_wth + c * 1024, 1024), ap32(sl, 1024)))
            vv = op("vector", lambda v_, c=c, sl=slot:
                    v_.tensor_tensor(ap16(A_wtl + c * 1024, 1024), ap32(sl, 1024),
                                     ap16(A_wth + c * 1024, 1024), ALU.subtract), v2t)
            _wt_slot_last[c % 2] = vv
            return vv

        W("tensor", ld_c, v_ld)
        tok_mm_done = None
        _tk_evac = [0, 0]
        for c in range(8):
            pso = psB if c % 2 == 0 else psA
            if _tk_evac[c % 2] > 0:
                W("tensor", v2t, _tk_evac[c % 2])
            for e in range(NEXP):
                lhs = a32[0:16, e * 1024 + c * 128: e * 1024 + (c + 1) * 128]
                rhs = a32[0:16, 6144 + e * LT: 6144 + (e + 1) * LT]
                o = pso[:, e * 128: e * 128 + LT]
                last = (e == NEXP - 1)
                v = op("tensor", lambda t, o=o, l=lhs, r=rhs: t.matmul(o, l, r, start=True,
                                                                      stop=True),
                       t2v if last else None)
            tok_mm_done = v
            W("vector", t2v, v)
            src = pso[:, 0:768].rearrange("p (e x) -> p e x", x=128)[:, :, 0:LT]
            dsth = ap16(A_tkh0 + c * EL, EL).rearrange("p (e l) -> p e l", l=LT)
            dstl = ap16(A_tkl0 + c * EL, EL).rearrange("p (e l) -> p e l", l=LT)
            op("vector", lambda v_, d=dsth, s=src: v_.tensor_copy(d, s))
            v = op("vector", lambda v_, d=dstl, s=src, hh=dsth:
                   v_.tensor_tensor(d, s, hh, ALU.subtract), v2t)
            _tk_evac[c % 2] = v
            wt_chunk(c)
        tok_evac_done = P.val(v2t)
        wt_ready = tok_evac_done

        W("tensor", v2t, tok_evac_done)
        W("tensor", ld_c, v_ld)
        v = op("tensor", lambda t: t.matmul(psB[:, 0:1], ones1f[:], scin[:], start=True,
                                            stop=True), t2v)
        W("vector", t2v, v)
        op("vector", lambda v_: v_.tensor_copy(scv[:], psB[:, 0:1]))
        op("vector", lambda v_: v_.tensor_copy(ap16(A_wgh0, 48), ap32(C_wgs, 48)))
        op("vector", lambda v_: v_.tensor_tensor(ap16(A_wgl0, 48), ap32(C_wgs, 48),
                                                 ap16(A_wgh0, 48), ALU.subtract), v2t)
        spill_ready = P.val(v2t)

        _slot_last = [0, 0]

        def split_weight(dram_ap, ncols, offh, offl, pre_wait=0):
            splits = []
            for c in range(8):
                slot = C_stage + (c % 2) * 4096
                if _slot_last[c % 2] > 0:
                    W("sync", vch, _slot_last[c % 2])
                sm = spl if c % 2 == 0 else spl2
                vd = dma("sync", ap32(slot, ncols), dram_ap[c * 128:(c + 1) * 128, :], sm)
                W("vector", sm, vd)
                if c == 0 and pre_wait:
                    W("vector", ld_c, pre_wait)
                op("vector", lambda v_, c=c, s=slot, n=ncols, oh=offh:
                   v_.tensor_copy(ap16(oh + c * n, n), ap32(s, n)))
                vv = op("vector", lambda v_, c=c, s=slot, n=ncols, oh=offh, ol=offl:
                        v_.tensor_tensor(ap16(ol + c * n, n), ap32(s, n), ap16(oh + c * n, n),
                                         ALU.subtract), v2t)
                splits.append(vv)
                _slot_last[c % 2] = vv
            return P.val(v2t)

        W("sync", t2v, tok_mm_done)
        W("sync", v2t, vb1)
        wd_splits = []

        def wd_chunk_stage(c):
            slot = C_stage + (c % 2) * 4096
            if _slot_last[c % 2] > 0:
                W("sync", vch, _slot_last[c % 2])
            else:
                W("sync", vch, _wt_slot_last[c % 2])
            sm = spl if c % 2 == 0 else spl2
            vd = dma("sync", ap32(slot, 1024), wd_d[c * 128:(c + 1) * 128, :], sm)
            W("vector", sm, vd)
            op("vector", lambda v_, c=c, sl=slot:
               v_.tensor_copy(ap16(A_wdh0 + c * 1024, 1024), ap32(sl, 1024)))
            vv = op("vector", lambda v_, c=c, sl=slot:
                    v_.tensor_tensor(ap16(A_wdh0 + 8192 + c * 1024, 1024), ap32(sl, 1024),
                                     ap16(A_wdh0 + c * 1024, 1024), ALU.subtract), v2t)
            _slot_last[c % 2] = vv
            return vv

        W("tensor", v2t, wt_ready)
        _vt_evac = [0, 0]
        for d in range(8):
            base = (d % 2) * 1024
            if _vt_evac[d % 2] > 0:
                W("tensor", v2t, _vt_evac[d % 2])
            for p in range(3):
                lo = (A_wth, A_wth, A_wtl)[p]
                ro = (A_tkh0, A_tkl0, A_tkh0)[p]
                for c in range(8):
                    for s0, s1 in ((0, 512), (512, EL)):
                        lhs = ap16(lo + c * 1024 + d * 128, 128)
                        rhs = ap16(ro + c * EL + s0, s1 - s0)
                        o = psA[:, base + s0:base + s1]
                        stt = (p == 0 and c == 0)
                        stp = (p == 2 and c == 7)
                        last = stp and s1 == EL
                        v = op("tensor", lambda t, o=o, l=lhs, r=rhs, a=stt, b=stp:
                               t.matmul(o, l, r, start=a, stop=b), t2v if last else None)
            W("vector", t2v, v)
            op("vector", lambda v_, d=d, b=base: v_.tensor_scalar(ap32(C_vts, EL),
                                                                  psA[:, b:b + EL],
                                                                  wtbs[:, d:d + 1], None,
                                                                  ALU.add))
            op("vector", lambda v_, d=d: v_.tensor_copy(ap16(A_vth + d * EL, EL),
                                                        ap32(C_vts, EL)))
            vv = op("vector", lambda v_, d=d:
                    v_.tensor_tensor(ap16(A_vtl + d * EL, EL), ap32(C_vts, EL),
                                     ap16(A_vth + d * EL, EL), ALU.subtract), v2t)
            _vt_evac[d % 2] = vv
            wd_splits.append(wd_chunk_stage(d))
        vt_mm_done = P.val(t2v)
        wd_ready = P.val(vch)

        W("gpsimd", v2t, spill_ready)
        dma("gpsimd", sptk_d, ap16(A_tkh0, 9600), spe)
        dma("gpsimd", spwg_d, ap16(A_wgh0, 96), spe)

        op("vector", lambda v_: v_.memset(
            ap16(A_vth, 4800).rearrange("p (d e l) -> p d e l", e=6, l=LT)[:, :, :, 0:1], 0.0))
        op("vector", lambda v_: v_.memset(
            ap16(A_vtl, 4800).rearrange("p (d e l) -> p d e l", e=6, l=LT)[:, :, :, 0:1], 0.0),
           v2t)
        l0z = P.val(vch)


        sp_done = [0]

        def w1_chunk(c, q="sync", sms=None, hi_on_scalar=False):
            slot = C_stage + (c % 2) * 4096
            if _slot_last[c % 2] > 0:
                W(q, vch, _slot_last[c % 2])
            sm = (sms or (spl, spl2))[c % 2]
            vd = dma(q, ap32(slot, 4096), w1_d[c * 128:(c + 1) * 128, :], sm)
            if hi_on_scalar:
                W("scalar", sm, vd)
                if c == 4:
                    W("scalar", ld_c, sp_done[0])
                    W("scalar", spe, 32)
                hv = op("scalar", lambda sc_, c=c, s=slot:
                        sc_.copy(ap16(A_W1h + c * 4096, 4096), ap32(s, 4096)), s_dn)
                W("vector", s_dn, hv)
            else:
                W("vector", sm, vd)
                op("vector", lambda v_, c=c, s=slot:
                   v_.tensor_copy(ap16(A_W1h + c * 4096, 4096), ap32(s, 4096)))
            vv = op("vector", lambda v_, c=c, s=slot:
                    v_.tensor_tensor(ap16(A_W1l + c * 4096, 4096), ap32(s, 4096),
                                     ap16(A_W1h + c * 4096, 4096), ALU.subtract), v2t)
            _slot_last[c % 2] = vv

        W("vector", t2v, vt_mm_done)
        W("vector", spe, 32)
        w1_chunk(0)
        w1_chunk(1)

        op("vector", lambda v_: v_.memset(ap16(A_vwh0 + 4 * 1024, 1024, 64, 128), 0.0))
        op("vector", lambda v_: v_.memset(ap16(A_vwl0 + 4 * 1024, 1024, 64, 128), 0.0))

        W("tensor", v2t, wd_ready)
        W("tensor", v2t, l0z)
        _vw_evac = [0, 0]
        for k in range(5):
            kc = ELC[k]
            base = (k % 2) * 1024
            if _vw_evac[k % 2] > 0:
                W("tensor", v2t, _vw_evac[k % 2])
            for p in range(3):
                lo = (A_vth, A_vth, A_vtl)[p]
                ro = (A_wdh0, A_wdh0 + 8192, A_wdh0)[p]
                for dd in range(8):
                    for s in range(2):
                        lhs = ap16(lo + dd * EL + k * 128, kc)
                        rhs = ap16(ro + dd * 1024 + s * 512, 512)
                        o = psA[0:kc, base + s * 512:base + (s + 1) * 512]
                        stt = (p == 0 and dd == 0)
                        stp = (p == 2 and dd == 7)
                        last = stp and s == 1
                        v = op("tensor", lambda t, o=o, l=lhs, r=rhs, a=stt, b=stp:
                               t.matmul(o, l, r, start=a, stop=b), t2v if last else None)
            W("vector", t2v, v)
            op("vector", lambda v_, k=k, kc=kc, b=base:
               v_.tensor_copy(ap16(A_vwh0 + k * 1024, 1024, 0, kc), psA[0:kc, b:b + 1024]))
            vv = op("vector", lambda v_, k=k, kc=kc, b=base:
                    v_.tensor_tensor(ap16(A_vwl0 + k * 1024, 1024, 0, kc),
                                     psA[0:kc, b:b + 1024],
                                     ap16(A_vwh0 + k * 1024, 1024, 0, kc), ALU.subtract), v2t)
            _vw_evac[k % 2] = vv
        vw_evac_done = P.val(vch)
        vw_mm_done = P.val(t2v)

        W("vector", t2v, vw_mm_done)
        w1_chunk(2)
        w1_chunk(3)

        W("sync", t2v, vw_mm_done)
        W("sync", vch, vw_evac_done)
        dma("sync", spwd_d, ap16(A_wdh0, 16384), ld_c)
        sp_done[0] = dma("sync", spvw_d, ap16(A_vwh0, 10240), ld_c)

        W("vector", ld_c, sp_done[0])
        W("vector", spe, 32)
        for c in range(4, 8):
            w1_chunk(c, q="scalar", sms=(spf, spf2), hi_on_scalar=True)
        op("vector", lambda v_: v_.memset(carry[:], 0.0), v2t)

        bA_v, bA_s, bA_t = P.val(vch), P.val(sch), P.val(t2v)

        for _e in ("sync", "gpsimd", "scalar", "tensor"):
            W(_e, vch, bA_v)
            W(_e, sch, bA_s)
            W(_e, t2v, bA_t)
        W("scalar", ld_c, P.val(ld_c))
        W("tensor", ld_c, sp_done[0])

        def xslot(i):
            return ap32(C_xb + (i % 2) * 1024, 1024)

        lm = {}
        lm[("ldx", 0)] = dma("sync", xslot(0), x_d[0:128, :], xsem(0))
        if NT > 1:
            lm[("ldx", 1)] = dma("sync", xslot(1), x_d[128:256, :], xsem(1))

        W("scalar", xsem(0), lm[("ldx", 0)])
        v = op("scalar", lambda s: s.activation(ap32(C_sq, 1024), xslot(0), AF.Square,
                                                accum_out=st[:, 1:2]), s_dn)
        lm[("sqx", 0)] = v
        W("vector", xsem(0), lm[("ldx", 0)])
        op("vector", lambda v_: v_.reduce_sum(st[:, 0:1], xslot(0), axis=AXX))
        W("vector", s_dn, lm[("sqx", 0)])
        v = ln_tiny(0)
        W("scalar", v2s, v)
        lm[("istd1", 0)] = ln_istd(0)
        W("vector", s_dn, lm[("istd1", 0)])
        vsp()
        v = op("vector", lambda v_: v_.tensor_scalar(ap32(C_xn, 1024), xslot(0), st[:, 2:3],
                                                     st[:, 7:8], ALU.subtract, ALU.mult), v2t)
        lm[("xn", 0)] = v
        W("tensor", v2t, lm[("xn", 0)])
        for c in range(8):
            v = op("tensor", lambda t, o=psA[:, c * 128:(c + 1) * 128],
                   s=ap32(C_xn + c * 128, 128): t.transpose(o, s, ident[:]),
                   t2v if c == 7 else None)
        lm[("tp", 0)] = v
        W("vector", t2v, lm[("tp", 0)])
        op("vector", lambda v_: v_.tensor_copy(ap16(A_xnTh, 512), psA[:, 0:512]))
        v = op("vector", lambda v_: v_.tensor_tensor(ap16(A_xnTl, 512), psA[:, 0:512],
                                                     ap16(A_xnTh, 512), ALU.subtract), v2t)
        lm[("xnT1", 0)] = v
        op("vector", lambda v_: v_.tensor_copy(ap16(A_xnTh + 512, 512), psA[:, 512:1024]))
        v = op("vector", lambda v_: v_.tensor_tensor(ap16(A_xnTl + 512, 512),
                                                     psA[:, 512:1024],
                                                     ap16(A_xnTh + 512, 512),
                                                     ALU.subtract), v2t)
        lm[("xnT", 0)] = v

        for i in range(NT):
            xb = xslot(i)
            hb = ap32(C_hb + (i % 2) * 1024, 1024)

            if 0 < i and i + 1 < NT:
                W("sync", v2t, lm[("h", i - 1)])
                lm[("ldx", i + 1)] = dma("sync", xslot(i + 1),
                                         x_d[(i + 1) * 128:(i + 2) * 128, :], xsem(i + 1))

            if i + 1 < NT:
                W("scalar", xsem(i + 1), lm[("ldx", i + 1)])
                if i > 0:
                    W("scalar", v2t, lm[("h", i - 1)])
                v = op("scalar", lambda s, j=i + 1: s.activation(ap32(C_sq, 1024), xslot(j),
                                                                 AF.Square,
                                                                 accum_out=st[:, 1:2]), s_dn)
                lm[("sqx", i + 1)] = v

            if i + 1 < NT:
                W("vector", xsem(i + 1), lm[("ldx", i + 1)])
                op("vector", lambda v_, j=i + 1: v_.reduce_sum(st[:, 0:1], xslot(j), axis=AXX))
                W("vector", s_dn, lm[("sqx", i + 1)])
                lm[("var1", i + 1)] = ln_tiny(0)
                W("scalar", v2s, lm[("var1", i + 1)])
                lm[("istd1", i + 1)] = ln_istd(0)
                W("vector", s_dn, lm[("istd1", i + 1)])
                v = op("vector", lambda v_, j=i + 1: v_.tensor_scalar(ap32(C_xn, 1024),
                                                                      xslot(j), st[:, 2:3],
                                                                      st[:, 7:8], ALU.subtract,
                                                                      ALU.mult), v2t)
                lm[("xn", i + 1)] = v

            if i > 0:
                W("tensor", s_dn, lm[("kvf", i - 1)])
            for s in range(2, 6):
                op("tensor", lambda t, o=psA[:, s * 512:(s + 1) * 512],
                   r=a16[0:2, X_b1h + s * 512: X_b1h + (s + 1) * 512]:
                   t.matmul(o, ones2b[:], r, start=True, stop=False))
            W("tensor", v2t, lm[("xnT1", i)])
            for c in range(4):
                for p in range(3):
                    lo = (A_xnTh, A_xnTh, A_xnTl)[p]
                    ro = (A_W1h, A_W1l, A_W1h)[p]
                    lhs = ap16(lo + c * 128, 128)
                    for s in range(2, 6):
                        rhs = ap16(ro + c * 4096 + s * 512, 512)
                        op("tensor", lambda t, o=psA[:, s * 512:(s + 1) * 512], l=lhs, r=rhs:
                           t.matmul(o, l, r, start=False, stop=False))
            W("tensor", v2t, lm[("xnT", i)])
            for sb_ in range(6, 8):
                op("tensor", lambda t, o=psB[:, (sb_ - 6) * 512:(sb_ - 5) * 512],
                   r=a16[0:2, X_b1h + sb_ * 512: X_b1h + (sb_ + 1) * 512]:
                   t.matmul(o, ones2b[:], r, start=True, stop=False))
            for c in range(4):
                for p in range(3):
                    lo = (A_xnTh, A_xnTh, A_xnTl)[p]
                    lhs = ap16(lo + c * 128, 128)
                    for s in range(6, 8):
                        rhs = ap16((A_W1h, A_W1l, A_W1h)[p] + c * 4096 + s * 512, 512)
                        op("tensor", lambda t, o=psB[:, (s - 6) * 512:(s - 5) * 512], l=lhs,
                           r=rhs: t.matmul(o, l, r, start=False, stop=False))
            for c in range(4, 8):
                for p in range(3):
                    lo = (A_xnTh, A_xnTh, A_xnTl)[p]
                    ro = (A_W1h, A_W1l, A_W1h)[p]
                    lhs = ap16(lo + c * 128, 128)
                    for s in range(2, 8):
                        rhs = ap16(ro + c * 4096 + s * 512, 512)
                        o = (psA[:, s * 512:(s + 1) * 512] if s < 6
                             else psB[:, (s - 6) * 512:(s - 5) * 512])
                        stp = (c == 7 and p == 2)
                        last = stp and s == 7
                        v = op("tensor", lambda t, o=o, l=lhs, r=rhs, b=stp:
                               t.matmul(o, l, r, start=False, stop=b), t2v if last else None)
            lm[("qkv1", i)] = v
            if i > 0:
                W("tensor", s_dn, lm[("q", i - 1)])
            for s in range(2):
                op("tensor", lambda t, o=psA[:, s * 512:(s + 1) * 512],
                   r=a16[0:2, X_b1h + s * 512: X_b1h + (s + 1) * 512]:
                   t.matmul(o, ones2b[:], r, start=True, stop=False))
            for c in range(8):
                for p in range(3):
                    lo = (A_xnTh, A_xnTh, A_xnTl)[p]
                    ro = (A_W1h, A_W1l, A_W1h)[p]
                    lhs = ap16(lo + c * 128, 128)
                    for s in range(2):
                        rhs = ap16(ro + c * 4096 + s * 512, 512)
                        stp = (c == 7 and p == 2)
                        last = stp and s == 1
                        v = op("tensor", lambda t, o=psA[:, s * 512:(s + 1) * 512], l=lhs,
                               r=rhs, b=stp: t.matmul(o, l, r, start=False, stop=b),
                               t2v if last else None)
            lm[("qkv", i)] = v

            W("scalar", t2v, lm[("qkv1", i)])
            v = op("scalar", lambda s: s.copy(ap32(C_ku, 1024), psA[:, 1024:2048]), s_dn)
            lm[("k", i)] = v
            v = op("scalar", lambda s: s.activation(ap32(C_g2, 1024), psB[:, 0:1024], AF.Exp,
                                                    scale=-1.0), s_dn)
            lm[("eg", i)] = v
            W("scalar", t2v, lm[("qkv", i)])
            v = op("scalar", lambda s: s.copy(ap32(C_q, 1024), psA[:, 0:1024]), s_dn)
            lm[("q", i)] = v

            W("vector", t2v, lm[("qkv1", i)])
            W("vector", s_dn, lm[("k", i)])
            if i > 0:
                W("vector", c_dma, 16 * i)
            v = op("vector", lambda v_: v_.tensor_tensor(ap32(C_kv, 1024), ap32(C_ku, 1024),
                                                         psA[:, 2048:3072], ALU.mult), v2t)
            lm[("u", i)] = v

            W("tensor", v2t, lm[("u", i)])
            W("tensor", s_dn, lm[("eg", i)])
            if i > 0:
                W("tensor", c_dma, 16 * i)
            for s in range(2):
                oo = psB[:, s * 512:(s + 1) * 512]
                op("tensor", lambda t, o=oo, s=s: t.matmul(o, uones32[:],
                                                           ap32(C_kv + s * 512, 512),
                                                           start=True, stop=False))
                v = op("tensor", lambda t, o=oo, s=s: t.matmul(o, ones1f[:],
                                                               carry[0:1, s * 512:(s + 1) * 512],
                                                               start=False, stop=True),
                       t2v if s == 1 else None)
            lm[("cs", i)] = v

            W("scalar", t2v, lm[("cs", i)])
            v = op("scalar", lambda s: s.copy(ap32(C_kv, 1024), psB[:, 0:1024]), s_dn)
            lm[("kvf", i)] = v

            W("gpsimd", s_dn, lm[("kvf", i)])
            dma("gpsimd", carry[:], ap32(C_kv, 1024, 127, 128), c_dma)

            if i + 1 < NT:
                W("tensor", v2t, lm[("xn", i + 1)])
                W("tensor", s_dn, lm[("q", i)])
                for c in range(8):
                    v = op("tensor", lambda t, o=psA[:, c * 128:(c + 1) * 128],
                           s=ap32(C_xn + c * 128, 128): t.transpose(o, s, ident[:]),
                           t2v if c in (3, 7) else None)
                    if c == 3:
                        lm[("tp1", i + 1)] = v
                lm[("tp", i + 1)] = v
                W("vector", t2v, lm[("tp1", i + 1)])
                op("vector", lambda v_: v_.tensor_copy(ap16(A_xnTh, 512), psA[:, 0:512]))
                v = op("vector", lambda v_: v_.tensor_tensor(ap16(A_xnTl, 512), psA[:, 0:512],
                                                             ap16(A_xnTh, 512),
                                                             ALU.subtract), v2t)
                lm[("xnT1", i + 1)] = v
                W("vector", t2v, lm[("tp", i + 1)])
                op("vector", lambda v_: v_.tensor_copy(ap16(A_xnTh + 512, 512),
                                                       psA[:, 512:1024]))
                v = op("vector", lambda v_: v_.tensor_tensor(ap16(A_xnTl + 512, 512),
                                                             psA[:, 512:1024],
                                                             ap16(A_xnTh + 512, 512),
                                                             ALU.subtract), v2t)
                lm[("xnT", i + 1)] = v

            W("vector", s_dn, lm[("q", i)])
            op("vector", lambda v_: v_.tensor_scalar(ap32(C_g, 1024), ap32(C_g2, 1024), 1.0,
                                                     None, ALU.add))
            op("vector", lambda v_: v_.reciprocal(ap32(C_g2, 1024), ap32(C_g, 1024)))
            rsp()
            rsp()
            if i + 1 < NT:
                W("vector", s_dn, lm[("sqx", i + 1)])
            W("vector", s_dn, lm[("kvf", i)])
            op("vector", lambda v_: v_.tensor_tensor(ap32(C_sq, 1024), ap32(C_q, 1024),
                                                     ap32(C_kv, 1024), ALU.mult))
            v = op("vector", lambda v_: v_.tensor_tensor(ap32(C_g, 1024), ap32(C_sq, 1024),
                                                         ap32(C_g2, 1024), ALU.mult), v2s)
            lm[("a", i)] = v
            W("scalar", v2s, lm[("a", i)])
            v = op("scalar", lambda s: s.activation(ap32(C_sq, 1024), ap32(C_g, 1024),
                                                    AF.Square, accum_out=st[:, 9:10]), s_dn)
            lm[("sqa", i)] = v
            op("vector", lambda v_: v_.reduce_sum(st[:, 8:9], ap32(C_g, 1024), axis=AXX))
            W("vector", s_dn, lm[("sqa", i)])
            lm[("var2", i)] = ln_tiny(8)
            W("scalar", v2s, lm[("var2", i)])
            lm[("istd2", i)] = ln_istd(8)
            W("vector", s_dn, lm[("istd2", i)])
            vsp()
            if i >= 2:
                W("vector", shsem(i - 2), lm[("sth", i - 2)])
            op("vector", lambda v_: v_.tensor_scalar(ap32(C_sq, 1024), ap32(C_g, 1024),
                                                     st[:, 10:11], st[:, 15:16],
                                                     ALU.subtract, ALU.mult))
            v = op("vector", lambda v_, x=xb, h=hb: v_.tensor_tensor(h, x, ap32(C_sq, 1024),
                                                                     ALU.add), v2t)
            lm[("h", i)] = v

            W("gpsimd", v2t, lm[("h", i)])
            lm[("sth", i)] = dma("gpsimd", h_d[i * 128:(i + 1) * 128, :], hb, shsem(i))

            op("vector", lambda v_, h=hb, j=i: v_.reduce_sum(lnst[:, 2 * j:2 * j + 1], h,
                                                             axis=AXX))
            W("scalar", v2t, lm[("h", i)])
            v = op("scalar", lambda s, h=hb, j=i: s.activation(ap32(C_sq, 1024), h, AF.Square,
                                                               accum_out=lnst[:, 2 * j + 1:
                                                                              2 * j + 2]),
                   s_dn)
            lm[("sqh", i)] = v

        bB_v, bB_s, bB_t = P.val(vch), P.val(sch), P.val(t2v)

        def hslot(i):
            return ap32(C_hb2 + (i % 3) * 1024, 1024)

        def hnslot(i):
            return ap32(C_hn2 + (i % 3) * 1024, 1024)

        W("sync", vch, lm[("h", NT - 2)])
        W("sync", shsem(0), lm[("sth", 0)])
        lm[("ldh", 0)] = dma("sync", hslot(0), h_d[0:128, :], hsem(0))
        W("sync", t2v, lm[("qkv", NT - 1)])
        W("sync", spe, 32)
        dma("sync", ap16(B_tkh, 9600), sptk_d, spl)
        rl_a = dma("sync", ap16(B_wgh, 96), spwg_d, spl)
        dma("sync", ap16(B_wdh, 16384), spwd_d, spl2)
        rl_b = dma("sync", ap16(B_vwh, 10240), spvw_d, spl2)

        for _e in ("sync", "gpsimd", "scalar", "tensor"):
            W(_e, vch, bB_v)
            W(_e, sch, bB_s)
            W(_e, t2v, bB_t)
        for _e in ("vector", "scalar", "sync", "gpsimd"):
            W(_e, c_dma, P.val(c_dma))
            W(_e, ld_x, P.val(ld_x))
            W(_e, ld_x2, P.val(ld_x2))
            W(_e, st_h, P.val(st_h))
            W(_e, st_h2, P.val(st_h2))

        if NT > 1:
            W("sync", shsem(1), lm[("sth", 1)])
            lm[("ldh", 1)] = dma("sync", hslot(1), h_d[128:256, :], hsem(1))
        op("vector", lambda v_: v_.tensor_copy(st[:, 0:2], lnst[:, 0:2]))
        v = ln_tiny(0)
        W("scalar", v2s, v)
        lm[("istd3", 0)] = ln_istd(0)
        W("vector", s_dn, lm[("istd3", 0)])
        vsp()
        W("vector", hsem(0), lm[("ldh", 0)])
        v = op("vector", lambda v_: v_.tensor_scalar(hnslot(0), hslot(0), st[:, 2:3],
                                                     st[:, 7:8], ALU.subtract, ALU.mult), v2t)
        lm[("hn", 0)] = v
        W("tensor", v2t, lm[("hn", 0)])
        for c in range(8):
            v = op("tensor", lambda t, o=psA[:, 1536 + c * 128:1664 + c * 128],
                   s=ap32(C_hn2 + c * 128, 128): t.transpose(o, s, ident[:]),
                   t2v if c == 7 else None)
        lm[("tpB", 0)] = v
        W("vector", t2v, lm[("tpB", 0)])
        op("vector", lambda v_: v_.tensor_copy(ap16(B_hnTh, 512), psA[:, 1536:2048]))
        v = op("vector", lambda v_: v_.tensor_tensor(ap16(B_hnTl, 512), psA[:, 1536:2048],
                                                     ap16(B_hnTh, 512), ALU.subtract), v2t)
        lm[("hnT1", 0)] = v
        op("vector", lambda v_: v_.tensor_copy(ap16(B_hnTh + 512, 512), psA[:, 2048:2560]))
        v = op("vector", lambda v_: v_.tensor_tensor(ap16(B_hnTl + 512, 512),
                                                     psA[:, 2048:2560],
                                                     ap16(B_hnTh + 512, 512),
                                                     ALU.subtract), v2t)
        lm[("hnT", 0)] = v

        for i in range(NT):
            hb = hslot(i)

            if i >= 1 and i + 1 < NT:
                if i >= 2:
                    W("sync", sosem(i - 2), lm[("sto", i - 2)])
                W("sync", vch, lm[("hn", i - 1)])
                W("sync", shsem(i + 1), lm[("sth", i + 1)])
                lm[("ldh", i + 1)] = dma("sync", hslot(i + 1),
                                         h_d[(i + 1) * 128:(i + 2) * 128, :], hsem(i + 1))

            if i == 0:
                W("tensor", spl, rl_a)
            W("tensor", v2t, lm[("hnT", i)])
            for c in range(8):
                for p in range(3):
                    lo = (B_hnTh, B_hnTh, B_hnTl)[p]
                    lhs = ap16(lo + c * 128, 128)
                    stt = (c == 0 and p == 0)
                    stp = (c == 7 and p == 2)
                    rhs = ap16((B_wgh, B_wgl, B_wgh)[p] + c * 6, 6)
                    op("tensor", lambda t, o=psA[:, 1024:1030], l=lhs, r=rhs, a=stt, b=stp:
                       t.matmul(o, l, r, start=a, stop=b))
                    for s0, s1 in ((0, 512), (512, EL)):
                        rhs = ap16((B_tkh, B_tkl, B_tkh)[p] + c * EL + s0, s1 - s0)
                        last = stp and s1 == EL
                        v = op("tensor", lambda t, o=psA[:, 1536 + s0:1536 + s1], l=lhs, r=rhs,
                               a=stt, b=stp: t.matmul(o, l, r, start=a, stop=b),
                               t2v if last else None)
            lm[("sc", i)] = v

            if i > 0:
                W("scalar", t2v, lm[("dl", i - 1)])
                if i > 1:
                    W("scalar", v2t, lm[("ot", i - 2)])
                v = op("scalar", lambda s: s.activation(ap32(C_s2, 1024), psA[:, 0:1024],
                                                        AF.Copy, scale=scv[:]), s_dn)
                lm[("s2", i - 1)] = v

            if i + 1 < NT:
                op("vector", lambda v_, j=i + 1: v_.tensor_copy(st[:, 0:2],
                                                                lnst[:, 2 * j:2 * j + 2]))
                lm[("var3", i + 1)] = ln_tiny(0)
                W("scalar", v2s, lm[("var3", i + 1)])
                lm[("istd3", i + 1)] = ln_istd(0)

            W("vector", t2v, lm[("sc", i)])
            op("vector", lambda v_: v_.tensor_copy(st[:, 32:38], psA[:, 1024:1030]))
            for e in range(NEXP):
                op("vector", lambda v_, e=e: v_.reduce_max(st[:, 54 + e:55 + e],
                                                           psA[:, 1536 + e * LT:
                                                               1536 + (e + 1) * LT],
                                                           axis=AXX))
            op("vector", lambda v_: v_.reduce_max(st[:, 38:39], st[:, 32:38], axis=AXX))
            vsp()
            op("vector", lambda v_: v_.tensor_scalar(st[:, 60:66], st[:, 54:60], -ISC, 1000.0,
                                                     ALU.mult, ALU.subtract))
            vsp()
            op("vector", lambda v_: v_.tensor_scalar(st[:, 40:46], st[:, 32:38], st[:, 38:39],
                                                     1000.0, ALU.is_equal, ALU.mult))
            vsp()
            vsp()
            op("vector", lambda v_: v_.tensor_tensor(st[:, 48:54], st[:, 40:46], iota[:],
                                                     ALU.subtract))
            vsp()
            vsp()
            op("vector", lambda v_: v_.reduce_max(st[:, 46:47], st[:, 48:54], axis=AXX))
            vsp()
            vsp()
            op("vector", lambda v_: v_.tensor_scalar(st[:, 84:90], st[:, 48:54], st[:, 46:47],
                                                     1000.0, ALU.is_equal, ALU.mult))
            rsp()
            rsp()
            op("vector", lambda v_: v_.tensor_tensor(st[:, 54:60], st[:, 60:66], st[:, 84:90],
                                                     ALU.add))
            rsp()
            rsp()
            lm[("am", i)] = vsp(v2s)

            if i + 1 < NT:
                W("vector", s_dn, lm[("istd3", i + 1)])
                W("vector", hsem(i + 1), lm[("ldh", i + 1)])
                v = op("vector", lambda v_, j=i + 1: v_.tensor_scalar(
                    hnslot(j), hslot(j), st[:, 2:3], st[:, 7:8], ALU.subtract, ALU.mult), v2t)
                lm[("hn", i + 1)] = v

            if i == 0:
                W("tensor", spl2, rl_b)
            if i > 0:
                W("tensor", s_dn, lm[("s2", i - 1)])
            for c in range(8):
                for p in range(3):
                    lo = (B_hnTh, B_hnTh, B_hnTl)[p]
                    lhs = ap16(lo + c * 128, 128)
                    stt = (c == 0 and p == 0)
                    stp = (c == 7 and p == 2)
                    for s in range(2):
                        rhs = ap16((B_wdh, B_wdl, B_wdh)[p] + c * 1024 + s * 512, 512)
                        last = stp and s == 1
                        v = op("tensor", lambda t, o=psA[:, s * 512:(s + 1) * 512], l=lhs,
                               r=rhs, a=stt: t.matmul(o, l, r, start=a, stop=False),
                               t2v if last else None)
            lm[("xw", i)] = v

            if i > 0:
                v = op("scalar", lambda s: s.activation(ap32(C_s3, 1024), ap32(C_s2, 1024),
                                                        AF.Exp), s_dn)
                lm[("expd", i - 1)] = v
            W("scalar", v2s, lm[("am", i)])
            for e in range(NEXP):
                v = op("scalar", lambda s, e=e: s.activation(ap32(C_exps + e * LT, LT),
                                                             psA[:, 1536 + e * LT:
                                                                 1536 + (e + 1) * LT],
                                                             AF.Exp,
                                                             bias=st[:, 54 + e:55 + e],
                                                             scale=ISC),
                       s_dn if e == NEXP - 1 else None)
            lm[("exp", i)] = v

            if i > 0:
                W("vector", s_dn, lm[("expd", i - 1)])
                op("vector", lambda v_: v_.tensor_scalar(ap32(C_s1, 1024), ap32(C_s3, 1024),
                                                         0.0, FEPS, ALU.is_equal, ALU.mult))
                v = op("vector", lambda v_: v_.tensor_tensor(ap32(C_s2, 1024),
                                                             ap32(C_s3, 1024),
                                                             ap32(C_s1, 1024), ALU.add), v2s)
                lm[("eadj", i - 1)] = v

            op("scalar", lambda s: s.activation(ap32(C_mask + 32, 600), ap32(C_exps, 600),
                                                AF.Copy, accum_out=ap32(C_mask + 8, 1)))
            op("scalar", lambda s: s.activation(ap32(C_mask + 16, 1), ap32(C_mask + 8, 1),
                                                AF.Ln))
            op("scalar", lambda s: s.activation(ap32(C_mask + 24, 1), ap32(C_mask + 16, 1),
                                                AF.Exp, scale=-1.0))
            v = op("scalar", lambda s: s.activation(ap32(C_mask + 32, 600), ap32(C_exps, 600),
                                                    AF.Copy, scale=ap32(C_mask + 24, 1)),
                   s_dn)
            lm[("mask", i)] = v

            if i > 0:
                W("tensor", s_dn, lm[("maTh", i - 1)])
                W("tensor", v2t, lm[("maT", i - 1)])
            W("tensor", s_dn, lm[("mask", i)])
            for k in range(5):
                kc = ELC[k]
                v = op("tensor", lambda t, o=psB[0:kc, k * 128:k * 128 + 128],
                       s=ap32(C_mask + 32 + k * 128, kc): t.transpose(o, s, ident[:]),
                       t2v if k == 4 else None)
            lm[("tpM", i)] = v

            W("scalar", t2v, lm[("tpM", i)])
            v = op("scalar", lambda s: s.copy(ap16(B_maTh, 640), psB[:, 0:640]), s_dn)
            lm[("maTh", i)] = v
            if i > 0:
                W("scalar", v2s, lm[("eadj", i - 1)])
                v = op("scalar", lambda s: s.activation(ap32(C_s3, 1024), ap32(C_s2, 1024),
                                                        AF.Ln), s_dn)
                lm[("lc", i - 1)] = v

            W("vector", t2v, lm[("tpM", i)])
            W("vector", s_dn, lm[("maTh", i)])
            v = op("vector", lambda v_: v_.tensor_tensor(ap16(B_maTl, 640), psB[:, 0:640],
                                                         ap16(B_maTh, 640), ALU.subtract), v2t)
            lm[("maT", i)] = v
            lm[("maT1", i)] = v

            if i + 1 < NT:
                W("tensor", v2t, lm[("hn", i + 1)])
                W("tensor", s_dn, lm[("exp", i)])
                for c in range(8):
                    v = op("tensor", lambda t, o=psA[:, 1536 + c * 128:1664 + c * 128],
                           s=ap32(C_hn2 + ((i + 1) % 3) * 1024 + c * 128, 128):
                           t.transpose(o, s, ident[:]),
                           t2v if c in (3, 7) else None)
                    if c == 3:
                        lm[("tpB1", i + 1)] = v
                lm[("tpB", i + 1)] = v
                W("vector", t2v, lm[("tpB1", i + 1)])
                op("vector", lambda v_: v_.tensor_copy(ap16(B_hnTh, 512),
                                                       psA[:, 1536:2048]))
                v = op("vector", lambda v_: v_.tensor_tensor(ap16(B_hnTl, 512),
                                                             psA[:, 1536:2048],
                                                             ap16(B_hnTh, 512),
                                                             ALU.subtract), v2t)
                lm[("hnT1", i + 1)] = v
                W("vector", t2v, lm[("tpB", i + 1)])
                op("vector", lambda v_: v_.tensor_copy(ap16(B_hnTh + 512, 512),
                                                       psA[:, 2048:2560]))
                v = op("vector", lambda v_: v_.tensor_tensor(ap16(B_hnTl + 512, 512),
                                                             psA[:, 2048:2560],
                                                             ap16(B_hnTh + 512, 512),
                                                             ALU.subtract), v2t)
                lm[("hnT", i + 1)] = v

            if i > 0:
                W("vector", s_dn, lm[("lc", i - 1)])
                v = op("vector", lambda v_, j=i - 1: v_.tensor_tensor(ap32(C_s1, 1024),
                                                                      ap32(C_s3, 1024),
                                                                      hnslot(j), ALU.add), v2s)
                lm[("fr", i - 1)] = v
                op("vector", lambda v_: v_.reduce_sum(st[:, 8:9], ap32(C_s1, 1024), axis=AXX))

            if i > 0:
                W("scalar", v2s, lm[("fr", i - 1)])
                v = op("scalar", lambda s: s.activation(ap32(C_s2, 1024), ap32(C_s1, 1024),
                                                        AF.Square, accum_out=st[:, 9:10]), s_dn)
                lm[("sqf", i - 1)] = v

            if i > 0:
                W("vector", s_dn, lm[("sqf", i - 1)])
                lm[("var4", i - 1)] = ln_tiny(8)
                W("scalar", v2s, lm[("var4", i - 1)])
                lm[("istd4", i - 1)] = ln_istd(8)

            W("tensor", s_dn, lm[("maTh", i)])
            for p in range(3):
                if p == 2:
                    W("tensor", v2t, lm[("maT", i)])
                lo = (B_maTh, B_maTh, B_maTl)[p]
                ro = (B_vwh, B_vwl, B_vwh)[p]
                for k in range(5):
                    kc = ELC[k]
                    lhs = ap16(lo + k * 128, 128, 0, kc)
                    for s in range(2):
                        rhs = ap16(ro + k * 1024 + s * 512, 512, 0, kc)
                        op("tensor", lambda t, o=psA[:, s * 512:(s + 1) * 512], l=lhs, r=rhs:
                           t.matmul(o, l, r, start=False, stop=False))
            for s in range(2):
                v = op("tensor", lambda t, o=psA[:, s * 512:(s + 1) * 512],
                       r=a16[0:2, X_wdbh + s * 512: X_wdbh + (s + 1) * 512]:
                       t.matmul(o, ones2b[:], r, start=False, stop=True),
                       t2v if s == 1 else None)
            lm[("dl", i)] = v

            if i > 0:
                W("vector", s_dn, lm[("istd4", i - 1)])
                op("vector", lambda v_: v_.tensor_scalar(ap32(C_s2, 1024), ap32(C_s1, 1024),
                                                         st[:, 10:11], st[:, 15:16],
                                                         ALU.subtract, ALU.mult))
                v = op("vector", lambda v_, j=i - 1:
                       v_.tensor_tensor(hslot(j), hslot(j), ap32(C_s2, 1024), ALU.add), v2t)
                lm[("ot", i - 1)] = v
                W("gpsimd", v2t, lm[("ot", i - 1)])
                if i >= 3:
                    W("gpsimd", sosem(i - 3), lm[("sto", i - 3)])
                lm[("sto", i - 1)] = dma("gpsimd", out_d[(i - 1) * 128:i * 128, :],
                                         hslot(i - 1), sosem(i - 1))

        i = NT
        W("scalar", t2v, lm[("dl", i - 1)])
        if i > 1:
            W("scalar", v2t, lm[("ot", i - 2)])
        v = op("scalar", lambda s: s.activation(ap32(C_s2, 1024), psA[:, 0:1024],
                                                AF.Copy, scale=scv[:]), s_dn)
        lm[("s2", i - 1)] = v
        v = op("scalar", lambda s: s.activation(ap32(C_s3, 1024), ap32(C_s2, 1024), AF.Exp),
               s_dn)
        lm[("expd", i - 1)] = v
        W("vector", s_dn, lm[("expd", i - 1)])
        op("vector", lambda v_: v_.tensor_scalar(ap32(C_s1, 1024), ap32(C_s3, 1024),
                                                 0.0, FEPS, ALU.is_equal, ALU.mult))
        v = op("vector", lambda v_: v_.tensor_tensor(ap32(C_s2, 1024), ap32(C_s3, 1024),
                                                     ap32(C_s1, 1024), ALU.add), v2s)
        lm[("eadj", i - 1)] = v
        W("scalar", v2s, lm[("eadj", i - 1)])
        v = op("scalar", lambda s: s.activation(ap32(C_s3, 1024), ap32(C_s2, 1024), AF.Ln),
               s_dn)
        lm[("lc", i - 1)] = v
        W("vector", s_dn, lm[("lc", i - 1)])
        v = op("vector", lambda v_, j=i - 1: v_.tensor_tensor(ap32(C_s1, 1024),
                                                              ap32(C_s3, 1024), hnslot(j),
                                                              ALU.add), v2s)
        lm[("fr", i - 1)] = v
        op("vector", lambda v_: v_.reduce_sum(st[:, 8:9], ap32(C_s1, 1024), axis=AXX))
        W("scalar", v2s, lm[("fr", i - 1)])
        v = op("scalar", lambda s: s.activation(ap32(C_s2, 1024), ap32(C_s1, 1024),
                                                AF.Square, accum_out=st[:, 9:10]), s_dn)
        lm[("sqf", i - 1)] = v
        W("vector", s_dn, lm[("sqf", i - 1)])
        lm[("var4", i - 1)] = ln_tiny(8)
        W("scalar", v2s, lm[("var4", i - 1)])
        lm[("istd4", i - 1)] = ln_istd(8)
        W("vector", s_dn, lm[("istd4", i - 1)])
        vsp()
        op("vector", lambda v_: v_.tensor_scalar(ap32(C_s2, 1024), ap32(C_s1, 1024),
                                                 st[:, 10:11], st[:, 15:16],
                                                 ALU.subtract, ALU.mult))
        v = op("vector", lambda v_, j=i - 1:
               v_.tensor_tensor(hslot(j), hslot(j), ap32(C_s2, 1024), ALU.add), v2t)
        lm[("ot", i - 1)] = v
        W("gpsimd", v2t, lm[("ot", i - 1)])
        if i >= 3:
            W("gpsimd", sosem(i - 3), lm[("sto", i - 3)])
        lm[("sto", i - 1)] = dma("gpsimd", out_d[(i - 1) * 128:i * 128, :],
                                 hslot(i - 1), sosem(i - 1))


        W("gpsimd", st_o, P.val(st_o))
        W("gpsimd", st_o2, P.val(st_o2))

        @block.sync
        def _(e):
            for f in P.ops["sync"]:
                f(e)

        @block.vector
        def _(e):
            for f in P.ops["vector"]:
                f(e)

        @block.scalar
        def _(e):
            for f in P.ops["scalar"]:
                f(e)

        @block.tensor
        def _(e):
            for f in P.ops["tensor"]:
                f(e)

        @block.gpsimd
        def _(e):
            for f in P.ops["gpsimd"]:
                f(e)

    return nc


def _prep(inputs):
    f32 = np.float32
    Wq = np.asarray(inputs["L0_attn_W_qkv"], f32)
    Wg = np.asarray(inputs["L0_attn_W_gate"], f32)
    w1 = np.ascontiguousarray(np.concatenate([Wq, Wg], axis=0).T)
    b1 = np.ascontiguousarray(np.concatenate([np.asarray(inputs["L0_attn_b_qkv"], f32),
                                              np.asarray(inputs["L0_attn_b_gate"], f32)]))
    at = np.ascontiguousarray(np.transpose(np.asarray(inputs["L0_ffn_A"], f32), (0, 2, 1)))
    bm = np.ascontiguousarray(np.asarray(inputs["L0_ffn_B"], f32))
    wg = np.ascontiguousarray(np.asarray(inputs["L0_ffn_w_gate"], f32))
    wt = np.ascontiguousarray(np.asarray(inputs["L0_ffn_wt_w"], f32).T)
    wd = np.ascontiguousarray(np.asarray(inputs["L0_ffn_wd_w"], f32).T)
    return {
        "w1": w1, "b1": b1, "at": at, "bm": bm, "wg": wg, "wt": wt,
        "wtb": np.ascontiguousarray(np.asarray(inputs["L0_ffn_wt_b"], f32)),
        "wd": wd,
        "wdb": np.ascontiguousarray(np.asarray(inputs["L0_ffn_wd_b"], f32)),
        "sc": np.ascontiguousarray(np.asarray(inputs["L0_ffn_scale"], f32)),
        "ident": np.eye(128, dtype=f32),
        "uones": np.triu(np.ones((128, 128), dtype=f32)).astype(ml_dtypes.bfloat16),
        "uones32": np.triu(np.ones((128, 128), dtype=f32)),
        "iota6": np.tile(np.arange(6, dtype=f32), (128, 1)),
        "ones1f": np.ones((1, 128), dtype=f32),
        "onesb": np.ones((1, 128), dtype=ml_dtypes.bfloat16),
        "ones2b": np.ones((2, 128), dtype=ml_dtypes.bfloat16),
        "cvals": np.tile(np.array([1e-5, -0.5, -1.0, 1024 ** -0.5, -(1024 ** -0.5),
                                   1000.0, np.finfo(np.float32).eps, 0.0], dtype=f32),
                         (128, 1)),
    }


def kernel(**inputs):
    x = np.asarray(inputs["L0_attn_x"], np.float32)
    base = _prep(inputs)
    S = x.shape[1]
    nc = build(S)
    in_maps = []
    for i in range(x.shape[0]):
        m = dict(base)
        m["x"] = np.ascontiguousarray(x[i])
        in_maps.append(m)
    res = run_bass_kernel_spmd(nc, in_maps, core_ids=list(range(x.shape[0])))
    return np.stack([np.asarray(res.results[i]["out"]) for i in range(x.shape[0])], axis=0)


# version 38
# speedup vs baseline: 1.0862x; 1.0046x over previous
import numpy as np
import ml_dtypes
import concourse.bass as bass
import concourse.mybir as mybir
from concourse.bass_utils import run_bass_kernel_spmd

F32 = mybir.dt.float32
BF16 = mybir.dt.bfloat16
AF = mybir.ActivationFunctionType
ALU = mybir.AluOpType
AXX = mybir.AxisListType.X

D = 1024
W1N = 4096
EL = 600
ELC = [128, 128, 128, 128, 88]
NEXP = 6
LT = 100
LN_EPS = 1e-5
FEPS = float(np.finfo(np.float32).eps)
ISC = float(1024 ** -0.5)

ENGINES = ("sync", "vector", "scalar", "tensor", "gpsimd")


class Plan:
    def __init__(self):
        self.ops = {e: [] for e in ENGINES}
        self.cnt = {}

    def inc(self, sem, by=1):
        k = id(sem)
        self.cnt[k] = self.cnt.get(k, 0) + by
        return self.cnt[k]

    def val(self, sem):
        return self.cnt.get(id(sem), 0)


def build(S, dbg=False):
    NT = S // 128
    nc = bass.Bass("TRN2", target_bir_lowering=False)

    x_d = nc.dram_tensor("x", [S, D], F32, kind="ExternalInput").ap()
    w1_d = nc.dram_tensor("w1", [D, W1N], F32, kind="ExternalInput").ap()
    b1_d = nc.dram_tensor("b1", [W1N], F32, kind="ExternalInput").ap()
    at_d = nc.dram_tensor("at", [NEXP, 16, LT], F32, kind="ExternalInput").ap()
    bm_d = nc.dram_tensor("bm", [NEXP, 16, D], F32, kind="ExternalInput").ap()
    wg_d = nc.dram_tensor("wg", [D, NEXP], F32, kind="ExternalInput").ap()
    wt_d = nc.dram_tensor("wt", [D, D], F32, kind="ExternalInput").ap()
    wtb_d = nc.dram_tensor("wtb", [D], F32, kind="ExternalInput").ap()
    wd_d = nc.dram_tensor("wd", [D, D], F32, kind="ExternalInput").ap()
    wdb_d = nc.dram_tensor("wdb", [D], F32, kind="ExternalInput").ap()
    sc_d = nc.dram_tensor("sc", [1], F32, kind="ExternalInput").ap()
    id_d = nc.dram_tensor("ident", [128, 128], F32, kind="ExternalInput").ap()
    uo_d = nc.dram_tensor("uones", [128, 128], BF16, kind="ExternalInput").ap()
    uo32_d = nc.dram_tensor("uones32", [128, 128], F32, kind="ExternalInput").ap()
    io_d = nc.dram_tensor("iota6", [128, NEXP], F32, kind="ExternalInput").ap()
    o1_d = nc.dram_tensor("ones1f", [1, 128], F32, kind="ExternalInput").ap()
    ob_d = nc.dram_tensor("onesb", [1, 128], BF16, kind="ExternalInput").ap()
    o2_d = nc.dram_tensor("ones2b", [2, 128], BF16, kind="ExternalInput").ap()
    cv_d = nc.dram_tensor("cvals", [128, 8], F32, kind="ExternalInput").ap()
    out_d = nc.dram_tensor("out", [S, D], F32, kind="ExternalOutput").ap()
    h_d = nc.dram_tensor("hscr", [S, D], F32,
                         kind="ExternalOutput" if dbg else "Internal").ap()
    spwd_d = nc.dram_tensor("spwd", [128, 16384], BF16, kind="Internal").ap()
    sptk_d = nc.dram_tensor("sptk", [128, 9600], BF16, kind="Internal").ap()
    spvw_d = nc.dram_tensor("spvw", [128, 10240], BF16, kind="Internal").ap()
    spwg_d = nc.dram_tensor("spwg", [128, 96], BF16, kind="Internal").ap()

    from contextlib import ExitStack
    with ExitStack() as _es:
        def _sb(*a):
            return _es.enter_context(nc.sbuf_tensor(*a))

        def _ps(*a):
            return _es.enter_context(nc.psum_tensor(*a))

        def _sem(n):
            return _es.enter_context(nc.semaphore(n))

        a16 = _sb("a16", [128, 79872], BF16)
        a32 = _sb("a32", [128, 11264], F32)
        st = _sb("stats", [128, 96], F32)
        wtbs = _sb("wtbs", [128, 8], F32)
        iota = _sb("iota_sb", [128, NEXP], F32)
        scin = _sb("scin", [1, 1], F32)
        scv = _sb("scv", [128, 1], F32)
        ident = _sb("ident_sb", [128, 128], F32)
        uones = _sb("uones_sb", [128, 128], BF16)
        uones32 = _sb("uones32_sb", [128, 128], F32)
        ones1f = _sb("ones1f_sb", [1, 128], F32)
        onesb = _sb("onesb_sb", [1, 128], BF16)
        ones2b = _sb("ones2b_sb", [2, 128], BF16)
        lnst = _sb("lnstats", [128, 96], F32)
        carry = _sb("carry", [1, D], F32)
        cst = _sb("cvals_sb", [128, 8], F32)
        for _k, _v in enumerate([LN_EPS, -0.5, -1.0, ISC, -ISC, 1000.0, FEPS, 0.0]):
            nc.const_aps.register(F32, _v, cst[:, _k:_k + 1])
        psA = _ps("psA", [128, 3072], F32)
        psB = _ps("psB", [128, 1024], F32)
        ld_c = _sem("ld_c")
        t2v = _sem("t2v")
        vch = _sem("vch")
        sch = _sem("sch")
        v2t = vch
        v2s = vch
        s_dn = sch
        ld_x = _sem("ld_x")
        c_dma = _sem("c_dma")
        st_h = _sem("st_h")
        ld_h = _sem("ld_h")
        st_o = _sem("st_o")
        ld_x2 = _sem("ld_x2")
        ld_h2 = _sem("ld_h2")
        st_h2 = _sem("st_h2")
        st_o2 = _sem("st_o2")
        spl = _sem("spl")
        spl2 = _sem("spl2")
        spe = _sem("spe")
        spf = _sem("spf")
        spf2 = _sem("spf2")
        block = _es.enter_context(nc.Block())
        P = Plan()

        A_wth, A_wtl = 0, 8192
        A_wdh0 = 16384
        A_tkh0 = 32768
        A_tkl0 = 32768 + 4800
        A_vth, A_vtl = 42368, 47168
        A_vwh0 = 51968
        A_vwl0 = 51968 + 5120
        A_wgh0 = 62208
        A_wgl0 = 62208 + 48
        A_W1h, A_W1l = 0, 32768
        A_xnTh, A_xnTl = 65536, 66560
        A_uh, A_ul = 67584, 68608
        B_wdh, B_wdl = 0, 8192
        B_tkh, B_tkl = 16384, 21184
        B_vwh, B_vwl = 25984, 31104
        B_wgh, B_wgl = 36224, 36272
        B_hnTh, B_hnTl = 36352, 37376
        B_maTh, B_maTl = 38400, 39040
        X_b1h, X_b1l = 69632, 73728
        X_wdbh, X_wdbl = 77824, 78848

        C_xb, C_xn, C_ku, C_q, C_g, C_g2, C_kv, C_hb = \
            0, 2048, 3072, 4096, 5120, 6144, 7168, 8192
        C_sq = 10240
        C_hb2, C_hn2, C_exps, C_mask, C_s1, C_s2 = \
            0, 3072, 6144, 6784, 7424, 8448
        C_s3 = 9472
        C_stage = 0
        C_vts = 8192
        C_wgs = 8792

        def ap16(off, n, p0=0, p1=128):
            return a16[p0:p1, off:off + n]

        def ap32(off, n, p0=0, p1=128):
            return a32[p0:p1, off:off + n]

        def W(eng, sem, val):
            if val > 0:
                P.ops[eng].append(lambda e, s=sem, v=val: e.wait_ge(s, v))

        def dma(eng, out, in_, sem, slow=False):
            v = P.inc(sem, 16)
            P.ops[eng].append(lambda e, o=out, i=in_, s=sem, sl=slow:
                              e.dma_start(out=o, in_=i,
                                          allow_slow_non_contiguous=sl).then_inc(s, 16))
            return v

        def op(eng, fn, sem=None):
            if eng in ("vector", "scalar"):
                ch = vch if eng == "vector" else sch
                prev = P.val(ch)
                if prev > 0:
                    P.ops[eng].append(lambda e, s=ch, v=prev: e.wait_ge(s, v))
                v = P.inc(ch, 1)
                P.ops[eng].append(lambda e, f=fn, s=ch: f(e).then_inc(s, 1))
                return v
            if sem is None:
                P.ops[eng].append(fn)
                return None
            v = P.inc(sem, 1)
            P.ops[eng].append(lambda e, f=fn, s=sem: f(e).then_inc(s, 1))
            return v

        def vsp(sem=None):
            return P.val(vch)

        def rsp():
            return op("vector", lambda v_: v_.tensor_copy(st[:, 94:95], st[:, 88:89]))

        def xsem(i):
            return ld_x if i % 2 == 0 else ld_x2

        def hsem(i):
            return ld_h if i % 2 == 0 else ld_h2

        def shsem(i):
            return st_h if i % 2 == 0 else st_h2

        def sosem(i):
            return st_o if i % 2 == 0 else st_o2

        def nop(eng, sem):
            v = P.inc(sem, 1)
            P.ops[eng].append(lambda e, s=sem: e.nop().then_inc(s, 1))
            return v

        def ln_tiny(o):
            vsp()
            vsp()
            op("vector", lambda v_: v_.tensor_scalar(st[:, o + 2:o + 3], st[:, o:o + 1],
                                                     1.0 / D, None, ALU.mult))
            op("vector", lambda v_: v_.tensor_scalar(st[:, o + 4:o + 5], st[:, o + 1:o + 2],
                                                     1.0 / D, None, ALU.mult))
            vsp()
            op("vector", lambda v_: v_.tensor_tensor(st[:, o + 3:o + 4], st[:, o + 2:o + 3],
                                                     st[:, o + 2:o + 3], ALU.mult))
            vsp()
            vsp()
            op("vector", lambda v_: v_.tensor_tensor(st[:, o + 5:o + 6], st[:, o + 4:o + 5],
                                                     st[:, o + 3:o + 4], ALU.subtract))
            vsp()
            return vsp(v2s)

        def ln_istd(o):
            op("scalar", lambda s, o=o: s.activation(st[:, o + 6:o + 7], st[:, o + 5:o + 6],
                                                     AF.Ln, bias=LN_EPS))
            return op("scalar", lambda s, o=o: s.activation(st[:, o + 7:o + 8],
                                                            st[:, o + 6:o + 7],
                                                            AF.Exp, scale=-0.5), s_dn)

        dma("sync", ident[:], id_d, ld_c)
        dma("sync", uones[:], uo_d, ld_c)
        dma("sync", uones32[:], uo32_d, ld_c)
        dma("sync", iota[:], io_d, ld_c)
        dma("sync", ones1f[:], o1_d, ld_c)
        dma("sync", onesb[:], ob_d, ld_c)
        dma("sync", ones2b[:], o2_d, ld_c)
        dma("sync", cst[:], cv_d, ld_c)
        dma("sync", scin[:], sc_d.rearrange("(a n) -> a n", a=1), ld_c)
        dma("sync", ap32(C_wgs, 48).rearrange("p (c e) -> p c e", e=6),
            wg_d.rearrange("(c p) e -> p c e", p=128), ld_c)
        dma("sync", a32[0:1, 0:4096], b1_d.rearrange("(a n) -> a n", a=1), ld_c)
        dma("sync", wtbs[:], wtb_d.rearrange("(c p) -> p c", p=128), ld_c, slow=True)
        v_bias = dma("sync", a32[0:1, 4096:5120], wdb_d.rearrange("(a n) -> a n", a=1), ld_c)

        op("vector", lambda v_: v_.memset(st[:, 84:96], 0.0))
        W("vector", ld_c, v_bias)
        op("vector", lambda v_: v_.tensor_copy(ap16(X_b1h, 4096, 0, 1), a32[0:1, 0:4096]))
        op("vector", lambda v_: v_.tensor_tensor(ap16(X_b1l, 4096, 0, 1), a32[0:1, 0:4096],
                                                 ap16(X_b1h, 4096, 0, 1), ALU.subtract))
        op("vector", lambda v_: v_.tensor_copy(ap16(X_wdbh, 1024, 0, 1), a32[0:1, 4096:5120]))
        vb1 = op("vector", lambda v_: v_.tensor_tensor(ap16(X_wdbl, 1024, 0, 1),
                                                       a32[0:1, 4096:5120],
                                                       ap16(X_wdbh, 1024, 0, 1),
                                                       ALU.subtract), v2t)

        W("sync", v2t, vb1)
        dma("sync", a16[1:2, X_b1h:X_b1h + 4096], ap16(X_b1l, 4096, 0, 1), ld_c)
        v_brow = dma("sync", a16[1:2, X_wdbh:X_wdbh + 1024], ap16(X_wdbl, 1024, 0, 1), ld_c)
        dma("sync", a32[0:16, 6144:6744].rearrange("r (e l) -> r e l", l=LT),
            at_d.rearrange("e r l -> r e l"), ld_c)
        v_ld = dma("sync", a32[0:16, 0:6144].rearrange("r (e c) -> r e c", c=D),
                   bm_d.rearrange("e r c -> r e c"), ld_c)

        _wt_slot_last = [0, 0]

        def wt_chunk(c):
            slot = 8840 + (c % 2) * 1024
            if _wt_slot_last[c % 2] > 0:
                W("sync", vch, _wt_slot_last[c % 2])
            sm = spl if c % 2 == 0 else spl2
            vd = dma("sync", ap32(slot, 1024), wt_d[c * 128:(c + 1) * 128, :], sm)
            W("vector", sm, vd)
            op("vector", lambda v_, c=c, sl=slot:
               v_.tensor_copy(ap16(A_wth + c * 1024, 1024), ap32(sl, 1024)))
            vv = op("vector", lambda v_, c=c, sl=slot:
                    v_.tensor_tensor(ap16(A_wtl + c * 1024, 1024), ap32(sl, 1024),
                                     ap16(A_wth + c * 1024, 1024), ALU.subtract), v2t)
            _wt_slot_last[c % 2] = vv
            return vv

        W("tensor", ld_c, v_ld)
        tok_mm_done = None
        _tk_evac = [0, 0]
        for c in range(8):
            pso = psB if c % 2 == 0 else psA
            if _tk_evac[c % 2] > 0:
                W("tensor", v2t, _tk_evac[c % 2])
            for e in range(NEXP):
                lhs = a32[0:16, e * 1024 + c * 128: e * 1024 + (c + 1) * 128]
                rhs = a32[0:16, 6144 + e * LT: 6144 + (e + 1) * LT]
                o = pso[:, e * 128: e * 128 + LT]
                last = (e == NEXP - 1)
                v = op("tensor", lambda t, o=o, l=lhs, r=rhs: t.matmul(o, l, r, start=True,
                                                                      stop=True),
                       t2v if last else None)
            tok_mm_done = v
            W("vector", t2v, v)
            src = pso[:, 0:768].rearrange("p (e x) -> p e x", x=128)[:, :, 0:LT]
            dsth = ap16(A_tkh0 + c * EL, EL).rearrange("p (e l) -> p e l", l=LT)
            dstl = ap16(A_tkl0 + c * EL, EL).rearrange("p (e l) -> p e l", l=LT)
            op("vector", lambda v_, d=dsth, s=src: v_.tensor_copy(d, s))
            v = op("vector", lambda v_, d=dstl, s=src, hh=dsth:
                   v_.tensor_tensor(d, s, hh, ALU.subtract), v2t)
            _tk_evac[c % 2] = v
            if c == 6:
                tok_psB_free = v
            wt_chunk(c)
        tok_evac_done = P.val(v2t)
        wt_ready = tok_evac_done

        W("tensor", v2t, tok_psB_free)
        W("tensor", ld_c, v_ld)
        v = op("tensor", lambda t: t.matmul(psB[:, 0:1], ones1f[:], scin[:], start=True,
                                            stop=True), t2v)
        W("vector", t2v, v)
        op("vector", lambda v_: v_.tensor_copy(scv[:], psB[:, 0:1]))
        op("vector", lambda v_: v_.tensor_copy(ap16(A_wgh0, 48), ap32(C_wgs, 48)))
        op("vector", lambda v_: v_.tensor_tensor(ap16(A_wgl0, 48), ap32(C_wgs, 48),
                                                 ap16(A_wgh0, 48), ALU.subtract), v2t)
        spill_ready = P.val(v2t)

        _slot_last = [0, 0]

        def split_weight(dram_ap, ncols, offh, offl, pre_wait=0):
            splits = []
            for c in range(8):
                slot = C_stage + (c % 2) * 4096
                if _slot_last[c % 2] > 0:
                    W("sync", vch, _slot_last[c % 2])
                sm = spl if c % 2 == 0 else spl2
                vd = dma("sync", ap32(slot, ncols), dram_ap[c * 128:(c + 1) * 128, :], sm)
                W("vector", sm, vd)
                if c == 0 and pre_wait:
                    W("vector", ld_c, pre_wait)
                op("vector", lambda v_, c=c, s=slot, n=ncols, oh=offh:
                   v_.tensor_copy(ap16(oh + c * n, n), ap32(s, n)))
                vv = op("vector", lambda v_, c=c, s=slot, n=ncols, oh=offh, ol=offl:
                        v_.tensor_tensor(ap16(ol + c * n, n), ap32(s, n), ap16(oh + c * n, n),
                                         ALU.subtract), v2t)
                splits.append(vv)
                _slot_last[c % 2] = vv
            return P.val(v2t)

        W("sync", t2v, tok_mm_done)
        W("sync", v2t, vb1)
        wd_splits = []

        def wd_chunk_stage(c):
            slot = C_stage + (c % 2) * 4096
            if _slot_last[c % 2] > 0:
                W("sync", vch, _slot_last[c % 2])
            else:
                W("sync", vch, _wt_slot_last[c % 2])
            sm = spl if c % 2 == 0 else spl2
            vd = dma("sync", ap32(slot, 1024), wd_d[c * 128:(c + 1) * 128, :], sm)
            W("vector", sm, vd)
            op("vector", lambda v_, c=c, sl=slot:
               v_.tensor_copy(ap16(A_wdh0 + c * 1024, 1024), ap32(sl, 1024)))
            vv = op("vector", lambda v_, c=c, sl=slot:
                    v_.tensor_tensor(ap16(A_wdh0 + 8192 + c * 1024, 1024), ap32(sl, 1024),
                                     ap16(A_wdh0 + c * 1024, 1024), ALU.subtract), v2t)
            _slot_last[c % 2] = vv
            return vv

        W("tensor", v2t, wt_ready)
        _vt_evac = [0, 0]
        for d in range(8):
            base = (d % 2) * 1024
            if _vt_evac[d % 2] > 0:
                W("tensor", v2t, _vt_evac[d % 2])
            for p in range(3):
                lo = (A_wth, A_wth, A_wtl)[p]
                ro = (A_tkh0, A_tkl0, A_tkh0)[p]
                for c in range(8):
                    for s0, s1 in ((0, 512), (512, EL)):
                        lhs = ap16(lo + c * 1024 + d * 128, 128)
                        rhs = ap16(ro + c * EL + s0, s1 - s0)
                        o = psA[:, base + s0:base + s1]
                        stt = (p == 0 and c == 0)
                        stp = (p == 2 and c == 7)
                        last = stp and s1 == EL
                        v = op("tensor", lambda t, o=o, l=lhs, r=rhs, a=stt, b=stp:
                               t.matmul(o, l, r, start=a, stop=b), t2v if last else None)
            W("vector", t2v, v)
            op("vector", lambda v_, d=d, b=base: v_.tensor_scalar(ap32(C_vts, EL),
                                                                  psA[:, b:b + EL],
                                                                  wtbs[:, d:d + 1], None,
                                                                  ALU.add))
            op("vector", lambda v_, d=d: v_.tensor_copy(ap16(A_vth + d * EL, EL),
                                                        ap32(C_vts, EL)))
            vv = op("vector", lambda v_, d=d:
                    v_.tensor_tensor(ap16(A_vtl + d * EL, EL), ap32(C_vts, EL),
                                     ap16(A_vth + d * EL, EL), ALU.subtract), v2t)
            _vt_evac[d % 2] = vv
            if d < 4:
                wd_splits.append(wd_chunk_stage(2 * d))
                wd_splits.append(wd_chunk_stage(2 * d + 1))
        vt_mm_done = P.val(t2v)
        wd_ready = P.val(vch)

        W("gpsimd", v2t, spill_ready)
        dma("gpsimd", sptk_d, ap16(A_tkh0, 9600), spe)
        dma("gpsimd", spwg_d, ap16(A_wgh0, 96), spe)

        op("vector", lambda v_: v_.memset(
            ap16(A_vth, 4800).rearrange("p (d e l) -> p d e l", e=6, l=LT)[:, :, :, 0:1], 0.0))
        op("vector", lambda v_: v_.memset(
            ap16(A_vtl, 4800).rearrange("p (d e l) -> p d e l", e=6, l=LT)[:, :, :, 0:1], 0.0),
           v2t)
        l0z = P.val(vch)


        sp_done = [0]

        def w1_chunk(c, q="sync", sms=None, hi_on_scalar=False):
            slot = C_stage + (c % 2) * 4096
            if _slot_last[c % 2] > 0:
                W(q, vch, _slot_last[c % 2])
            sm = (sms or (spl, spl2))[c % 2]
            vd = dma(q, ap32(slot, 4096), w1_d[c * 128:(c + 1) * 128, :], sm)
            if hi_on_scalar:
                W("scalar", sm, vd)
                if c == 4:
                    W("scalar", ld_c, sp_done[0])
                    W("scalar", spe, 32)
                hv = op("scalar", lambda sc_, c=c, s=slot:
                        sc_.copy(ap16(A_W1h + c * 4096, 4096), ap32(s, 4096)), s_dn)
                W("vector", s_dn, hv)
            else:
                W("vector", sm, vd)
                op("vector", lambda v_, c=c, s=slot:
                   v_.tensor_copy(ap16(A_W1h + c * 4096, 4096), ap32(s, 4096)))
            vv = op("vector", lambda v_, c=c, s=slot:
                    v_.tensor_tensor(ap16(A_W1l + c * 4096, 4096), ap32(s, 4096),
                                     ap16(A_W1h + c * 4096, 4096), ALU.subtract), v2t)
            _slot_last[c % 2] = vv

        W("vector", t2v, vt_mm_done)
        W("vector", spe, 32)
        w1_chunk(0)
        w1_chunk(1)

        op("vector", lambda v_: v_.memset(ap16(A_vwh0 + 4 * 1024, 1024, 64, 128), 0.0))
        op("vector", lambda v_: v_.memset(ap16(A_vwl0 + 4 * 1024, 1024, 64, 128), 0.0))

        W("tensor", v2t, wd_ready)
        W("tensor", v2t, l0z)
        _vw_evac = [0, 0]
        for k in range(5):
            kc = ELC[k]
            base = (k % 2) * 1024
            if _vw_evac[k % 2] > 0:
                W("tensor", v2t, _vw_evac[k % 2])
            for p in range(3):
                lo = (A_vth, A_vth, A_vtl)[p]
                ro = (A_wdh0, A_wdh0 + 8192, A_wdh0)[p]
                for dd in range(8):
                    for s in range(2):
                        lhs = ap16(lo + dd * EL + k * 128, kc)
                        rhs = ap16(ro + dd * 1024 + s * 512, 512)
                        o = psA[0:kc, base + s * 512:base + (s + 1) * 512]
                        stt = (p == 0 and dd == 0)
                        stp = (p == 2 and dd == 7)
                        last = stp and s == 1
                        v = op("tensor", lambda t, o=o, l=lhs, r=rhs, a=stt, b=stp:
                               t.matmul(o, l, r, start=a, stop=b), t2v if last else None)
            W("vector", t2v, v)
            op("vector", lambda v_, k=k, kc=kc, b=base:
               v_.tensor_copy(ap16(A_vwh0 + k * 1024, 1024, 0, kc), psA[0:kc, b:b + 1024]))
            vv = op("vector", lambda v_, k=k, kc=kc, b=base:
                    v_.tensor_tensor(ap16(A_vwl0 + k * 1024, 1024, 0, kc),
                                     psA[0:kc, b:b + 1024],
                                     ap16(A_vwh0 + k * 1024, 1024, 0, kc), ALU.subtract), v2t)
            _vw_evac[k % 2] = vv
        vw_evac_done = P.val(vch)
        vw_mm_done = P.val(t2v)

        W("vector", t2v, vw_mm_done)
        w1_chunk(2)
        w1_chunk(3)

        W("sync", t2v, vw_mm_done)
        W("sync", vch, vw_evac_done)
        dma("sync", spwd_d, ap16(A_wdh0, 16384), ld_c)
        sp_done[0] = dma("sync", spvw_d, ap16(A_vwh0, 10240), ld_c)

        W("vector", ld_c, sp_done[0])
        W("vector", spe, 32)
        for c in range(4, 8):
            w1_chunk(c, q="scalar", sms=(spf, spf2), hi_on_scalar=True)
        op("vector", lambda v_: v_.memset(carry[:], 0.0), v2t)

        bA_v, bA_s, bA_t = P.val(vch), P.val(sch), P.val(t2v)

        for _e in ("sync", "gpsimd", "scalar", "tensor"):
            W(_e, vch, bA_v)
            W(_e, sch, bA_s)
            W(_e, t2v, bA_t)
        W("scalar", ld_c, P.val(ld_c))
        W("tensor", ld_c, sp_done[0])

        def xslot(i):
            return ap32(C_xb + (i % 2) * 1024, 1024)

        lm = {}
        lm[("ldx", 0)] = dma("sync", xslot(0), x_d[0:128, :], xsem(0))
        if NT > 1:
            lm[("ldx", 1)] = dma("sync", xslot(1), x_d[128:256, :], xsem(1))

        W("scalar", xsem(0), lm[("ldx", 0)])
        v = op("scalar", lambda s: s.activation(ap32(C_sq, 1024), xslot(0), AF.Square,
                                                accum_out=st[:, 1:2]), s_dn)
        lm[("sqx", 0)] = v
        W("vector", xsem(0), lm[("ldx", 0)])
        op("vector", lambda v_: v_.reduce_sum(st[:, 0:1], xslot(0), axis=AXX))
        W("vector", s_dn, lm[("sqx", 0)])
        v = ln_tiny(0)
        W("scalar", v2s, v)
        lm[("istd1", 0)] = ln_istd(0)
        W("vector", s_dn, lm[("istd1", 0)])
        vsp()
        v = op("vector", lambda v_: v_.tensor_scalar(ap32(C_xn, 1024), xslot(0), st[:, 2:3],
                                                     st[:, 7:8], ALU.subtract, ALU.mult), v2t)
        lm[("xn", 0)] = v
        W("tensor", v2t, lm[("xn", 0)])
        for c in range(8):
            v = op("tensor", lambda t, o=psA[:, c * 128:(c + 1) * 128],
                   s=ap32(C_xn + c * 128, 128): t.transpose(o, s, ident[:]),
                   t2v if c == 7 else None)
        lm[("tp", 0)] = v
        W("vector", t2v, lm[("tp", 0)])
        op("vector", lambda v_: v_.tensor_copy(ap16(A_xnTh, 512), psA[:, 0:512]))
        v = op("vector", lambda v_: v_.tensor_tensor(ap16(A_xnTl, 512), psA[:, 0:512],
                                                     ap16(A_xnTh, 512), ALU.subtract), v2t)
        lm[("xnT1", 0)] = v
        op("vector", lambda v_: v_.tensor_copy(ap16(A_xnTh + 512, 512), psA[:, 512:1024]))
        v = op("vector", lambda v_: v_.tensor_tensor(ap16(A_xnTl + 512, 512),
                                                     psA[:, 512:1024],
                                                     ap16(A_xnTh + 512, 512),
                                                     ALU.subtract), v2t)
        lm[("xnT", 0)] = v

        for i in range(NT):
            xb = xslot(i)
            hb = ap32(C_hb + (i % 2) * 1024, 1024)

            if 0 < i and i + 1 < NT:
                W("sync", v2t, lm[("h", i - 1)])
                lm[("ldx", i + 1)] = dma("sync", xslot(i + 1),
                                         x_d[(i + 1) * 128:(i + 2) * 128, :], xsem(i + 1))

            if i + 1 < NT:
                W("scalar", xsem(i + 1), lm[("ldx", i + 1)])
                if i > 0:
                    W("scalar", v2t, lm[("h", i - 1)])
                v = op("scalar", lambda s, j=i + 1: s.activation(ap32(C_sq, 1024), xslot(j),
                                                                 AF.Square,
                                                                 accum_out=st[:, 1:2]), s_dn)
                lm[("sqx", i + 1)] = v

            if i + 1 < NT:
                W("vector", xsem(i + 1), lm[("ldx", i + 1)])
                op("vector", lambda v_, j=i + 1: v_.reduce_sum(st[:, 0:1], xslot(j), axis=AXX))
                W("vector", s_dn, lm[("sqx", i + 1)])
                lm[("var1", i + 1)] = ln_tiny(0)
                W("scalar", v2s, lm[("var1", i + 1)])
                lm[("istd1", i + 1)] = ln_istd(0)
                W("vector", s_dn, lm[("istd1", i + 1)])
                v = op("vector", lambda v_, j=i + 1: v_.tensor_scalar(ap32(C_xn, 1024),
                                                                      xslot(j), st[:, 2:3],
                                                                      st[:, 7:8], ALU.subtract,
                                                                      ALU.mult), v2t)
                lm[("xn", i + 1)] = v

            if i > 0:
                W("tensor", s_dn, lm[("kvf", i - 1)])
            for s in range(2, 6):
                op("tensor", lambda t, o=psA[:, s * 512:(s + 1) * 512],
                   r=a16[0:2, X_b1h + s * 512: X_b1h + (s + 1) * 512]:
                   t.matmul(o, ones2b[:], r, start=True, stop=False))
            W("tensor", v2t, lm[("xnT1", i)])
            for c in range(4):
                for p in range(3):
                    lo = (A_xnTh, A_xnTh, A_xnTl)[p]
                    ro = (A_W1h, A_W1l, A_W1h)[p]
                    lhs = ap16(lo + c * 128, 128)
                    for s in range(2, 6):
                        rhs = ap16(ro + c * 4096 + s * 512, 512)
                        op("tensor", lambda t, o=psA[:, s * 512:(s + 1) * 512], l=lhs, r=rhs:
                           t.matmul(o, l, r, start=False, stop=False))
            W("tensor", v2t, lm[("xnT", i)])
            for sb_ in range(6, 8):
                op("tensor", lambda t, o=psB[:, (sb_ - 6) * 512:(sb_ - 5) * 512],
                   r=a16[0:2, X_b1h + sb_ * 512: X_b1h + (sb_ + 1) * 512]:
                   t.matmul(o, ones2b[:], r, start=True, stop=False))
            for c in range(4):
                for p in range(3):
                    lo = (A_xnTh, A_xnTh, A_xnTl)[p]
                    lhs = ap16(lo + c * 128, 128)
                    for s in range(6, 8):
                        rhs = ap16((A_W1h, A_W1l, A_W1h)[p] + c * 4096 + s * 512, 512)
                        op("tensor", lambda t, o=psB[:, (s - 6) * 512:(s - 5) * 512], l=lhs,
                           r=rhs: t.matmul(o, l, r, start=False, stop=False))
            for c in range(4, 8):
                for p in range(3):
                    lo = (A_xnTh, A_xnTh, A_xnTl)[p]
                    ro = (A_W1h, A_W1l, A_W1h)[p]
                    lhs = ap16(lo + c * 128, 128)
                    for s in range(2, 8):
                        rhs = ap16(ro + c * 4096 + s * 512, 512)
                        o = (psA[:, s * 512:(s + 1) * 512] if s < 6
                             else psB[:, (s - 6) * 512:(s - 5) * 512])
                        stp = (c == 7 and p == 2)
                        last = stp and s == 7
                        v = op("tensor", lambda t, o=o, l=lhs, r=rhs, b=stp:
                               t.matmul(o, l, r, start=False, stop=b), t2v if last else None)
            lm[("qkv1", i)] = v
            if i > 0:
                W("tensor", s_dn, lm[("q", i - 1)])
            for s in range(2):
                op("tensor", lambda t, o=psA[:, s * 512:(s + 1) * 512],
                   r=a16[0:2, X_b1h + s * 512: X_b1h + (s + 1) * 512]:
                   t.matmul(o, ones2b[:], r, start=True, stop=False))
            for c in range(8):
                for p in range(3):
                    lo = (A_xnTh, A_xnTh, A_xnTl)[p]
                    ro = (A_W1h, A_W1l, A_W1h)[p]
                    lhs = ap16(lo + c * 128, 128)
                    for s in range(2):
                        rhs = ap16(ro + c * 4096 + s * 512, 512)
                        stp = (c == 7 and p == 2)
                        last = stp and s == 1
                        v = op("tensor", lambda t, o=psA[:, s * 512:(s + 1) * 512], l=lhs,
                               r=rhs, b=stp: t.matmul(o, l, r, start=False, stop=b),
                               t2v if last else None)
            lm[("qkv", i)] = v

            W("scalar", t2v, lm[("qkv1", i)])
            v = op("scalar", lambda s: s.copy(ap32(C_ku, 1024), psA[:, 1024:2048]), s_dn)
            lm[("k", i)] = v
            v = op("scalar", lambda s: s.activation(ap32(C_g2, 1024), psB[:, 0:1024], AF.Exp,
                                                    scale=-1.0), s_dn)
            lm[("eg", i)] = v
            W("scalar", t2v, lm[("qkv", i)])
            v = op("scalar", lambda s: s.copy(ap32(C_q, 1024), psA[:, 0:1024]), s_dn)
            lm[("q", i)] = v

            W("vector", t2v, lm[("qkv1", i)])
            W("vector", s_dn, lm[("k", i)])
            if i > 0:
                W("vector", c_dma, 16 * i)
            v = op("vector", lambda v_: v_.tensor_tensor(ap32(C_kv, 1024), ap32(C_ku, 1024),
                                                         psA[:, 2048:3072], ALU.mult), v2t)
            lm[("u", i)] = v

            W("tensor", v2t, lm[("u", i)])
            W("tensor", s_dn, lm[("eg", i)])
            if i > 0:
                W("tensor", c_dma, 16 * i)
            for s in range(2):
                oo = psB[:, s * 512:(s + 1) * 512]
                op("tensor", lambda t, o=oo, s=s: t.matmul(o, uones32[:],
                                                           ap32(C_kv + s * 512, 512),
                                                           start=True, stop=False))
                v = op("tensor", lambda t, o=oo, s=s: t.matmul(o, ones1f[:],
                                                               carry[0:1, s * 512:(s + 1) * 512],
                                                               start=False, stop=True),
                       t2v if s == 1 else None)
            lm[("cs", i)] = v

            W("scalar", t2v, lm[("cs", i)])
            v = op("scalar", lambda s: s.copy(ap32(C_kv, 1024), psB[:, 0:1024]), s_dn)
            lm[("kvf", i)] = v

            W("gpsimd", s_dn, lm[("kvf", i)])
            dma("gpsimd", carry[:], ap32(C_kv, 1024, 127, 128), c_dma)

            if i + 1 < NT:
                W("tensor", v2t, lm[("xn", i + 1)])
                W("tensor", s_dn, lm[("q", i)])
                for c in range(8):
                    v = op("tensor", lambda t, o=psA[:, c * 128:(c + 1) * 128],
                           s=ap32(C_xn + c * 128, 128): t.transpose(o, s, ident[:]),
                           t2v if c in (3, 7) else None)
                    if c == 3:
                        lm[("tp1", i + 1)] = v
                lm[("tp", i + 1)] = v
                W("vector", t2v, lm[("tp1", i + 1)])
                op("vector", lambda v_: v_.tensor_copy(ap16(A_xnTh, 512), psA[:, 0:512]))
                v = op("vector", lambda v_: v_.tensor_tensor(ap16(A_xnTl, 512), psA[:, 0:512],
                                                             ap16(A_xnTh, 512),
                                                             ALU.subtract), v2t)
                lm[("xnT1", i + 1)] = v
                W("vector", t2v, lm[("tp", i + 1)])
                op("vector", lambda v_: v_.tensor_copy(ap16(A_xnTh + 512, 512),
                                                       psA[:, 512:1024]))
                v = op("vector", lambda v_: v_.tensor_tensor(ap16(A_xnTl + 512, 512),
                                                             psA[:, 512:1024],
                                                             ap16(A_xnTh + 512, 512),
                                                             ALU.subtract), v2t)
                lm[("xnT", i + 1)] = v

            W("vector", s_dn, lm[("q", i)])
            op("vector", lambda v_: v_.tensor_scalar(ap32(C_g, 1024), ap32(C_g2, 1024), 1.0,
                                                     None, ALU.add))
            op("vector", lambda v_: v_.reciprocal(ap32(C_g2, 1024), ap32(C_g, 1024)))
            rsp()
            rsp()
            if i + 1 < NT:
                W("vector", s_dn, lm[("sqx", i + 1)])
            W("vector", s_dn, lm[("kvf", i)])
            op("vector", lambda v_: v_.tensor_tensor(ap32(C_sq, 1024), ap32(C_q, 1024),
                                                     ap32(C_kv, 1024), ALU.mult))
            v = op("vector", lambda v_: v_.tensor_tensor(ap32(C_g, 1024), ap32(C_sq, 1024),
                                                         ap32(C_g2, 1024), ALU.mult), v2s)
            lm[("a", i)] = v
            W("scalar", v2s, lm[("a", i)])
            v = op("scalar", lambda s: s.activation(ap32(C_sq, 1024), ap32(C_g, 1024),
                                                    AF.Square, accum_out=st[:, 9:10]), s_dn)
            lm[("sqa", i)] = v
            op("vector", lambda v_: v_.reduce_sum(st[:, 8:9], ap32(C_g, 1024), axis=AXX))
            W("vector", s_dn, lm[("sqa", i)])
            lm[("var2", i)] = ln_tiny(8)
            W("scalar", v2s, lm[("var2", i)])
            lm[("istd2", i)] = ln_istd(8)
            W("vector", s_dn, lm[("istd2", i)])
            vsp()
            if i >= 2:
                W("vector", shsem(i - 2), lm[("sth", i - 2)])
            op("vector", lambda v_: v_.tensor_scalar(ap32(C_sq, 1024), ap32(C_g, 1024),
                                                     st[:, 10:11], st[:, 15:16],
                                                     ALU.subtract, ALU.mult))
            v = op("vector", lambda v_, x=xb, h=hb: v_.tensor_tensor(h, x, ap32(C_sq, 1024),
                                                                     ALU.add), v2t)
            lm[("h", i)] = v

            W("gpsimd", v2t, lm[("h", i)])
            lm[("sth", i)] = dma("gpsimd", h_d[i * 128:(i + 1) * 128, :], hb, shsem(i))

            op("vector", lambda v_, h=hb, j=i: v_.reduce_sum(lnst[:, 2 * j:2 * j + 1], h,
                                                             axis=AXX))
            W("scalar", v2t, lm[("h", i)])
            v = op("scalar", lambda s, h=hb, j=i: s.activation(ap32(C_sq, 1024), h, AF.Square,
                                                               accum_out=lnst[:, 2 * j + 1:
                                                                              2 * j + 2]),
                   s_dn)
            lm[("sqh", i)] = v

        bB_v, bB_s, bB_t = P.val(vch), P.val(sch), P.val(t2v)

        def hslot(i):
            return ap32(C_hb2 + (i % 3) * 1024, 1024)

        def hnslot(i):
            return ap32(C_hn2 + (i % 3) * 1024, 1024)

        W("sync", vch, lm[("h", NT - 2)])
        W("sync", shsem(0), lm[("sth", 0)])
        lm[("ldh", 0)] = dma("sync", hslot(0), h_d[0:128, :], hsem(0))
        W("sync", t2v, lm[("qkv", NT - 1)])
        W("sync", spe, 32)
        dma("sync", ap16(B_tkh, 9600), sptk_d, spl)
        rl_a = dma("sync", ap16(B_wgh, 96), spwg_d, spl)
        dma("sync", ap16(B_wdh, 16384), spwd_d, spl2)
        rl_b = dma("sync", ap16(B_vwh, 10240), spvw_d, spl2)

        for _e in ("sync", "gpsimd", "scalar", "tensor"):
            W(_e, vch, bB_v)
            W(_e, sch, bB_s)
            W(_e, t2v, bB_t)
        for _e in ("vector", "scalar", "sync", "gpsimd"):
            W(_e, c_dma, P.val(c_dma))
            W(_e, ld_x, P.val(ld_x))
            W(_e, ld_x2, P.val(ld_x2))
            W(_e, st_h, P.val(st_h))
            W(_e, st_h2, P.val(st_h2))

        if NT > 1:
            W("sync", shsem(1), lm[("sth", 1)])
            lm[("ldh", 1)] = dma("sync", hslot(1), h_d[128:256, :], hsem(1))
        op("vector", lambda v_: v_.tensor_copy(st[:, 0:2], lnst[:, 0:2]))
        v = ln_tiny(0)
        W("scalar", v2s, v)
        lm[("istd3", 0)] = ln_istd(0)
        W("vector", s_dn, lm[("istd3", 0)])
        vsp()
        W("vector", hsem(0), lm[("ldh", 0)])
        v = op("vector", lambda v_: v_.tensor_scalar(hnslot(0), hslot(0), st[:, 2:3],
                                                     st[:, 7:8], ALU.subtract, ALU.mult), v2t)
        lm[("hn", 0)] = v
        W("tensor", v2t, lm[("hn", 0)])
        for c in range(8):
            v = op("tensor", lambda t, o=psA[:, 1536 + c * 128:1664 + c * 128],
                   s=ap32(C_hn2 + c * 128, 128): t.transpose(o, s, ident[:]),
                   t2v if c == 7 else None)
        lm[("tpB", 0)] = v
        W("vector", t2v, lm[("tpB", 0)])
        op("vector", lambda v_: v_.tensor_copy(ap16(B_hnTh, 512), psA[:, 1536:2048]))
        v = op("vector", lambda v_: v_.tensor_tensor(ap16(B_hnTl, 512), psA[:, 1536:2048],
                                                     ap16(B_hnTh, 512), ALU.subtract), v2t)
        lm[("hnT1", 0)] = v
        op("vector", lambda v_: v_.tensor_copy(ap16(B_hnTh + 512, 512), psA[:, 2048:2560]))
        v = op("vector", lambda v_: v_.tensor_tensor(ap16(B_hnTl + 512, 512),
                                                     psA[:, 2048:2560],
                                                     ap16(B_hnTh + 512, 512),
                                                     ALU.subtract), v2t)
        lm[("hnT", 0)] = v

        for i in range(NT):
            hb = hslot(i)

            if i >= 1 and i + 1 < NT:
                if i >= 2:
                    W("sync", sosem(i - 2), lm[("sto", i - 2)])
                W("sync", vch, lm[("hn", i - 1)])
                W("sync", shsem(i + 1), lm[("sth", i + 1)])
                lm[("ldh", i + 1)] = dma("sync", hslot(i + 1),
                                         h_d[(i + 1) * 128:(i + 2) * 128, :], hsem(i + 1))

            if i == 0:
                W("tensor", spl, rl_a)
            W("tensor", v2t, lm[("hnT", i)])
            for c in range(8):
                for p in range(3):
                    lo = (B_hnTh, B_hnTh, B_hnTl)[p]
                    lhs = ap16(lo + c * 128, 128)
                    stt = (c == 0 and p == 0)
                    stp = (c == 7 and p == 2)
                    rhs = ap16((B_wgh, B_wgl, B_wgh)[p] + c * 6, 6)
                    op("tensor", lambda t, o=psA[:, 1024:1030], l=lhs, r=rhs, a=stt, b=stp:
                       t.matmul(o, l, r, start=a, stop=b))
                    for s0, s1 in ((0, 512), (512, EL)):
                        rhs = ap16((B_tkh, B_tkl, B_tkh)[p] + c * EL + s0, s1 - s0)
                        last = stp and s1 == EL
                        v = op("tensor", lambda t, o=psA[:, 1536 + s0:1536 + s1], l=lhs, r=rhs,
                               a=stt, b=stp: t.matmul(o, l, r, start=a, stop=b),
                               t2v if last else None)
            lm[("sc", i)] = v

            if i > 0:
                W("scalar", t2v, lm[("dl", i - 1)])
                if i > 1:
                    W("scalar", v2t, lm[("ot", i - 2)])
                v = op("scalar", lambda s: s.activation(ap32(C_s2, 1024), psA[:, 0:1024],
                                                        AF.Copy, scale=scv[:]), s_dn)
                lm[("s2", i - 1)] = v

            if i + 1 < NT:
                op("vector", lambda v_, j=i + 1: v_.tensor_copy(st[:, 0:2],
                                                                lnst[:, 2 * j:2 * j + 2]))
                lm[("var3", i + 1)] = ln_tiny(0)
                W("scalar", v2s, lm[("var3", i + 1)])
                lm[("istd3", i + 1)] = ln_istd(0)

            W("vector", t2v, lm[("sc", i)])
            op("vector", lambda v_: v_.tensor_copy(st[:, 32:38], psA[:, 1024:1030]))
            for e in range(NEXP):
                op("vector", lambda v_, e=e: v_.reduce_max(st[:, 54 + e:55 + e],
                                                           psA[:, 1536 + e * LT:
                                                               1536 + (e + 1) * LT],
                                                           axis=AXX))
            op("vector", lambda v_: v_.reduce_max(st[:, 38:39], st[:, 32:38], axis=AXX))
            vsp()
            op("vector", lambda v_: v_.tensor_scalar(st[:, 60:66], st[:, 54:60], -ISC, 1000.0,
                                                     ALU.mult, ALU.subtract))
            vsp()
            op("vector", lambda v_: v_.tensor_scalar(st[:, 40:46], st[:, 32:38], st[:, 38:39],
                                                     1000.0, ALU.is_equal, ALU.mult))
            vsp()
            vsp()
            op("vector", lambda v_: v_.tensor_tensor(st[:, 48:54], st[:, 40:46], iota[:],
                                                     ALU.subtract))
            vsp()
            vsp()
            op("vector", lambda v_: v_.reduce_max(st[:, 46:47], st[:, 48:54], axis=AXX))
            vsp()
            vsp()
            op("vector", lambda v_: v_.tensor_scalar(st[:, 84:90], st[:, 48:54], st[:, 46:47],
                                                     1000.0, ALU.is_equal, ALU.mult))
            rsp()
            rsp()
            op("vector", lambda v_: v_.tensor_tensor(st[:, 54:60], st[:, 60:66], st[:, 84:90],
                                                     ALU.add))
            rsp()
            rsp()
            lm[("am", i)] = vsp(v2s)

            if i + 1 < NT:
                W("vector", s_dn, lm[("istd3", i + 1)])
                W("vector", hsem(i + 1), lm[("ldh", i + 1)])
                v = op("vector", lambda v_, j=i + 1: v_.tensor_scalar(
                    hnslot(j), hslot(j), st[:, 2:3], st[:, 7:8], ALU.subtract, ALU.mult), v2t)
                lm[("hn", i + 1)] = v

            if i == 0:
                W("tensor", spl2, rl_b)
            if i > 0:
                W("tensor", s_dn, lm[("s2", i - 1)])
            for c in range(8):
                for p in range(3):
                    lo = (B_hnTh, B_hnTh, B_hnTl)[p]
                    lhs = ap16(lo + c * 128, 128)
                    stt = (c == 0 and p == 0)
                    stp = (c == 7 and p == 2)
                    for s in range(2):
                        rhs = ap16((B_wdh, B_wdl, B_wdh)[p] + c * 1024 + s * 512, 512)
                        last = stp and s == 1
                        v = op("tensor", lambda t, o=psA[:, s * 512:(s + 1) * 512], l=lhs,
                               r=rhs, a=stt: t.matmul(o, l, r, start=a, stop=False),
                               t2v if last else None)
            lm[("xw", i)] = v

            if i > 0:
                v = op("scalar", lambda s: s.activation(ap32(C_s3, 1024), ap32(C_s2, 1024),
                                                        AF.Exp), s_dn)
                lm[("expd", i - 1)] = v
            W("scalar", v2s, lm[("am", i)])
            for e in range(NEXP):
                v = op("scalar", lambda s, e=e: s.activation(ap32(C_exps + e * LT, LT),
                                                             psA[:, 1536 + e * LT:
                                                                 1536 + (e + 1) * LT],
                                                             AF.Exp,
                                                             bias=st[:, 54 + e:55 + e],
                                                             scale=ISC),
                       s_dn if e == NEXP - 1 else None)
            lm[("exp", i)] = v

            if i > 0:
                W("vector", s_dn, lm[("expd", i - 1)])
                op("vector", lambda v_: v_.tensor_scalar(ap32(C_s1, 1024), ap32(C_s3, 1024),
                                                         0.0, FEPS, ALU.is_equal, ALU.mult))
                v = op("vector", lambda v_: v_.tensor_tensor(ap32(C_s2, 1024),
                                                             ap32(C_s3, 1024),
                                                             ap32(C_s1, 1024), ALU.add), v2s)
                lm[("eadj", i - 1)] = v

            op("scalar", lambda s: s.activation(ap32(C_mask + 32, 600), ap32(C_exps, 600),
                                                AF.Copy, accum_out=ap32(C_mask + 8, 1)))
            op("scalar", lambda s: s.activation(ap32(C_mask + 16, 1), ap32(C_mask + 8, 1),
                                                AF.Ln))
            op("scalar", lambda s: s.activation(ap32(C_mask + 24, 1), ap32(C_mask + 16, 1),
                                                AF.Exp, scale=-1.0))
            v = op("scalar", lambda s: s.activation(ap32(C_mask + 32, 600), ap32(C_exps, 600),
                                                    AF.Copy, scale=ap32(C_mask + 24, 1)),
                   s_dn)
            lm[("mask", i)] = v

            if i > 0:
                W("tensor", s_dn, lm[("maTh", i - 1)])
                W("tensor", v2t, lm[("maT", i - 1)])
            W("tensor", s_dn, lm[("mask", i)])
            for k in range(5):
                kc = ELC[k]
                v = op("tensor", lambda t, o=psB[0:kc, k * 128:k * 128 + 128],
                       s=ap32(C_mask + 32 + k * 128, kc): t.transpose(o, s, ident[:]),
                       t2v if k == 4 else None)
            lm[("tpM", i)] = v

            W("scalar", t2v, lm[("tpM", i)])
            v = op("scalar", lambda s: s.copy(ap16(B_maTh, 640), psB[:, 0:640]), s_dn)
            lm[("maTh", i)] = v
            if i > 0:
                W("scalar", v2s, lm[("eadj", i - 1)])
                v = op("scalar", lambda s: s.activation(ap32(C_s3, 1024), ap32(C_s2, 1024),
                                                        AF.Ln), s_dn)
                lm[("lc", i - 1)] = v

            W("vector", t2v, lm[("tpM", i)])
            W("vector", s_dn, lm[("maTh", i)])
            v = op("vector", lambda v_: v_.tensor_tensor(ap16(B_maTl, 640), psB[:, 0:640],
                                                         ap16(B_maTh, 640), ALU.subtract), v2t)
            lm[("maT", i)] = v
            lm[("maT1", i)] = v

            if i + 1 < NT:
                W("tensor", v2t, lm[("hn", i + 1)])
                W("tensor", s_dn, lm[("exp", i)])
                for c in range(8):
                    v = op("tensor", lambda t, o=psA[:, 1536 + c * 128:1664 + c * 128],
                           s=ap32(C_hn2 + ((i + 1) % 3) * 1024 + c * 128, 128):
                           t.transpose(o, s, ident[:]),
                           t2v if c in (3, 7) else None)
                    if c == 3:
                        lm[("tpB1", i + 1)] = v
                lm[("tpB", i + 1)] = v
                W("vector", t2v, lm[("tpB1", i + 1)])
                op("vector", lambda v_: v_.tensor_copy(ap16(B_hnTh, 512),
                                                       psA[:, 1536:2048]))
                v = op("vector", lambda v_: v_.tensor_tensor(ap16(B_hnTl, 512),
                                                             psA[:, 1536:2048],
                                                             ap16(B_hnTh, 512),
                                                             ALU.subtract), v2t)
                lm[("hnT1", i + 1)] = v
                W("vector", t2v, lm[("tpB", i + 1)])
                op("vector", lambda v_: v_.tensor_copy(ap16(B_hnTh + 512, 512),
                                                       psA[:, 2048:2560]))
                v = op("vector", lambda v_: v_.tensor_tensor(ap16(B_hnTl + 512, 512),
                                                             psA[:, 2048:2560],
                                                             ap16(B_hnTh + 512, 512),
                                                             ALU.subtract), v2t)
                lm[("hnT", i + 1)] = v

            if i > 0:
                W("vector", s_dn, lm[("lc", i - 1)])
                v = op("vector", lambda v_, j=i - 1: v_.tensor_tensor(ap32(C_s1, 1024),
                                                                      ap32(C_s3, 1024),
                                                                      hnslot(j), ALU.add), v2s)
                lm[("fr", i - 1)] = v
                op("vector", lambda v_: v_.reduce_sum(st[:, 8:9], ap32(C_s1, 1024), axis=AXX))

            if i > 0:
                W("scalar", v2s, lm[("fr", i - 1)])
                v = op("scalar", lambda s: s.activation(ap32(C_s2, 1024), ap32(C_s1, 1024),
                                                        AF.Square, accum_out=st[:, 9:10]), s_dn)
                lm[("sqf", i - 1)] = v

            if i > 0:
                W("vector", s_dn, lm[("sqf", i - 1)])
                lm[("var4", i - 1)] = ln_tiny(8)
                W("scalar", v2s, lm[("var4", i - 1)])
                lm[("istd4", i - 1)] = ln_istd(8)

            W("tensor", s_dn, lm[("maTh", i)])
            for p in range(3):
                if p == 2:
                    W("tensor", v2t, lm[("maT", i)])
                lo = (B_maTh, B_maTh, B_maTl)[p]
                ro = (B_vwh, B_vwl, B_vwh)[p]
                for k in range(5):
                    kc = ELC[k]
                    lhs = ap16(lo + k * 128, 128, 0, kc)
                    for s in range(2):
                        rhs = ap16(ro + k * 1024 + s * 512, 512, 0, kc)
                        op("tensor", lambda t, o=psA[:, s * 512:(s + 1) * 512], l=lhs, r=rhs:
                           t.matmul(o, l, r, start=False, stop=False))
            for s in range(2):
                v = op("tensor", lambda t, o=psA[:, s * 512:(s + 1) * 512],
                       r=a16[0:2, X_wdbh + s * 512: X_wdbh + (s + 1) * 512]:
                       t.matmul(o, ones2b[:], r, start=False, stop=True),
                       t2v if s == 1 else None)
            lm[("dl", i)] = v

            if i > 0:
                W("vector", s_dn, lm[("istd4", i - 1)])
                op("vector", lambda v_: v_.tensor_scalar(ap32(C_s2, 1024), ap32(C_s1, 1024),
                                                         st[:, 10:11], st[:, 15:16],
                                                         ALU.subtract, ALU.mult))
                v = op("vector", lambda v_, j=i - 1:
                       v_.tensor_tensor(hslot(j), hslot(j), ap32(C_s2, 1024), ALU.add), v2t)
                lm[("ot", i - 1)] = v
                W("gpsimd", v2t, lm[("ot", i - 1)])
                if i >= 3:
                    W("gpsimd", sosem(i - 3), lm[("sto", i - 3)])
                lm[("sto", i - 1)] = dma("gpsimd", out_d[(i - 1) * 128:i * 128, :],
                                         hslot(i - 1), sosem(i - 1))

        i = NT
        W("scalar", t2v, lm[("dl", i - 1)])
        if i > 1:
            W("scalar", v2t, lm[("ot", i - 2)])
        v = op("scalar", lambda s: s.activation(ap32(C_s2, 1024), psA[:, 0:1024],
                                                AF.Copy, scale=scv[:]), s_dn)
        lm[("s2", i - 1)] = v
        v = op("scalar", lambda s: s.activation(ap32(C_s3, 1024), ap32(C_s2, 1024), AF.Exp),
               s_dn)
        lm[("expd", i - 1)] = v
        W("vector", s_dn, lm[("expd", i - 1)])
        op("vector", lambda v_: v_.tensor_scalar(ap32(C_s1, 1024), ap32(C_s3, 1024),
                                                 0.0, FEPS, ALU.is_equal, ALU.mult))
        v = op("vector", lambda v_: v_.tensor_tensor(ap32(C_s2, 1024), ap32(C_s3, 1024),
                                                     ap32(C_s1, 1024), ALU.add), v2s)
        lm[("eadj", i - 1)] = v
        W("scalar", v2s, lm[("eadj", i - 1)])
        v = op("scalar", lambda s: s.activation(ap32(C_s3, 1024), ap32(C_s2, 1024), AF.Ln),
               s_dn)
        lm[("lc", i - 1)] = v
        W("vector", s_dn, lm[("lc", i - 1)])
        v = op("vector", lambda v_, j=i - 1: v_.tensor_tensor(ap32(C_s1, 1024),
                                                              ap32(C_s3, 1024), hnslot(j),
                                                              ALU.add), v2s)
        lm[("fr", i - 1)] = v
        op("vector", lambda v_: v_.reduce_sum(st[:, 8:9], ap32(C_s1, 1024), axis=AXX))
        W("scalar", v2s, lm[("fr", i - 1)])
        v = op("scalar", lambda s: s.activation(ap32(C_s2, 1024), ap32(C_s1, 1024),
                                                AF.Square, accum_out=st[:, 9:10]), s_dn)
        lm[("sqf", i - 1)] = v
        W("vector", s_dn, lm[("sqf", i - 1)])
        lm[("var4", i - 1)] = ln_tiny(8)
        W("scalar", v2s, lm[("var4", i - 1)])
        lm[("istd4", i - 1)] = ln_istd(8)
        W("vector", s_dn, lm[("istd4", i - 1)])
        vsp()
        op("vector", lambda v_: v_.tensor_scalar(ap32(C_s2, 1024), ap32(C_s1, 1024),
                                                 st[:, 10:11], st[:, 15:16],
                                                 ALU.subtract, ALU.mult))
        v = op("vector", lambda v_, j=i - 1:
               v_.tensor_tensor(hslot(j), hslot(j), ap32(C_s2, 1024), ALU.add), v2t)
        lm[("ot", i - 1)] = v
        W("gpsimd", v2t, lm[("ot", i - 1)])
        if i >= 3:
            W("gpsimd", sosem(i - 3), lm[("sto", i - 3)])
        lm[("sto", i - 1)] = dma("gpsimd", out_d[(i - 1) * 128:i * 128, :],
                                 hslot(i - 1), sosem(i - 1))


        W("gpsimd", st_o, P.val(st_o))
        W("gpsimd", st_o2, P.val(st_o2))

        @block.sync
        def _(e):
            for f in P.ops["sync"]:
                f(e)

        @block.vector
        def _(e):
            for f in P.ops["vector"]:
                f(e)

        @block.scalar
        def _(e):
            for f in P.ops["scalar"]:
                f(e)

        @block.tensor
        def _(e):
            for f in P.ops["tensor"]:
                f(e)

        @block.gpsimd
        def _(e):
            for f in P.ops["gpsimd"]:
                f(e)

    return nc


def _prep(inputs):
    f32 = np.float32
    Wq = np.asarray(inputs["L0_attn_W_qkv"], f32)
    Wg = np.asarray(inputs["L0_attn_W_gate"], f32)
    w1 = np.ascontiguousarray(np.concatenate([Wq, Wg], axis=0).T)
    b1 = np.ascontiguousarray(np.concatenate([np.asarray(inputs["L0_attn_b_qkv"], f32),
                                              np.asarray(inputs["L0_attn_b_gate"], f32)]))
    at = np.ascontiguousarray(np.transpose(np.asarray(inputs["L0_ffn_A"], f32), (0, 2, 1)))
    bm = np.ascontiguousarray(np.asarray(inputs["L0_ffn_B"], f32))
    wg = np.ascontiguousarray(np.asarray(inputs["L0_ffn_w_gate"], f32))
    wt = np.ascontiguousarray(np.asarray(inputs["L0_ffn_wt_w"], f32).T)
    wd = np.ascontiguousarray(np.asarray(inputs["L0_ffn_wd_w"], f32).T)
    return {
        "w1": w1, "b1": b1, "at": at, "bm": bm, "wg": wg, "wt": wt,
        "wtb": np.ascontiguousarray(np.asarray(inputs["L0_ffn_wt_b"], f32)),
        "wd": wd,
        "wdb": np.ascontiguousarray(np.asarray(inputs["L0_ffn_wd_b"], f32)),
        "sc": np.ascontiguousarray(np.asarray(inputs["L0_ffn_scale"], f32)),
        "ident": np.eye(128, dtype=f32),
        "uones": np.triu(np.ones((128, 128), dtype=f32)).astype(ml_dtypes.bfloat16),
        "uones32": np.triu(np.ones((128, 128), dtype=f32)),
        "iota6": np.tile(np.arange(6, dtype=f32), (128, 1)),
        "ones1f": np.ones((1, 128), dtype=f32),
        "onesb": np.ones((1, 128), dtype=ml_dtypes.bfloat16),
        "ones2b": np.ones((2, 128), dtype=ml_dtypes.bfloat16),
        "cvals": np.tile(np.array([1e-5, -0.5, -1.0, 1024 ** -0.5, -(1024 ** -0.5),
                                   1000.0, np.finfo(np.float32).eps, 0.0], dtype=f32),
                         (128, 1)),
    }


def kernel(**inputs):
    x = np.asarray(inputs["L0_attn_x"], np.float32)
    base = _prep(inputs)
    S = x.shape[1]
    nc = build(S)
    in_maps = []
    for i in range(x.shape[0]):
        m = dict(base)
        m["x"] = np.ascontiguousarray(x[i])
        in_maps.append(m)
    res = run_bass_kernel_spmd(nc, in_maps, core_ids=list(range(x.shape[0])))
    return np.stack([np.asarray(res.results[i]["out"]) for i in range(x.shape[0])], axis=0)


# version 39
# speedup vs baseline: 1.0868x; 1.0006x over previous
import numpy as np
import ml_dtypes
import concourse.bass as bass
import concourse.mybir as mybir
from concourse.bass_utils import run_bass_kernel_spmd

F32 = mybir.dt.float32
BF16 = mybir.dt.bfloat16
AF = mybir.ActivationFunctionType
ALU = mybir.AluOpType
AXX = mybir.AxisListType.X

D = 1024
W1N = 4096
EL = 600
ELC = [128, 128, 128, 128, 88]
NEXP = 6
LT = 100
LN_EPS = 1e-5
FEPS = float(np.finfo(np.float32).eps)
ISC = float(1024 ** -0.5)

ENGINES = ("sync", "vector", "scalar", "tensor", "gpsimd")


class Plan:
    def __init__(self):
        self.ops = {e: [] for e in ENGINES}
        self.cnt = {}

    def inc(self, sem, by=1):
        k = id(sem)
        self.cnt[k] = self.cnt.get(k, 0) + by
        return self.cnt[k]

    def val(self, sem):
        return self.cnt.get(id(sem), 0)


def build(S, dbg=False):
    NT = S // 128
    nc = bass.Bass("TRN2", target_bir_lowering=False)

    x_d = nc.dram_tensor("x", [S, D], F32, kind="ExternalInput").ap()
    w1_d = nc.dram_tensor("w1", [D, W1N], F32, kind="ExternalInput").ap()
    b1_d = nc.dram_tensor("b1", [W1N], F32, kind="ExternalInput").ap()
    at_d = nc.dram_tensor("at", [NEXP, 16, LT], F32, kind="ExternalInput").ap()
    bm_d = nc.dram_tensor("bm", [NEXP, 16, D], F32, kind="ExternalInput").ap()
    wg_d = nc.dram_tensor("wg", [D, NEXP], F32, kind="ExternalInput").ap()
    wt_d = nc.dram_tensor("wt", [D, D], F32, kind="ExternalInput").ap()
    wtb_d = nc.dram_tensor("wtb", [D], F32, kind="ExternalInput").ap()
    wd_d = nc.dram_tensor("wd", [D, D], F32, kind="ExternalInput").ap()
    wdb_d = nc.dram_tensor("wdb", [D], F32, kind="ExternalInput").ap()
    sc_d = nc.dram_tensor("sc", [1], F32, kind="ExternalInput").ap()
    id_d = nc.dram_tensor("ident", [128, 128], F32, kind="ExternalInput").ap()
    uo_d = nc.dram_tensor("uones", [128, 128], BF16, kind="ExternalInput").ap()
    uo32_d = nc.dram_tensor("uones32", [128, 128], F32, kind="ExternalInput").ap()
    io_d = nc.dram_tensor("iota6", [128, NEXP], F32, kind="ExternalInput").ap()
    o1_d = nc.dram_tensor("ones1f", [1, 128], F32, kind="ExternalInput").ap()
    ob_d = nc.dram_tensor("onesb", [1, 128], BF16, kind="ExternalInput").ap()
    o2_d = nc.dram_tensor("ones2b", [2, 128], BF16, kind="ExternalInput").ap()
    cv_d = nc.dram_tensor("cvals", [128, 8], F32, kind="ExternalInput").ap()
    out_d = nc.dram_tensor("out", [S, D], F32, kind="ExternalOutput").ap()
    h_d = nc.dram_tensor("hscr", [S, D], F32,
                         kind="ExternalOutput" if dbg else "Internal").ap()
    spwd_d = nc.dram_tensor("spwd", [128, 16384], BF16, kind="Internal").ap()
    sptk_d = nc.dram_tensor("sptk", [128, 9600], BF16, kind="Internal").ap()
    spvw_d = nc.dram_tensor("spvw", [128, 10240], BF16, kind="Internal").ap()
    spwg_d = nc.dram_tensor("spwg", [128, 96], BF16, kind="Internal").ap()

    from contextlib import ExitStack
    with ExitStack() as _es:
        def _sb(*a):
            return _es.enter_context(nc.sbuf_tensor(*a))

        def _ps(*a):
            return _es.enter_context(nc.psum_tensor(*a))

        def _sem(n):
            return _es.enter_context(nc.semaphore(n))

        a16 = _sb("a16", [128, 79872], BF16)
        a32 = _sb("a32", [128, 11264], F32)
        st = _sb("stats", [128, 96], F32)
        wtbs = _sb("wtbs", [128, 8], F32)
        iota = _sb("iota_sb", [128, NEXP], F32)
        scin = _sb("scin", [1, 1], F32)
        scv = _sb("scv", [128, 1], F32)
        ident = _sb("ident_sb", [128, 128], F32)
        uones = _sb("uones_sb", [128, 128], BF16)
        uones32 = _sb("uones32_sb", [128, 128], F32)
        ones1f = _sb("ones1f_sb", [1, 128], F32)
        onesb = _sb("onesb_sb", [1, 128], BF16)
        ones2b = _sb("ones2b_sb", [2, 128], BF16)
        lnst = _sb("lnstats", [128, 96], F32)
        carry = _sb("carry", [1, D], F32)
        cst = _sb("cvals_sb", [128, 8], F32)
        for _k, _v in enumerate([LN_EPS, -0.5, -1.0, ISC, -ISC, 1000.0, FEPS, 0.0]):
            nc.const_aps.register(F32, _v, cst[:, _k:_k + 1])
        psA = _ps("psA", [128, 3072], F32)
        psB = _ps("psB", [128, 1024], F32)
        ld_c = _sem("ld_c")
        t2v = _sem("t2v")
        vch = _sem("vch")
        sch = _sem("sch")
        v2t = vch
        v2s = vch
        s_dn = sch
        ld_x = _sem("ld_x")
        c_dma = _sem("c_dma")
        st_h = _sem("st_h")
        ld_h = _sem("ld_h")
        st_o = _sem("st_o")
        ld_x2 = _sem("ld_x2")
        ld_h2 = _sem("ld_h2")
        st_h2 = _sem("st_h2")
        st_o2 = _sem("st_o2")
        spl = _sem("spl")
        spl2 = _sem("spl2")
        spe = _sem("spe")
        spf = _sem("spf")
        spf2 = _sem("spf2")
        sp1 = _sem("sp1")
        block = _es.enter_context(nc.Block())
        P = Plan()

        A_wth, A_wtl = 0, 8192
        A_wdh0 = 16384
        A_tkh0 = 32768
        A_tkl0 = 32768 + 4800
        A_vth, A_vtl = 42368, 47168
        A_vwh0 = 51968
        A_vwl0 = 51968 + 5120
        A_wgh0 = 62208
        A_wgl0 = 62208 + 48
        A_W1h, A_W1l = 0, 32768
        A_xnTh, A_xnTl = 65536, 66560
        A_uh, A_ul = 67584, 68608
        B_wdh, B_wdl = 0, 8192
        B_tkh, B_tkl = 16384, 21184
        B_vwh, B_vwl = 25984, 31104
        B_wgh, B_wgl = 36224, 36272
        B_hnTh, B_hnTl = 36352, 37376
        B_maTh, B_maTl = 38400, 39040
        X_b1h, X_b1l = 69632, 73728
        X_wdbh, X_wdbl = 77824, 78848

        C_xb, C_xn, C_ku, C_q, C_g, C_g2, C_kv, C_hb = \
            0, 2048, 3072, 4096, 5120, 6144, 7168, 8192
        C_sq = 10240
        C_hb2, C_hn2, C_exps, C_mask, C_s1, C_s2 = \
            0, 3072, 6144, 6784, 7424, 8448
        C_s3 = 9472
        C_stage = 0
        C_vts = 8192
        C_wgs = 8792

        def ap16(off, n, p0=0, p1=128):
            return a16[p0:p1, off:off + n]

        def ap32(off, n, p0=0, p1=128):
            return a32[p0:p1, off:off + n]

        def W(eng, sem, val):
            if val > 0:
                P.ops[eng].append(lambda e, s=sem, v=val: e.wait_ge(s, v))

        def dma(eng, out, in_, sem, slow=False):
            v = P.inc(sem, 16)
            P.ops[eng].append(lambda e, o=out, i=in_, s=sem, sl=slow:
                              e.dma_start(out=o, in_=i,
                                          allow_slow_non_contiguous=sl).then_inc(s, 16))
            return v

        def op(eng, fn, sem=None):
            if eng in ("vector", "scalar"):
                ch = vch if eng == "vector" else sch
                prev = P.val(ch)
                if prev > 0:
                    P.ops[eng].append(lambda e, s=ch, v=prev: e.wait_ge(s, v))
                v = P.inc(ch, 1)
                P.ops[eng].append(lambda e, f=fn, s=ch: f(e).then_inc(s, 1))
                return v
            if sem is None:
                P.ops[eng].append(fn)
                return None
            v = P.inc(sem, 1)
            P.ops[eng].append(lambda e, f=fn, s=sem: f(e).then_inc(s, 1))
            return v

        def vsp(sem=None):
            return P.val(vch)

        def rsp():
            return op("vector", lambda v_: v_.tensor_copy(st[:, 94:95], st[:, 88:89]))

        def xsem(i):
            return ld_x if i % 2 == 0 else ld_x2

        def hsem(i):
            return ld_h if i % 2 == 0 else ld_h2

        def shsem(i):
            return st_h if i % 2 == 0 else st_h2

        def sosem(i):
            return st_o if i % 2 == 0 else st_o2

        def nop(eng, sem):
            v = P.inc(sem, 1)
            P.ops[eng].append(lambda e, s=sem: e.nop().then_inc(s, 1))
            return v

        def ln_tiny(o):
            vsp()
            vsp()
            op("vector", lambda v_: v_.tensor_scalar(st[:, o + 2:o + 3], st[:, o:o + 1],
                                                     1.0 / D, None, ALU.mult))
            op("vector", lambda v_: v_.tensor_scalar(st[:, o + 4:o + 5], st[:, o + 1:o + 2],
                                                     1.0 / D, None, ALU.mult))
            vsp()
            op("vector", lambda v_: v_.tensor_tensor(st[:, o + 3:o + 4], st[:, o + 2:o + 3],
                                                     st[:, o + 2:o + 3], ALU.mult))
            vsp()
            vsp()
            op("vector", lambda v_: v_.tensor_tensor(st[:, o + 5:o + 6], st[:, o + 4:o + 5],
                                                     st[:, o + 3:o + 4], ALU.subtract))
            vsp()
            return vsp(v2s)

        def ln_istd(o):
            op("scalar", lambda s, o=o: s.activation(st[:, o + 6:o + 7], st[:, o + 5:o + 6],
                                                     AF.Ln, bias=LN_EPS))
            return op("scalar", lambda s, o=o: s.activation(st[:, o + 7:o + 8],
                                                            st[:, o + 6:o + 7],
                                                            AF.Exp, scale=-0.5), s_dn)

        dma("sync", ident[:], id_d, ld_c)
        dma("sync", uones[:], uo_d, ld_c)
        dma("sync", uones32[:], uo32_d, ld_c)
        dma("sync", iota[:], io_d, ld_c)
        dma("sync", ones1f[:], o1_d, ld_c)
        dma("sync", onesb[:], ob_d, ld_c)
        dma("sync", ones2b[:], o2_d, ld_c)
        dma("sync", cst[:], cv_d, ld_c)
        dma("sync", scin[:], sc_d.rearrange("(a n) -> a n", a=1), ld_c)
        dma("sync", ap32(C_wgs, 48).rearrange("p (c e) -> p c e", e=6),
            wg_d.rearrange("(c p) e -> p c e", p=128), ld_c)
        dma("sync", a32[0:1, 0:4096], b1_d.rearrange("(a n) -> a n", a=1), ld_c)
        dma("sync", wtbs[:], wtb_d.rearrange("(c p) -> p c", p=128), ld_c, slow=True)
        v_bias = dma("sync", a32[0:1, 4096:5120], wdb_d.rearrange("(a n) -> a n", a=1), ld_c)

        op("vector", lambda v_: v_.memset(st[:, 84:96], 0.0))
        W("vector", ld_c, v_bias)
        op("vector", lambda v_: v_.tensor_copy(ap16(X_b1h, 4096, 0, 1), a32[0:1, 0:4096]))
        op("vector", lambda v_: v_.tensor_tensor(ap16(X_b1l, 4096, 0, 1), a32[0:1, 0:4096],
                                                 ap16(X_b1h, 4096, 0, 1), ALU.subtract))
        op("vector", lambda v_: v_.tensor_copy(ap16(X_wdbh, 1024, 0, 1), a32[0:1, 4096:5120]))
        vb1 = op("vector", lambda v_: v_.tensor_tensor(ap16(X_wdbl, 1024, 0, 1),
                                                       a32[0:1, 4096:5120],
                                                       ap16(X_wdbh, 1024, 0, 1),
                                                       ALU.subtract), v2t)

        W("sync", v2t, vb1)
        dma("sync", a16[1:2, X_b1h:X_b1h + 4096], ap16(X_b1l, 4096, 0, 1), ld_c)
        v_brow = dma("sync", a16[1:2, X_wdbh:X_wdbh + 1024], ap16(X_wdbl, 1024, 0, 1), ld_c)
        dma("sync", a32[0:16, 6144:6744].rearrange("r (e l) -> r e l", l=LT),
            at_d.rearrange("e r l -> r e l"), ld_c)
        v_ld = dma("sync", a32[0:16, 0:6144].rearrange("r (e c) -> r e c", c=D),
                   bm_d.rearrange("e r c -> r e c"), ld_c)

        _wt_slot_last = [0, 0]

        def wt_chunk(c):
            slot = 8840 + (c % 2) * 1024
            if _wt_slot_last[c % 2] > 0:
                W("sync", vch, _wt_slot_last[c % 2])
            sm = spl if c % 2 == 0 else spl2
            vd = dma("sync", ap32(slot, 1024), wt_d[c * 128:(c + 1) * 128, :], sm)
            W("vector", sm, vd)
            op("vector", lambda v_, c=c, sl=slot:
               v_.tensor_copy(ap16(A_wth + c * 1024, 1024), ap32(sl, 1024)))
            vv = op("vector", lambda v_, c=c, sl=slot:
                    v_.tensor_tensor(ap16(A_wtl + c * 1024, 1024), ap32(sl, 1024),
                                     ap16(A_wth + c * 1024, 1024), ALU.subtract), v2t)
            _wt_slot_last[c % 2] = vv
            return vv

        W("tensor", ld_c, v_ld)
        tok_mm_done = None
        _tk_evac = [0, 0]
        for c in range(8):
            pso = psB if c % 2 == 0 else psA
            if _tk_evac[c % 2] > 0:
                W("tensor", v2t, _tk_evac[c % 2])
            for e in range(NEXP):
                lhs = a32[0:16, e * 1024 + c * 128: e * 1024 + (c + 1) * 128]
                rhs = a32[0:16, 6144 + e * LT: 6144 + (e + 1) * LT]
                o = pso[:, e * 128: e * 128 + LT]
                last = (e == NEXP - 1)
                v = op("tensor", lambda t, o=o, l=lhs, r=rhs: t.matmul(o, l, r, start=True,
                                                                      stop=True),
                       t2v if last else None)
            tok_mm_done = v
            W("vector", t2v, v)
            src = pso[:, 0:768].rearrange("p (e x) -> p e x", x=128)[:, :, 0:LT]
            dsth = ap16(A_tkh0 + c * EL, EL).rearrange("p (e l) -> p e l", l=LT)
            dstl = ap16(A_tkl0 + c * EL, EL).rearrange("p (e l) -> p e l", l=LT)
            op("vector", lambda v_, d=dsth, s=src: v_.tensor_copy(d, s))
            v = op("vector", lambda v_, d=dstl, s=src, hh=dsth:
                   v_.tensor_tensor(d, s, hh, ALU.subtract), v2t)
            _tk_evac[c % 2] = v
            wt_chunk(c)
        tok_evac_done = P.val(v2t)
        wt_ready = tok_evac_done

        W("tensor", v2t, tok_evac_done)
        W("tensor", ld_c, v_ld)
        v = op("tensor", lambda t: t.matmul(psB[:, 0:1], ones1f[:], scin[:], start=True,
                                            stop=True), t2v)
        W("vector", t2v, v)
        op("vector", lambda v_: v_.tensor_copy(scv[:], psB[:, 0:1]))
        op("vector", lambda v_: v_.tensor_copy(ap16(A_wgh0, 48), ap32(C_wgs, 48)))
        op("vector", lambda v_: v_.tensor_tensor(ap16(A_wgl0, 48), ap32(C_wgs, 48),
                                                 ap16(A_wgh0, 48), ALU.subtract), v2t)
        spill_ready = P.val(v2t)

        _slot_last = [0, 0]

        def split_weight(dram_ap, ncols, offh, offl, pre_wait=0):
            splits = []
            for c in range(8):
                slot = C_stage + (c % 2) * 4096
                if _slot_last[c % 2] > 0:
                    W("sync", vch, _slot_last[c % 2])
                sm = spl if c % 2 == 0 else spl2
                vd = dma("sync", ap32(slot, ncols), dram_ap[c * 128:(c + 1) * 128, :], sm)
                W("vector", sm, vd)
                if c == 0 and pre_wait:
                    W("vector", ld_c, pre_wait)
                op("vector", lambda v_, c=c, s=slot, n=ncols, oh=offh:
                   v_.tensor_copy(ap16(oh + c * n, n), ap32(s, n)))
                vv = op("vector", lambda v_, c=c, s=slot, n=ncols, oh=offh, ol=offl:
                        v_.tensor_tensor(ap16(ol + c * n, n), ap32(s, n), ap16(oh + c * n, n),
                                         ALU.subtract), v2t)
                splits.append(vv)
                _slot_last[c % 2] = vv
            return P.val(v2t)

        W("sync", t2v, tok_mm_done)
        W("sync", v2t, vb1)
        wd_splits = []

        def wd_chunk_stage(c):
            slot = C_stage + (c % 2) * 4096
            if _slot_last[c % 2] > 0:
                W("sync", vch, _slot_last[c % 2])
            else:
                W("sync", vch, _wt_slot_last[c % 2])
            sm = spl if c % 2 == 0 else spl2
            vd = dma("sync", ap32(slot, 1024), wd_d[c * 128:(c + 1) * 128, :], sm)
            W("vector", sm, vd)
            op("vector", lambda v_, c=c, sl=slot:
               v_.tensor_copy(ap16(A_wdh0 + c * 1024, 1024), ap32(sl, 1024)))
            vv = op("vector", lambda v_, c=c, sl=slot:
                    v_.tensor_tensor(ap16(A_wdh0 + 8192 + c * 1024, 1024), ap32(sl, 1024),
                                     ap16(A_wdh0 + c * 1024, 1024), ALU.subtract), v2t)
            _slot_last[c % 2] = vv
            return vv

        W("tensor", v2t, wt_ready)
        _vt_evac = [0, 0]
        for d in range(8):
            base = (d % 2) * 1024
            if _vt_evac[d % 2] > 0:
                W("tensor", v2t, _vt_evac[d % 2])
            for p in range(3):
                lo = (A_wth, A_wth, A_wtl)[p]
                ro = (A_tkh0, A_tkl0, A_tkh0)[p]
                for c in range(8):
                    for s0, s1 in ((0, 512), (512, EL)):
                        lhs = ap16(lo + c * 1024 + d * 128, 128)
                        rhs = ap16(ro + c * EL + s0, s1 - s0)
                        o = psA[:, base + s0:base + s1]
                        stt = (p == 0 and c == 0)
                        stp = (p == 2 and c == 7)
                        last = stp and s1 == EL
                        v = op("tensor", lambda t, o=o, l=lhs, r=rhs, a=stt, b=stp:
                               t.matmul(o, l, r, start=a, stop=b), t2v if last else None)
            W("vector", t2v, v)
            op("vector", lambda v_, d=d, b=base: v_.tensor_scalar(ap32(C_vts, EL),
                                                                  psA[:, b:b + EL],
                                                                  wtbs[:, d:d + 1], None,
                                                                  ALU.add))
            op("vector", lambda v_, d=d: v_.tensor_copy(ap16(A_vth + d * EL, EL),
                                                        ap32(C_vts, EL)))
            vv = op("vector", lambda v_, d=d:
                    v_.tensor_tensor(ap16(A_vtl + d * EL, EL), ap32(C_vts, EL),
                                     ap16(A_vth + d * EL, EL), ALU.subtract), v2t)
            _vt_evac[d % 2] = vv
            wd_splits.append(wd_chunk_stage(d))
        vt_mm_done = P.val(t2v)
        wd_ready = P.val(vch)

        W("gpsimd", v2t, spill_ready)
        dma("gpsimd", sptk_d, ap16(A_tkh0, 9600), spe)
        dma("gpsimd", spwg_d, ap16(A_wgh0, 96), spe)

        op("vector", lambda v_: v_.memset(
            ap16(A_vth, 4800).rearrange("p (d e l) -> p d e l", e=6, l=LT)[:, :, :, 0:1], 0.0))
        op("vector", lambda v_: v_.memset(
            ap16(A_vtl, 4800).rearrange("p (d e l) -> p d e l", e=6, l=LT)[:, :, :, 0:1], 0.0),
           v2t)
        l0z = P.val(vch)


        sp_done = [0]

        def w1_chunk(c, q="sync", sms=None, hi_on_scalar=False):
            slot = C_stage + (c % 2) * 4096
            if _slot_last[c % 2] > 0:
                W(q, vch, _slot_last[c % 2])
            sm = (sms or (spl, spl2))[c % 2]
            vd = dma(q, ap32(slot, 4096), w1_d[c * 128:(c + 1) * 128, :], sm)
            if hi_on_scalar:
                W("scalar", sm, vd)
                if c == 4:
                    W("scalar", sp1, 16)
                hv = op("scalar", lambda sc_, c=c, s=slot:
                        sc_.copy(ap16(A_W1h + c * 4096, 4096), ap32(s, 4096)), s_dn)
                W("vector", s_dn, hv)
            else:
                W("vector", sm, vd)
                op("vector", lambda v_, c=c, s=slot:
                   v_.tensor_copy(ap16(A_W1h + c * 4096, 4096), ap32(s, 4096)))
            vv = op("vector", lambda v_, c=c, s=slot:
                    v_.tensor_tensor(ap16(A_W1l + c * 4096, 4096), ap32(s, 4096),
                                     ap16(A_W1h + c * 4096, 4096), ALU.subtract), v2t)
            _slot_last[c % 2] = vv

        W("vector", t2v, vt_mm_done)
        W("vector", spe, 32)
        w1_chunk(0)
        w1_chunk(1)

        op("vector", lambda v_: v_.memset(ap16(A_vwh0 + 4 * 1024, 1024, 64, 128), 0.0))
        op("vector", lambda v_: v_.memset(ap16(A_vwl0 + 4 * 1024, 1024, 64, 128), 0.0))

        W("tensor", v2t, wd_ready)
        W("tensor", v2t, l0z)
        _vw_evac = [0, 0]
        for k in range(5):
            kc = ELC[k]
            base = (k % 2) * 1024
            if _vw_evac[k % 2] > 0:
                W("tensor", v2t, _vw_evac[k % 2])
            for p in range(3):
                lo = (A_vth, A_vth, A_vtl)[p]
                ro = (A_wdh0, A_wdh0 + 8192, A_wdh0)[p]
                for dd in range(8):
                    for s in range(2):
                        lhs = ap16(lo + dd * EL + k * 128, kc)
                        rhs = ap16(ro + dd * 1024 + s * 512, 512)
                        o = psA[0:kc, base + s * 512:base + (s + 1) * 512]
                        stt = (p == 0 and dd == 0)
                        stp = (p == 2 and dd == 7)
                        last = stp and s == 1
                        v = op("tensor", lambda t, o=o, l=lhs, r=rhs, a=stt, b=stp:
                               t.matmul(o, l, r, start=a, stop=b), t2v if last else None)
            W("vector", t2v, v)
            op("vector", lambda v_, k=k, kc=kc, b=base:
               v_.tensor_copy(ap16(A_vwh0 + k * 1024, 1024, 0, kc), psA[0:kc, b:b + 1024]))
            vv = op("vector", lambda v_, k=k, kc=kc, b=base:
                    v_.tensor_tensor(ap16(A_vwl0 + k * 1024, 1024, 0, kc),
                                     psA[0:kc, b:b + 1024],
                                     ap16(A_vwh0 + k * 1024, 1024, 0, kc), ALU.subtract), v2t)
            _vw_evac[k % 2] = vv
        vw_evac_done = P.val(vch)
        vw_mm_done = P.val(t2v)

        W("vector", t2v, vw_mm_done)
        w1_chunk(2)
        w1_chunk(3)

        W("sync", t2v, vw_mm_done)
        spwd_done = dma("sync", spwd_d, ap16(A_wdh0, 16384), sp1)
        W("sync", vch, vw_evac_done)
        sp_done[0] = dma("sync", spvw_d, ap16(A_vwh0, 10240), ld_c)

        W("vector", ld_c, sp_done[0])
        W("vector", spe, 32)
        for c in range(4, 8):
            w1_chunk(c, q="scalar", sms=(spf, spf2), hi_on_scalar=True)
        op("vector", lambda v_: v_.memset(carry[:], 0.0), v2t)

        bA_v, bA_s, bA_t = P.val(vch), P.val(sch), P.val(t2v)

        for _e in ("sync", "gpsimd", "scalar", "tensor"):
            W(_e, vch, bA_v)
            W(_e, sch, bA_s)
            W(_e, t2v, bA_t)
        W("scalar", ld_c, P.val(ld_c))
        W("tensor", ld_c, sp_done[0])

        def xslot(i):
            return ap32(C_xb + (i % 2) * 1024, 1024)

        lm = {}
        lm[("ldx", 0)] = dma("sync", xslot(0), x_d[0:128, :], xsem(0))
        if NT > 1:
            lm[("ldx", 1)] = dma("sync", xslot(1), x_d[128:256, :], xsem(1))

        W("scalar", xsem(0), lm[("ldx", 0)])
        v = op("scalar", lambda s: s.activation(ap32(C_sq, 1024), xslot(0), AF.Square,
                                                accum_out=st[:, 1:2]), s_dn)
        lm[("sqx", 0)] = v
        W("vector", xsem(0), lm[("ldx", 0)])
        op("vector", lambda v_: v_.reduce_sum(st[:, 0:1], xslot(0), axis=AXX))
        W("vector", s_dn, lm[("sqx", 0)])
        v = ln_tiny(0)
        W("scalar", v2s, v)
        lm[("istd1", 0)] = ln_istd(0)
        W("vector", s_dn, lm[("istd1", 0)])
        vsp()
        v = op("vector", lambda v_: v_.tensor_scalar(ap32(C_xn, 1024), xslot(0), st[:, 2:3],
                                                     st[:, 7:8], ALU.subtract, ALU.mult), v2t)
        lm[("xn", 0)] = v
        W("tensor", v2t, lm[("xn", 0)])
        for c in range(8):
            v = op("tensor", lambda t, o=psA[:, c * 128:(c + 1) * 128],
                   s=ap32(C_xn + c * 128, 128): t.transpose(o, s, ident[:]),
                   t2v if c == 7 else None)
        lm[("tp", 0)] = v
        W("vector", t2v, lm[("tp", 0)])
        op("vector", lambda v_: v_.tensor_copy(ap16(A_xnTh, 512), psA[:, 0:512]))
        v = op("vector", lambda v_: v_.tensor_tensor(ap16(A_xnTl, 512), psA[:, 0:512],
                                                     ap16(A_xnTh, 512), ALU.subtract), v2t)
        lm[("xnT1", 0)] = v
        op("vector", lambda v_: v_.tensor_copy(ap16(A_xnTh + 512, 512), psA[:, 512:1024]))
        v = op("vector", lambda v_: v_.tensor_tensor(ap16(A_xnTl + 512, 512),
                                                     psA[:, 512:1024],
                                                     ap16(A_xnTh + 512, 512),
                                                     ALU.subtract), v2t)
        lm[("xnT", 0)] = v

        for i in range(NT):
            xb = xslot(i)
            hb = ap32(C_hb + (i % 2) * 1024, 1024)

            if 0 < i and i + 1 < NT:
                W("sync", v2t, lm[("h", i - 1)])
                lm[("ldx", i + 1)] = dma("sync", xslot(i + 1),
                                         x_d[(i + 1) * 128:(i + 2) * 128, :], xsem(i + 1))

            if i + 1 < NT:
                W("scalar", xsem(i + 1), lm[("ldx", i + 1)])
                if i > 0:
                    W("scalar", v2t, lm[("h", i - 1)])
                v = op("scalar", lambda s, j=i + 1: s.activation(ap32(C_sq, 1024), xslot(j),
                                                                 AF.Square,
                                                                 accum_out=st[:, 1:2]), s_dn)
                lm[("sqx", i + 1)] = v

            if i + 1 < NT:
                W("vector", xsem(i + 1), lm[("ldx", i + 1)])
                op("vector", lambda v_, j=i + 1: v_.reduce_sum(st[:, 0:1], xslot(j), axis=AXX))
                W("vector", s_dn, lm[("sqx", i + 1)])
                lm[("var1", i + 1)] = ln_tiny(0)
                W("scalar", v2s, lm[("var1", i + 1)])
                lm[("istd1", i + 1)] = ln_istd(0)
                W("vector", s_dn, lm[("istd1", i + 1)])
                v = op("vector", lambda v_, j=i + 1: v_.tensor_scalar(ap32(C_xn, 1024),
                                                                      xslot(j), st[:, 2:3],
                                                                      st[:, 7:8], ALU.subtract,
                                                                      ALU.mult), v2t)
                lm[("xn", i + 1)] = v

            if i > 0:
                W("tensor", s_dn, lm[("kvf", i - 1)])
            for s in range(2, 6):
                op("tensor", lambda t, o=psA[:, s * 512:(s + 1) * 512],
                   r=a16[0:2, X_b1h + s * 512: X_b1h + (s + 1) * 512]:
                   t.matmul(o, ones2b[:], r, start=True, stop=False))
            W("tensor", v2t, lm[("xnT1", i)])
            for c in range(4):
                for p in range(3):
                    lo = (A_xnTh, A_xnTh, A_xnTl)[p]
                    ro = (A_W1h, A_W1l, A_W1h)[p]
                    lhs = ap16(lo + c * 128, 128)
                    for s in range(2, 6):
                        rhs = ap16(ro + c * 4096 + s * 512, 512)
                        op("tensor", lambda t, o=psA[:, s * 512:(s + 1) * 512], l=lhs, r=rhs:
                           t.matmul(o, l, r, start=False, stop=False))
            W("tensor", v2t, lm[("xnT", i)])
            for sb_ in range(6, 8):
                op("tensor", lambda t, o=psB[:, (sb_ - 6) * 512:(sb_ - 5) * 512],
                   r=a16[0:2, X_b1h + sb_ * 512: X_b1h + (sb_ + 1) * 512]:
                   t.matmul(o, ones2b[:], r, start=True, stop=False))
            for c in range(4):
                for p in range(3):
                    lo = (A_xnTh, A_xnTh, A_xnTl)[p]
                    lhs = ap16(lo + c * 128, 128)
                    for s in range(6, 8):
                        rhs = ap16((A_W1h, A_W1l, A_W1h)[p] + c * 4096 + s * 512, 512)
                        op("tensor", lambda t, o=psB[:, (s - 6) * 512:(s - 5) * 512], l=lhs,
                           r=rhs: t.matmul(o, l, r, start=False, stop=False))
            for c in range(4, 8):
                for p in range(3):
                    lo = (A_xnTh, A_xnTh, A_xnTl)[p]
                    ro = (A_W1h, A_W1l, A_W1h)[p]
                    lhs = ap16(lo + c * 128, 128)
                    for s in range(2, 8):
                        rhs = ap16(ro + c * 4096 + s * 512, 512)
                        o = (psA[:, s * 512:(s + 1) * 512] if s < 6
                             else psB[:, (s - 6) * 512:(s - 5) * 512])
                        stp = (c == 7 and p == 2)
                        last = stp and s == 7
                        v = op("tensor", lambda t, o=o, l=lhs, r=rhs, b=stp:
                               t.matmul(o, l, r, start=False, stop=b), t2v if last else None)
            lm[("qkv1", i)] = v
            if i > 0:
                W("tensor", s_dn, lm[("q", i - 1)])
            for s in range(2):
                op("tensor", lambda t, o=psA[:, s * 512:(s + 1) * 512],
                   r=a16[0:2, X_b1h + s * 512: X_b1h + (s + 1) * 512]:
                   t.matmul(o, ones2b[:], r, start=True, stop=False))
            for c in range(8):
                for p in range(3):
                    lo = (A_xnTh, A_xnTh, A_xnTl)[p]
                    ro = (A_W1h, A_W1l, A_W1h)[p]
                    lhs = ap16(lo + c * 128, 128)
                    for s in range(2):
                        rhs = ap16(ro + c * 4096 + s * 512, 512)
                        stp = (c == 7 and p == 2)
                        last = stp and s == 1
                        v = op("tensor", lambda t, o=psA[:, s * 512:(s + 1) * 512], l=lhs,
                               r=rhs, b=stp: t.matmul(o, l, r, start=False, stop=b),
                               t2v if last else None)
            lm[("qkv", i)] = v

            W("scalar", t2v, lm[("qkv1", i)])
            v = op("scalar", lambda s: s.copy(ap32(C_ku, 1024), psA[:, 1024:2048]), s_dn)
            lm[("k", i)] = v
            v = op("scalar", lambda s: s.activation(ap32(C_g2, 1024), psB[:, 0:1024], AF.Exp,
                                                    scale=-1.0), s_dn)
            lm[("eg", i)] = v
            W("scalar", t2v, lm[("qkv", i)])
            v = op("scalar", lambda s: s.copy(ap32(C_q, 1024), psA[:, 0:1024]), s_dn)
            lm[("q", i)] = v

            W("vector", t2v, lm[("qkv1", i)])
            W("vector", s_dn, lm[("k", i)])
            if i > 0:
                W("vector", c_dma, 16 * i)
            v = op("vector", lambda v_: v_.tensor_tensor(ap32(C_kv, 1024), ap32(C_ku, 1024),
                                                         psA[:, 2048:3072], ALU.mult), v2t)
            lm[("u", i)] = v

            W("tensor", v2t, lm[("u", i)])
            W("tensor", s_dn, lm[("eg", i)])
            if i > 0:
                W("tensor", c_dma, 16 * i)
            for s in range(2):
                oo = psB[:, s * 512:(s + 1) * 512]
                op("tensor", lambda t, o=oo, s=s: t.matmul(o, uones32[:],
                                                           ap32(C_kv + s * 512, 512),
                                                           start=True, stop=False))
                v = op("tensor", lambda t, o=oo, s=s: t.matmul(o, ones1f[:],
                                                               carry[0:1, s * 512:(s + 1) * 512],
                                                               start=False, stop=True),
                       t2v if s == 1 else None)
            lm[("cs", i)] = v

            W("scalar", t2v, lm[("cs", i)])
            v = op("scalar", lambda s: s.copy(ap32(C_kv, 1024), psB[:, 0:1024]), s_dn)
            lm[("kvf", i)] = v

            W("gpsimd", s_dn, lm[("kvf", i)])
            dma("gpsimd", carry[:], ap32(C_kv, 1024, 127, 128), c_dma)

            if i + 1 < NT:
                W("tensor", v2t, lm[("xn", i + 1)])
                W("tensor", s_dn, lm[("q", i)])
                for c in range(8):
                    v = op("tensor", lambda t, o=psA[:, c * 128:(c + 1) * 128],
                           s=ap32(C_xn + c * 128, 128): t.transpose(o, s, ident[:]),
                           t2v if c in (3, 7) else None)
                    if c == 3:
                        lm[("tp1", i + 1)] = v
                lm[("tp", i + 1)] = v
                W("vector", t2v, lm[("tp1", i + 1)])
                op("vector", lambda v_: v_.tensor_copy(ap16(A_xnTh, 512), psA[:, 0:512]))
                v = op("vector", lambda v_: v_.tensor_tensor(ap16(A_xnTl, 512), psA[:, 0:512],
                                                             ap16(A_xnTh, 512),
                                                             ALU.subtract), v2t)
                lm[("xnT1", i + 1)] = v
                W("vector", t2v, lm[("tp", i + 1)])
                op("vector", lambda v_: v_.tensor_copy(ap16(A_xnTh + 512, 512),
                                                       psA[:, 512:1024]))
                v = op("vector", lambda v_: v_.tensor_tensor(ap16(A_xnTl + 512, 512),
                                                             psA[:, 512:1024],
                                                             ap16(A_xnTh + 512, 512),
                                                             ALU.subtract), v2t)
                lm[("xnT", i + 1)] = v

            W("vector", s_dn, lm[("q", i)])
            op("vector", lambda v_: v_.tensor_scalar(ap32(C_g, 1024), ap32(C_g2, 1024), 1.0,
                                                     None, ALU.add))
            op("vector", lambda v_: v_.reciprocal(ap32(C_g2, 1024), ap32(C_g, 1024)))
            rsp()
            rsp()
            if i + 1 < NT:
                W("vector", s_dn, lm[("sqx", i + 1)])
            W("vector", s_dn, lm[("kvf", i)])
            op("vector", lambda v_: v_.tensor_tensor(ap32(C_sq, 1024), ap32(C_q, 1024),
                                                     ap32(C_kv, 1024), ALU.mult))
            v = op("vector", lambda v_: v_.tensor_tensor(ap32(C_g, 1024), ap32(C_sq, 1024),
                                                         ap32(C_g2, 1024), ALU.mult), v2s)
            lm[("a", i)] = v
            W("scalar", v2s, lm[("a", i)])
            v = op("scalar", lambda s: s.activation(ap32(C_sq, 1024), ap32(C_g, 1024),
                                                    AF.Square, accum_out=st[:, 9:10]), s_dn)
            lm[("sqa", i)] = v
            op("vector", lambda v_: v_.reduce_sum(st[:, 8:9], ap32(C_g, 1024), axis=AXX))
            W("vector", s_dn, lm[("sqa", i)])
            lm[("var2", i)] = ln_tiny(8)
            W("scalar", v2s, lm[("var2", i)])
            lm[("istd2", i)] = ln_istd(8)
            W("vector", s_dn, lm[("istd2", i)])
            vsp()
            if i >= 2:
                W("vector", shsem(i - 2), lm[("sth", i - 2)])
            op("vector", lambda v_: v_.tensor_scalar(ap32(C_sq, 1024), ap32(C_g, 1024),
                                                     st[:, 10:11], st[:, 15:16],
                                                     ALU.subtract, ALU.mult))
            v = op("vector", lambda v_, x=xb, h=hb: v_.tensor_tensor(h, x, ap32(C_sq, 1024),
                                                                     ALU.add), v2t)
            lm[("h", i)] = v

            W("gpsimd", v2t, lm[("h", i)])
            lm[("sth", i)] = dma("gpsimd", h_d[i * 128:(i + 1) * 128, :], hb, shsem(i))

            op("vector", lambda v_, h=hb, j=i: v_.reduce_sum(lnst[:, 2 * j:2 * j + 1], h,
                                                             axis=AXX))
            W("scalar", v2t, lm[("h", i)])
            v = op("scalar", lambda s, h=hb, j=i: s.activation(ap32(C_sq, 1024), h, AF.Square,
                                                               accum_out=lnst[:, 2 * j + 1:
                                                                              2 * j + 2]),
                   s_dn)
            lm[("sqh", i)] = v

        bB_v, bB_s, bB_t = P.val(vch), P.val(sch), P.val(t2v)

        def hslot(i):
            return ap32(C_hb2 + (i % 3) * 1024, 1024)

        def hnslot(i):
            return ap32(C_hn2 + (i % 3) * 1024, 1024)

        W("sync", vch, lm[("h", NT - 2)])
        W("sync", shsem(0), lm[("sth", 0)])
        lm[("ldh", 0)] = dma("sync", hslot(0), h_d[0:128, :], hsem(0))
        W("sync", t2v, lm[("qkv", NT - 1)])
        W("sync", spe, 32)
        dma("sync", ap16(B_tkh, 9600), sptk_d, spl)
        rl_a = dma("sync", ap16(B_wgh, 96), spwg_d, spl)
        dma("sync", ap16(B_wdh, 16384), spwd_d, spl2)
        rl_b = dma("sync", ap16(B_vwh, 10240), spvw_d, spl2)

        for _e in ("sync", "gpsimd", "scalar", "tensor"):
            W(_e, vch, bB_v)
            W(_e, sch, bB_s)
            W(_e, t2v, bB_t)
        for _e in ("vector", "scalar", "sync", "gpsimd"):
            W(_e, c_dma, P.val(c_dma))
            W(_e, ld_x, P.val(ld_x))
            W(_e, ld_x2, P.val(ld_x2))
            W(_e, st_h, P.val(st_h))
            W(_e, st_h2, P.val(st_h2))

        if NT > 1:
            W("sync", shsem(1), lm[("sth", 1)])
            lm[("ldh", 1)] = dma("sync", hslot(1), h_d[128:256, :], hsem(1))
        op("vector", lambda v_: v_.tensor_copy(st[:, 0:2], lnst[:, 0:2]))
        v = ln_tiny(0)
        W("scalar", v2s, v)
        lm[("istd3", 0)] = ln_istd(0)
        W("vector", s_dn, lm[("istd3", 0)])
        vsp()
        W("vector", hsem(0), lm[("ldh", 0)])
        v = op("vector", lambda v_: v_.tensor_scalar(hnslot(0), hslot(0), st[:, 2:3],
                                                     st[:, 7:8], ALU.subtract, ALU.mult), v2t)
        lm[("hn", 0)] = v
        W("tensor", v2t, lm[("hn", 0)])
        for c in range(8):
            v = op("tensor", lambda t, o=psA[:, 1536 + c * 128:1664 + c * 128],
                   s=ap32(C_hn2 + c * 128, 128): t.transpose(o, s, ident[:]),
                   t2v if c == 7 else None)
        lm[("tpB", 0)] = v
        W("vector", t2v, lm[("tpB", 0)])
        op("vector", lambda v_: v_.tensor_copy(ap16(B_hnTh, 512), psA[:, 1536:2048]))
        v = op("vector", lambda v_: v_.tensor_tensor(ap16(B_hnTl, 512), psA[:, 1536:2048],
                                                     ap16(B_hnTh, 512), ALU.subtract), v2t)
        lm[("hnT1", 0)] = v
        op("vector", lambda v_: v_.tensor_copy(ap16(B_hnTh + 512, 512), psA[:, 2048:2560]))
        v = op("vector", lambda v_: v_.tensor_tensor(ap16(B_hnTl + 512, 512),
                                                     psA[:, 2048:2560],
                                                     ap16(B_hnTh + 512, 512),
                                                     ALU.subtract), v2t)
        lm[("hnT", 0)] = v

        for i in range(NT):
            hb = hslot(i)

            if i >= 1 and i + 1 < NT:
                if i >= 2:
                    W("sync", sosem(i - 2), lm[("sto", i - 2)])
                W("sync", vch, lm[("hn", i - 1)])
                W("sync", shsem(i + 1), lm[("sth", i + 1)])
                lm[("ldh", i + 1)] = dma("sync", hslot(i + 1),
                                         h_d[(i + 1) * 128:(i + 2) * 128, :], hsem(i + 1))

            if i == 0:
                W("tensor", spl, rl_a)
            W("tensor", v2t, lm[("hnT", i)])
            for c in range(8):
                for p in range(3):
                    lo = (B_hnTh, B_hnTh, B_hnTl)[p]
                    lhs = ap16(lo + c * 128, 128)
                    stt = (c == 0 and p == 0)
                    stp = (c == 7 and p == 2)
                    rhs = ap16((B_wgh, B_wgl, B_wgh)[p] + c * 6, 6)
                    op("tensor", lambda t, o=psA[:, 1024:1030], l=lhs, r=rhs, a=stt, b=stp:
                       t.matmul(o, l, r, start=a, stop=b))
                    for s0, s1 in ((0, 512), (512, EL)):
                        rhs = ap16((B_tkh, B_tkl, B_tkh)[p] + c * EL + s0, s1 - s0)
                        last = stp and s1 == EL
                        v = op("tensor", lambda t, o=psA[:, 1536 + s0:1536 + s1], l=lhs, r=rhs,
                               a=stt, b=stp: t.matmul(o, l, r, start=a, stop=b),
                               t2v if last else None)
            lm[("sc", i)] = v

            if i > 0:
                W("scalar", t2v, lm[("dl", i - 1)])
                if i > 1:
                    W("scalar", v2t, lm[("ot", i - 2)])
                v = op("scalar", lambda s: s.activation(ap32(C_s2, 1024), psA[:, 0:1024],
                                                        AF.Copy, scale=scv[:]), s_dn)
                lm[("s2", i - 1)] = v

            if i + 1 < NT:
                op("vector", lambda v_, j=i + 1: v_.tensor_copy(st[:, 0:2],
                                                                lnst[:, 2 * j:2 * j + 2]))
                lm[("var3", i + 1)] = ln_tiny(0)
                W("scalar", v2s, lm[("var3", i + 1)])
                lm[("istd3", i + 1)] = ln_istd(0)

            W("vector", t2v, lm[("sc", i)])
            op("vector", lambda v_: v_.tensor_copy(st[:, 32:38], psA[:, 1024:1030]))
            for e in range(NEXP):
                op("vector", lambda v_, e=e: v_.reduce_max(st[:, 54 + e:55 + e],
                                                           psA[:, 1536 + e * LT:
                                                               1536 + (e + 1) * LT],
                                                           axis=AXX))
            op("vector", lambda v_: v_.reduce_max(st[:, 38:39], st[:, 32:38], axis=AXX))
            vsp()
            op("vector", lambda v_: v_.tensor_scalar(st[:, 60:66], st[:, 54:60], -ISC, 1000.0,
                                                     ALU.mult, ALU.subtract))
            vsp()
            op("vector", lambda v_: v_.tensor_scalar(st[:, 40:46], st[:, 32:38], st[:, 38:39],
                                                     1000.0, ALU.is_equal, ALU.mult))
            vsp()
            vsp()
            op("vector", lambda v_: v_.tensor_tensor(st[:, 48:54], st[:, 40:46], iota[:],
                                                     ALU.subtract))
            vsp()
            vsp()
            op("vector", lambda v_: v_.reduce_max(st[:, 46:47], st[:, 48:54], axis=AXX))
            vsp()
            vsp()
            op("vector", lambda v_: v_.tensor_scalar(st[:, 84:90], st[:, 48:54], st[:, 46:47],
                                                     1000.0, ALU.is_equal, ALU.mult))
            rsp()
            rsp()
            op("vector", lambda v_: v_.tensor_tensor(st[:, 54:60], st[:, 60:66], st[:, 84:90],
                                                     ALU.add))
            rsp()
            rsp()
            lm[("am", i)] = vsp(v2s)

            if i + 1 < NT:
                W("vector", s_dn, lm[("istd3", i + 1)])
                W("vector", hsem(i + 1), lm[("ldh", i + 1)])
                v = op("vector", lambda v_, j=i + 1: v_.tensor_scalar(
                    hnslot(j), hslot(j), st[:, 2:3], st[:, 7:8], ALU.subtract, ALU.mult), v2t)
                lm[("hn", i + 1)] = v

            if i == 0:
                W("tensor", spl2, rl_b)
            if i > 0:
                W("tensor", s_dn, lm[("s2", i - 1)])
            for c in range(8):
                for p in range(3):
                    lo = (B_hnTh, B_hnTh, B_hnTl)[p]
                    lhs = ap16(lo + c * 128, 128)
                    stt = (c == 0 and p == 0)
                    stp = (c == 7 and p == 2)
                    for s in range(2):
                        rhs = ap16((B_wdh, B_wdl, B_wdh)[p] + c * 1024 + s * 512, 512)
                        last = stp and s == 1
                        v = op("tensor", lambda t, o=psA[:, s * 512:(s + 1) * 512], l=lhs,
                               r=rhs, a=stt: t.matmul(o, l, r, start=a, stop=False),
                               t2v if last else None)
            lm[("xw", i)] = v

            if i > 0:
                v = op("scalar", lambda s: s.activation(ap32(C_s3, 1024), ap32(C_s2, 1024),
                                                        AF.Exp), s_dn)
                lm[("expd", i - 1)] = v
            W("scalar", v2s, lm[("am", i)])
            for e in range(NEXP):
                v = op("scalar", lambda s, e=e: s.activation(ap32(C_exps + e * LT, LT),
                                                             psA[:, 1536 + e * LT:
                                                                 1536 + (e + 1) * LT],
                                                             AF.Exp,
                                                             bias=st[:, 54 + e:55 + e],
                                                             scale=ISC),
                       s_dn if e == NEXP - 1 else None)
            lm[("exp", i)] = v

            if i > 0:
                W("vector", s_dn, lm[("expd", i - 1)])
                op("vector", lambda v_: v_.tensor_scalar(ap32(C_s1, 1024), ap32(C_s3, 1024),
                                                         0.0, FEPS, ALU.is_equal, ALU.mult))
                v = op("vector", lambda v_: v_.tensor_tensor(ap32(C_s2, 1024),
                                                             ap32(C_s3, 1024),
                                                             ap32(C_s1, 1024), ALU.add), v2s)
                lm[("eadj", i - 1)] = v

            op("scalar", lambda s: s.activation(ap32(C_mask + 32, 600), ap32(C_exps, 600),
                                                AF.Copy, accum_out=ap32(C_mask + 8, 1)))
            op("scalar", lambda s: s.activation(ap32(C_mask + 16, 1), ap32(C_mask + 8, 1),
                                                AF.Ln))
            op("scalar", lambda s: s.activation(ap32(C_mask + 24, 1), ap32(C_mask + 16, 1),
                                                AF.Exp, scale=-1.0))
            v = op("scalar", lambda s: s.activation(ap32(C_mask + 32, 600), ap32(C_exps, 600),
                                                    AF.Copy, scale=ap32(C_mask + 24, 1)),
                   s_dn)
            lm[("mask", i)] = v

            if i > 0:
                W("tensor", s_dn, lm[("maTh", i - 1)])
                W("tensor", v2t, lm[("maT", i - 1)])
            W("tensor", s_dn, lm[("mask", i)])
            for k in range(5):
                kc = ELC[k]
                v = op("tensor", lambda t, o=psB[0:kc, k * 128:k * 128 + 128],
                       s=ap32(C_mask + 32 + k * 128, kc): t.transpose(o, s, ident[:]),
                       t2v if k == 4 else None)
            lm[("tpM", i)] = v

            W("scalar", t2v, lm[("tpM", i)])
            v = op("scalar", lambda s: s.copy(ap16(B_maTh, 640), psB[:, 0:640]), s_dn)
            lm[("maTh", i)] = v
            if i > 0:
                W("scalar", v2s, lm[("eadj", i - 1)])
                v = op("scalar", lambda s: s.activation(ap32(C_s3, 1024), ap32(C_s2, 1024),
                                                        AF.Ln), s_dn)
                lm[("lc", i - 1)] = v

            W("vector", t2v, lm[("tpM", i)])
            W("vector", s_dn, lm[("maTh", i)])
            v = op("vector", lambda v_: v_.tensor_tensor(ap16(B_maTl, 640), psB[:, 0:640],
                                                         ap16(B_maTh, 640), ALU.subtract), v2t)
            lm[("maT", i)] = v
            lm[("maT1", i)] = v

            if i + 1 < NT:
                W("tensor", v2t, lm[("hn", i + 1)])
                W("tensor", s_dn, lm[("exp", i)])
                for c in range(8):
                    v = op("tensor", lambda t, o=psA[:, 1536 + c * 128:1664 + c * 128],
                           s=ap32(C_hn2 + ((i + 1) % 3) * 1024 + c * 128, 128):
                           t.transpose(o, s, ident[:]),
                           t2v if c in (3, 7) else None)
                    if c == 3:
                        lm[("tpB1", i + 1)] = v
                lm[("tpB", i + 1)] = v
                W("vector", t2v, lm[("tpB1", i + 1)])
                op("vector", lambda v_: v_.tensor_copy(ap16(B_hnTh, 512),
                                                       psA[:, 1536:2048]))
                v = op("vector", lambda v_: v_.tensor_tensor(ap16(B_hnTl, 512),
                                                             psA[:, 1536:2048],
                                                             ap16(B_hnTh, 512),
                                                             ALU.subtract), v2t)
                lm[("hnT1", i + 1)] = v
                W("vector", t2v, lm[("tpB", i + 1)])
                op("vector", lambda v_: v_.tensor_copy(ap16(B_hnTh + 512, 512),
                                                       psA[:, 2048:2560]))
                v = op("vector", lambda v_: v_.tensor_tensor(ap16(B_hnTl + 512, 512),
                                                             psA[:, 2048:2560],
                                                             ap16(B_hnTh + 512, 512),
                                                             ALU.subtract), v2t)
                lm[("hnT", i + 1)] = v

            if i > 0:
                W("vector", s_dn, lm[("lc", i - 1)])
                v = op("vector", lambda v_, j=i - 1: v_.tensor_tensor(ap32(C_s1, 1024),
                                                                      ap32(C_s3, 1024),
                                                                      hnslot(j), ALU.add), v2s)
                lm[("fr", i - 1)] = v
                op("vector", lambda v_: v_.reduce_sum(st[:, 8:9], ap32(C_s1, 1024), axis=AXX))

            if i > 0:
                W("scalar", v2s, lm[("fr", i - 1)])
                v = op("scalar", lambda s: s.activation(ap32(C_s2, 1024), ap32(C_s1, 1024),
                                                        AF.Square, accum_out=st[:, 9:10]), s_dn)
                lm[("sqf", i - 1)] = v

            if i > 0:
                W("vector", s_dn, lm[("sqf", i - 1)])
                lm[("var4", i - 1)] = ln_tiny(8)
                W("scalar", v2s, lm[("var4", i - 1)])
                lm[("istd4", i - 1)] = ln_istd(8)

            W("tensor", s_dn, lm[("maTh", i)])
            for p in range(3):
                if p == 2:
                    W("tensor", v2t, lm[("maT", i)])
                lo = (B_maTh, B_maTh, B_maTl)[p]
                ro = (B_vwh, B_vwl, B_vwh)[p]
                for k in range(5):
                    kc = ELC[k]
                    lhs = ap16(lo + k * 128, 128, 0, kc)
                    for s in range(2):
                        rhs = ap16(ro + k * 1024 + s * 512, 512, 0, kc)
                        op("tensor", lambda t, o=psA[:, s * 512:(s + 1) * 512], l=lhs, r=rhs:
                           t.matmul(o, l, r, start=False, stop=False))
            for s in range(2):
                v = op("tensor", lambda t, o=psA[:, s * 512:(s + 1) * 512],
                       r=a16[0:2, X_wdbh + s * 512: X_wdbh + (s + 1) * 512]:
                       t.matmul(o, ones2b[:], r, start=False, stop=True),
                       t2v if s == 1 else None)
            lm[("dl", i)] = v

            if i > 0:
                W("vector", s_dn, lm[("istd4", i - 1)])
                op("vector", lambda v_: v_.tensor_scalar(ap32(C_s2, 1024), ap32(C_s1, 1024),
                                                         st[:, 10:11], st[:, 15:16],
                                                         ALU.subtract, ALU.mult))
                v = op("vector", lambda v_, j=i - 1:
                       v_.tensor_tensor(hslot(j), hslot(j), ap32(C_s2, 1024), ALU.add), v2t)
                lm[("ot", i - 1)] = v
                W("gpsimd", v2t, lm[("ot", i - 1)])
                if i >= 3:
                    W("gpsimd", sosem(i - 3), lm[("sto", i - 3)])
                lm[("sto", i - 1)] = dma("gpsimd", out_d[(i - 1) * 128:i * 128, :],
                                         hslot(i - 1), sosem(i - 1))

        i = NT
        W("scalar", t2v, lm[("dl", i - 1)])
        if i > 1:
            W("scalar", v2t, lm[("ot", i - 2)])
        v = op("scalar", lambda s: s.activation(ap32(C_s2, 1024), psA[:, 0:1024],
                                                AF.Copy, scale=scv[:]), s_dn)
        lm[("s2", i - 1)] = v
        v = op("scalar", lambda s: s.activation(ap32(C_s3, 1024), ap32(C_s2, 1024), AF.Exp),
               s_dn)
        lm[("expd", i - 1)] = v
        W("vector", s_dn, lm[("expd", i - 1)])
        op("vector", lambda v_: v_.tensor_scalar(ap32(C_s1, 1024), ap32(C_s3, 1024),
                                                 0.0, FEPS, ALU.is_equal, ALU.mult))
        v = op("vector", lambda v_: v_.tensor_tensor(ap32(C_s2, 1024), ap32(C_s3, 1024),
                                                     ap32(C_s1, 1024), ALU.add), v2s)
        lm[("eadj", i - 1)] = v
        W("scalar", v2s, lm[("eadj", i - 1)])
        v = op("scalar", lambda s: s.activation(ap32(C_s3, 1024), ap32(C_s2, 1024), AF.Ln),
               s_dn)
        lm[("lc", i - 1)] = v
        W("vector", s_dn, lm[("lc", i - 1)])
        v = op("vector", lambda v_, j=i - 1: v_.tensor_tensor(ap32(C_s1, 1024),
                                                              ap32(C_s3, 1024), hnslot(j),
                                                              ALU.add), v2s)
        lm[("fr", i - 1)] = v
        op("vector", lambda v_: v_.reduce_sum(st[:, 8:9], ap32(C_s1, 1024), axis=AXX))
        W("scalar", v2s, lm[("fr", i - 1)])
        v = op("scalar", lambda s: s.activation(ap32(C_s2, 1024), ap32(C_s1, 1024),
                                                AF.Square, accum_out=st[:, 9:10]), s_dn)
        lm[("sqf", i - 1)] = v
        W("vector", s_dn, lm[("sqf", i - 1)])
        lm[("var4", i - 1)] = ln_tiny(8)
        W("scalar", v2s, lm[("var4", i - 1)])
        lm[("istd4", i - 1)] = ln_istd(8)
        W("vector", s_dn, lm[("istd4", i - 1)])
        vsp()
        op("vector", lambda v_: v_.tensor_scalar(ap32(C_s2, 1024), ap32(C_s1, 1024),
                                                 st[:, 10:11], st[:, 15:16],
                                                 ALU.subtract, ALU.mult))
        v = op("vector", lambda v_, j=i - 1:
               v_.tensor_tensor(hslot(j), hslot(j), ap32(C_s2, 1024), ALU.add), v2t)
        lm[("ot", i - 1)] = v
        W("gpsimd", v2t, lm[("ot", i - 1)])
        if i >= 3:
            W("gpsimd", sosem(i - 3), lm[("sto", i - 3)])
        lm[("sto", i - 1)] = dma("gpsimd", out_d[(i - 1) * 128:i * 128, :],
                                 hslot(i - 1), sosem(i - 1))


        W("gpsimd", st_o, P.val(st_o))
        W("gpsimd", st_o2, P.val(st_o2))

        @block.sync
        def _(e):
            for f in P.ops["sync"]:
                f(e)

        @block.vector
        def _(e):
            for f in P.ops["vector"]:
                f(e)

        @block.scalar
        def _(e):
            for f in P.ops["scalar"]:
                f(e)

        @block.tensor
        def _(e):
            for f in P.ops["tensor"]:
                f(e)

        @block.gpsimd
        def _(e):
            for f in P.ops["gpsimd"]:
                f(e)

    return nc


def _prep(inputs):
    f32 = np.float32
    Wq = np.asarray(inputs["L0_attn_W_qkv"], f32)
    Wg = np.asarray(inputs["L0_attn_W_gate"], f32)
    w1 = np.ascontiguousarray(np.concatenate([Wq, Wg], axis=0).T)
    b1 = np.ascontiguousarray(np.concatenate([np.asarray(inputs["L0_attn_b_qkv"], f32),
                                              np.asarray(inputs["L0_attn_b_gate"], f32)]))
    at = np.ascontiguousarray(np.transpose(np.asarray(inputs["L0_ffn_A"], f32), (0, 2, 1)))
    bm = np.ascontiguousarray(np.asarray(inputs["L0_ffn_B"], f32))
    wg = np.ascontiguousarray(np.asarray(inputs["L0_ffn_w_gate"], f32))
    wt = np.ascontiguousarray(np.asarray(inputs["L0_ffn_wt_w"], f32).T)
    wd = np.ascontiguousarray(np.asarray(inputs["L0_ffn_wd_w"], f32).T)
    return {
        "w1": w1, "b1": b1, "at": at, "bm": bm, "wg": wg, "wt": wt,
        "wtb": np.ascontiguousarray(np.asarray(inputs["L0_ffn_wt_b"], f32)),
        "wd": wd,
        "wdb": np.ascontiguousarray(np.asarray(inputs["L0_ffn_wd_b"], f32)),
        "sc": np.ascontiguousarray(np.asarray(inputs["L0_ffn_scale"], f32)),
        "ident": np.eye(128, dtype=f32),
        "uones": np.triu(np.ones((128, 128), dtype=f32)).astype(ml_dtypes.bfloat16),
        "uones32": np.triu(np.ones((128, 128), dtype=f32)),
        "iota6": np.tile(np.arange(6, dtype=f32), (128, 1)),
        "ones1f": np.ones((1, 128), dtype=f32),
        "onesb": np.ones((1, 128), dtype=ml_dtypes.bfloat16),
        "ones2b": np.ones((2, 128), dtype=ml_dtypes.bfloat16),
        "cvals": np.tile(np.array([1e-5, -0.5, -1.0, 1024 ** -0.5, -(1024 ** -0.5),
                                   1000.0, np.finfo(np.float32).eps, 0.0], dtype=f32),
                         (128, 1)),
    }


def kernel(**inputs):
    x = np.asarray(inputs["L0_attn_x"], np.float32)
    base = _prep(inputs)
    S = x.shape[1]
    nc = build(S)
    in_maps = []
    for i in range(x.shape[0]):
        m = dict(base)
        m["x"] = np.ascontiguousarray(x[i])
        in_maps.append(m)
    res = run_bass_kernel_spmd(nc, in_maps, core_ids=list(range(x.shape[0])))
    return np.stack([np.asarray(res.results[i]["out"]) for i in range(x.shape[0])], axis=0)
